# version 31
# baseline (speedup 1.0000x reference)
import numpy as np
from contextlib import ExitStack

import concourse.bass as bass
import concourse.mybir as mybir
from concourse.bass_utils import run_bass_kernel_spmd

F32 = mybir.dt.float32
BF16 = mybir.dt.bfloat16
AF = mybir.ActivationFunctionType
ALU = mybir.AluOpType
AX = mybir.AxisListType

S, D, C, K, E, F = 2048, 1024, 2048, 16, 8, 4096
T = 512
NCH = S // T
DT = D // 128
CT = C // 128
EPS = 1e-5
N_CORES = 8
CAP = 768
GROWS = E * CAP
RCS = ((0, 512), (512, 256))
HUGE = float(1 << 22)


class Sched:
    ENG = ("sync", "scalar", "vector", "tensor", "gpsimd")

    def __init__(self, nc, ctx):
        self.nc = nc
        self.ctx = ctx
        self.sems = {}
        self.cnt = {}
        self.bags = set()
        for name in self.ENG:
            self.sems[name] = ctx.enter_context(nc.semaphore("sem_" + name))
            self.cnt[name] = 0
        self.ops = {e: [] for e in self.ENG}
        self.resw = {}
        self.resr = {}
        self.waited = {e: {} for e in self.ENG}
        self.pending = {e: [] for e in self.ENG}
        self.marks = []

    def mark(self, label):
        self.marks.append((label, dict(self.cnt)))

    def _qsem(self, stream, bag):
        if stream not in self.sems:
            self.sems[stream] = self.ctx.enter_context(
                self.nc.semaphore("sem_" + stream))
            self.cnt[stream] = 0
            if bag:
                self.bags.add(stream)
        return self.sems[stream]

    def _add_wait(self, eng, waits, actor, val):
        if actor in self.bags:
            val = self.cnt[actor]
        if val <= 0:
            return
        if eng == "tensor" and actor == "tensor":
            return
        if self.waited[eng].get(actor, 0) >= val:
            return
        self.waited[eng][actor] = val
        waits.append((self.sems[actor], val))

    def _gather(self, eng, actor, reads, writes):
        skip = actor if actor not in self.ENG else "\x00none"
        waits = []
        for a, v in self.pending[eng]:
            self._add_wait(eng, waits, a, v)
        self.pending[eng] = []
        for r in reads:
            w = self.resw.get(r)
            if w is not None and w[0] != skip:
                self._add_wait(eng, waits, w[0], w[1])
        for r in writes:
            w = self.resw.get(r)
            if w is not None and w[0] != skip:
                self._add_wait(eng, waits, w[0], w[1])
            for a, v in self.resr.get(r, {}).items():
                if a != skip:
                    self._add_wait(eng, waits, a, v)
        return waits

    def _commit(self, actor, val, reads, writes):
        for r in reads:
            d = self.resr.setdefault(r, {})
            d[actor] = max(d.get(actor, 0), val)
        for r in writes:
            self.resw[r] = (actor, val)
            self.resr[r] = {}

    @staticmethod
    def _excl_psum(reads, writes):
        pr = tuple(r for r in reads if r.startswith("ps"))
        if not pr:
            return reads, writes
        return (tuple(r for r in reads if not r.startswith("ps")),
                tuple(writes) + pr)

    def op(self, eng, fn, reads=(), writes=()):
        reads, writes = self._excl_psum(reads, writes)
        waits = self._gather(eng, eng, reads, writes)
        self.cnt[eng] += 1
        self.ops[eng].append((fn, waits, (self.sems[eng], 1)))
        self._commit(eng, self.cnt[eng], reads, writes)

    def dma(self, eng, stream, fn, reads=(), writes=(), bag=False, inc=16):
        reads, writes = self._excl_psum(reads, writes)
        sem = self._qsem(stream, False)
        waits = self._gather(eng, stream, reads, writes)
        if self.cnt[stream] > 0:
            self._add_wait(eng, waits, stream, self.cnt[stream])
        self.cnt[stream] += inc
        self.ops[eng].append((fn, waits, (sem, inc)))
        self._commit(stream, self.cnt[stream], reads, writes)

    def join(self):
        for e in self.ENG:
            pend = []
            for a in self.cnt:
                if a != e and self.cnt[a] > 0:
                    pend.append((a, self.cnt[a]))
            self.pending[e] = pend

    def finalize(self, block):
        for e in self.ENG:
            if not self.ops[e]:
                continue
            oplist = self.ops[e]

            def body(eng, oplist=oplist):
                for fn, waits, inc in oplist:
                    for s, v in waits:
                        eng.wait_ge(s, v)
                    inst = fn(eng)
                    inst.then_inc(inc[0], inc[1])
            getattr(block, e)(body)


def _ecopy(e, out, in_):
    if hasattr(e, "tensor_copy"):
        return e.tensor_copy(out=out, in_=in_)
    return e.copy(out=out, in_=in_)


_LAST_SCHED = []


def build_program(phases=(0, 1, 2), p0router=True, agmode=2):
    nc = bass.Bass("TRN2", target_bir_lowering=False, debug=False)
    dram = {}

    def din(name, shape):
        dram[name] = nc.dram_tensor(name, shape, F32, kind="ExternalInput").ap()
        return dram[name]

    x_d = din("x", [S, D])
    wi_sh = din("wi_sh", [D // 8, 2 * C])
    convw_d = din("conv_w", [CT, 128, 3])
    convb_d = din("conv_b", [CT, 128, 1])
    wst_d = din("w_state", [C, 32])
    wt_sh = din("wt_sh", [C // 8, C])
    btime_d = din("b_time", [CT, 128, 1])
    alog_d = din("a_log", [CT, 128, K])
    skip_d = din("skip", [CT, 128, 1])
    wo_sh = din("wo_sh", [C // 8, D])
    rw_d = din("router_w", [D, E])
    wv_sh = din("wv_sh", [D, F])
    svt_d = din("sv_t", [E, 128, F // 128])
    w_sh = din("w_sh", [D, F])
    st_d = din("s_t", [E, 128, F // 128])
    w1_sh = din("w1_sh", [F, D])
    s1t_d = din("s1_t", [E, 128, DT])
    ident_d = din("ident", [128, 128])
    ones_d = din("ones1", [1, 128])
    tril_d = din("tril", [128, 128])
    onescol_d = din("onescol", [128, 1])
    ecap_d = din("ecap", [1, E])
    capthr_d = din("capthr", [128, E])
    trmap_d = din("trmap", [128, E])
    out_d = nc.dram_tensor("out", [S, D], F32, kind="ExternalOutput").ap()
    def dint(name, shape):
        return nc.dram_tensor(name, shape, BF16, kind="Internal",
                              addr_space="Shared")
    def dintl(name, shape):
        return nc.dram_tensor(name, shape, BF16, kind="Internal")
    stg = {k: dintl(k + "_st", shp) for k, shp in
           [("wt", [C // 8, C]), ("wi", [D // 8, 2 * C]), ("wo", [C // 8, D])]}
    stgE = {k: [dintl(f"{k}_se{e2}", shp) for e2 in range(E)] for k, shp in
            [("wv", [D // 8, F]), ("w", [D // 8, F]), ("w1", [F // 8, D])]}
    wv_allE = [dint(f"wv_ae{e2}", [D, F]) for e2 in range(E)]
    w_allE = [dint(f"w_ae{e2}", [D, F]) for e2 in range(E)]
    w1_allE = [dint(f"w1_ae{e2}", [F, D]) for e2 in range(E)]
    wt_all = dint("wt_all", [C, C])
    wi_all = dint("wi_all", [D, 2 * C])
    wo_all = dint("wo_all", [C, D])
    win_d = wi_all.ap()
    wtime_d = wt_all.ap()
    wout_d = wo_all.ap()
    wv_dE = [t.ap() for t in wv_allE]
    w_dE = [t.ap() for t in w_allE]
    w1_dE = [t.ap() for t in w1_allE]
    xng_d = nc.dram_tensor("xng", [GROWS + 8, D], F32, kind="Internal")
    pairbuf_d = nc.dram_tensor("pairbuf", [GROWS, D], F32, kind="Internal")

    with ExitStack() as ctx:
        sched = Sched(nc, ctx)

        def sb(name, shape, dt=F32, c=ctx):
            return c.enter_context(nc.sbuf_tensor(name, shape, dt))

        xn_bf = [sb(f"xn_bf{d}", [128, S], BF16) for d in range(DT)]
        ident = sb("ident_sb", [128, 128])
        ones1 = sb("ones_sb", [1, 128])
        pos0i = sb("pos0i", [128, S // 128], mybir.dt.int32)
        pos1i = sb("pos1i", [128, S // 128], mybir.dt.int32)
        g0_sb = sb("g0_sb", [128, S // 128])
        g1_sb = sb("g1_sb", [128, S // 128])
        out_tok = [sb(f"out_tok{i}", [128, D]) for i in range(1)]

        ps = [ctx.enter_context(nc.psum_tensor(f"ps{i}", [128, 512], F32))
              for i in range(8)]

        s = sched
        s.dma("sync", "c_const", lambda e: e.dma_start(out=ident[:], in_=ident_d),
              writes=("ident",), bag=True)
        s.dma("sync", "c_const", lambda e: e.dma_start(out=ones1[:], in_=ones_d),
              writes=("ones1",), bag=True)
        s.dma("sync", "c_const", lambda e: e.dma_start(out=tril[:], in_=tril_d),
              writes=("tril",), bag=True)
        s.dma("sync", "c_const", lambda e: e.dma_start(out=onescol[:], in_=onescol_d),
              writes=("onescol",), bag=True)
        s.dma("sync", "c_const", lambda e: e.dma_start(out=capthr[:], in_=capthr_d),
              writes=("capthr",), bag=True)
        s.dma("sync", "c_const", lambda e: e.dma_start(out=trmap[:], in_=trmap_d),
              writes=("trmap",), bag=True)
        s.dma("sync", "outw", lambda e: e.dma_start(out=out_d, in_=x_d),
              writes=tuple(f"outrow{tt}" for tt in range(S // 128)), bag=True)


        with ExitStack() as ctxA:
            tril = ctxA.enter_context(nc.sbuf_tensor("tril_sb", [128, 128], F32))
            onescol = ctxA.enter_context(nc.sbuf_tensor("onescol_sb", [128, 1], F32))
            capthr = ctxA.enter_context(nc.sbuf_tensor("capthr_sb", [128, E], F32))
            trmap = ctxA.enter_context(nc.sbuf_tensor("trmap_sb", [128, E], F32))
            castf = [ctxA.enter_context(nc.sbuf_tensor(f"castf{i}", [128, 1024], F32)) for i in range(4)]
            castb = [ctxA.enter_context(nc.sbuf_tensor(f"castb{i}", [128, 1024], BF16)) for i in range(4)]
            ci = 0
            events = []
            agread = {}
            def cjob(dst_t, tag, src, r0, c0, dr, dc, cw):
                k2 = f"STG{tag}_{len(agread.setdefault(tag, []))}"
                agread[tag].append(k2)
                events.append(("cast", dst_t, k2, src, r0, c0, dr, dc, cw))
            if agmode >= 1:
                for ex in range(E):
                    for nm, src in [("wv", wv_sh), ("w", w_sh)]:
                        for c0 in range(0, F, 1024):
                            cjob(stgE[nm][ex], f"{nm}_{ex}", src,
                                 ex * 128, c0, 0, c0, 1024)
                    for j in range(4):
                        cjob(stgE["w1"][ex], f"w1_{ex}", w1_sh,
                             ex * (F // 8) + j * 128, 0, j * 128, 0, 1024)
                    if agmode >= 2:
                        for nm, dstE in [("wv", wv_allE), ("w", w_allE),
                                         ("w1", w1_allE)]:
                            events.append(("ag", f"ag{nm}{ex}", stgE[nm][ex],
                                           dstE[ex], f"AG{nm}{ex}", f"{nm}_{ex}"))
                for nm, src, rows in [("wt", wt_sh, C // 8), ("wi", wi_sh, D // 8),
                                      ("wo", wo_sh, C // 8)]:
                    cols = src.shape[1]
                    for r0 in range(0, rows, 128):
                        for c0 in range(0, cols, 1024):
                            cjob(stg[nm], nm, src, r0, c0, r0, c0,
                                 min(1024, cols - c0))
                if agmode >= 2:
                    for nm, dst in [("wt", wt_all), ("wi", wi_all), ("wo", wo_all)]:
                        events.append(("ag", f"ag_{nm}", stg[nm], dst,
                                       f"AG{nm}", nm))
            for ev in events:
                if ev[0] == "ag":
                    _, strm, src_t, dst_t, agkey, tag = ev
                    s.dma("gpsimd", strm,
                          lambda e, src_t=src_t, dst_t=dst_t: e.collective_compute(
                              "AllGather", ALU.bypass,
                              replica_groups=[list(range(N_CORES))],
                              ins=[src_t.ap().opt()],
                              outs=[dst_t.ap().opt()]),
                          reads=tuple(agread[tag]), writes=(agkey,), inc=1)
                    continue
                _, dst_t, skey, src, r0, c0, dr, dc, cw = ev
                b = ci % 4
                s.dma("gpsimd", f"cst{b}", lambda e, src=src, r0=r0, c0=c0, cw=cw, b=b: e.dma_start(
                    out=castf[b][:, 0:cw], in_=src[r0:r0 + 128, c0:c0 + cw]),
                    writes=(f"castf{b}",))
                s.op("vector" if ci % 2 == 0 else "scalar",
                     lambda e, cw=cw, b=b: _ecopy(e, castb[b][:, 0:cw], in_=castf[b][:, 0:cw]),
                     reads=(f"castf{b}",), writes=(f"castb{b}",))
                s.dma("gpsimd", f"cso{b}", lambda e, dst_t=dst_t, dr=dr, dc=dc, cw=cw, b=b: e.dma_start(
                    out=dst_t.ap()[dr:dr + 128, dc:dc + cw], in_=castb[b][:, 0:cw]),
                    reads=(f"castb{b}",), writes=(skey,))
                ci += 1
            s.op("vector", lambda e: e.memset(castf[0][:], 0.0), writes=("castf0",))
            for rt in range(GROWS // 128):
                s.dma("gpsimd", f"zf{rt % 4}", lambda e, rt=rt: e.dma_start(
                    out=xng_d.ap()[rt * 128:(rt + 1) * 128, :], in_=castf[0][:]),
                    reads=("castf0",), writes=("xng",), bag=True)

            with ExitStack() as ctx0:
                x_tok = [sb(f"x_tok{i}", [128, D], c=ctx0) for i in range(2)]
                xn_tok2 = [sb(f"xn_tok_{i}", [128, D], c=ctx0) for i in range(2)]
                sq_scr2 = [sb(f"sq_scr_{i}", [128, D], c=ctx0) for i in range(2)]
                xnf = [sb(f"xnf{d}", [128, 128], c=ctx0) for d in range(DT)]
                rw_sb = [sb(f"rw{d}", [128, E], c=ctx0) for d in range(DT)]
                m_t2 = [sb(f"m_t_{i}", [128, 1], c=ctx0) for i in range(2)]
                ss_t2 = [sb(f"ss_t_{i}", [128, 1], c=ctx0) for i in range(2)]
                msq_t2 = [sb(f"msq_t_{i}", [128, 1], c=ctx0) for i in range(2)]
                var_t2 = [sb(f"var_t_{i}", [128, 1], c=ctx0) for i in range(2)]
                std_t2 = [sb(f"std_t_{i}", [128, 1], c=ctx0) for i in range(2)]
                rstd_t2 = [sb(f"rstd_t_{i}", [128, 1], c=ctx0) for i in range(2)]
                lg_sb2 = [sb(f"lg_sb_{i}", [8, 128], c=ctx0) for i in range(2)]
                ltok2 = [sb(f"ltok_{i}", [128, E], c=ctx0) for i in range(2)]
                max82 = [sb(f"max8_{i}", [128, 8], c=ctx0) for i in range(2)]
                negmax2 = [sb(f"negmax_{i}", [128, 1], c=ctx0) for i in range(2)]
                ptok2 = [sb(f"ptok_{i}", [128, E], c=ctx0) for i in range(2)]
                sump2 = [sb(f"sump_{i}", [128, 1], c=ctx0) for i in range(2)]
                rsump2 = [sb(f"rsump_{i}", [128, 1], c=ctx0) for i in range(2)]
                e2v2 = [sb(f"e2v_{i}", [128, 1], c=ctx0) for i in range(2)]
                maskv2 = [sb(f"maskv_{i}", [128, E], c=ctx0) for i in range(2)]
                gmv2 = [sb(f"gmv_{i}", [128, E], c=ctx0) for i in range(2)]
                gts2 = [sb(f"gts_{i}", [128, E], c=ctx0) for i in range(2)]
                P_sb2 = [sb(f"P_sb_{i}", [128, E], c=ctx0) for i in range(2)]
                posm2 = [sb(f"posm_{i}", [128, E], c=ctx0) for i in range(2)]
                posi2 = [sb(f"posi_{i}", [128, E], mybir.dt.int32, c=ctx0) for i in range(2)]
                ovf2 = [sb(f"ovf_{i}", [128, E], c=ctx0) for i in range(2)]
                scr82 = [sb(f"scr8_{i}", [128, E], c=ctx0) for i in range(2)]
                baserow = sb("baserow", [1, E], c=ctx0)
                cnt_sb2 = [sb(f"cnt_sb_{i}", [E, 1], c=ctx0) for i in range(2)]
                cntrow2 = [sb(f"cntrow_{i}", [1, E], c=ctx0) for i in range(2)]
                p0f2 = [sb(f"p0f_{i}", [128, 1], c=ctx0) for i in range(2)]
                p1f2 = [sb(f"p1f_{i}", [128, 1], c=ctx0) for i in range(2)]
                s.dma("scalar", "c_par", lambda e: e.dma_start(out=baserow[:], in_=ecap_d),
                      writes=("baserow",), bag=True)

                for d in range(DT):
                    s.dma("sync", "c_rw",
                          lambda e, d=d: e.dma_start(out=rw_sb[d][:],
                                                     in_=rw_d[d * 128:(d + 1) * 128, :]),
                          writes=(f"rw{d}",), bag=True)

                for tt in range(S // 128):
                    tb = tt % 2
                    xb = x_tok[tt % 2]
                    xkey = f"xtok{tt % 2}"
                    cols = slice(tt * 128, (tt + 1) * 128)
                    s.dma("sync", f"xt{tt % 2}",
                          lambda e, tb=tb, tt=tt, xb=xb: e.dma_start(
                              out=xb[:], in_=x_d[tt * 128:(tt + 1) * 128, :]),
                          writes=(xkey,))
                    s.op("vector", lambda e, tb=tb, xb=xb: e.reduce_sum(
                        out=m_t2[tb][:], in_=xb[:], axis=AX.X), reads=(xkey,), writes=(f"m_t{tb}",))
                    s.op("vector", lambda e, tb=tb: e.tensor_scalar(
                        out=m_t2[tb][:], in0=m_t2[tb][:], scalar1=1.0 / D, scalar2=None,
                        op0=ALU.mult), reads=(f"m_t{tb}",), writes=(f"m_t{tb}",))
                    s.op("scalar", lambda e, tb=tb, xb=xb: e.activation(
                        out=sq_scr2[tb][:], in_=xb[:], func=AF.Square, accum_out=ss_t2[tb][:]),
                        reads=(xkey,), writes=(f"sq_scr{tb}", f"ss_t{tb}"))
                    s.op("vector", lambda e, tb=tb: e.tensor_tensor(
                        out=msq_t2[tb][:], in0=m_t2[tb][:], in1=m_t2[tb][:], op=ALU.mult),
                        reads=(f"m_t{tb}",), writes=(f"msq_t{tb}",))
                    s.op("vector", lambda e, tb=tb: e.scalar_tensor_tensor(
                        out=var_t2[tb][:], in0=ss_t2[tb][:], scalar=1.0 / D, in1=msq_t2[tb][:],
                        op0=ALU.mult, op1=ALU.subtract),
                        reads=(f"ss_t{tb}", f"msq_t{tb}"), writes=(f"var_t{tb}",))
                    s.op("vector", lambda e, tb=tb: e.tensor_scalar(
                        out=var_t2[tb][:], in0=var_t2[tb][:], scalar1=EPS, scalar2=None,
                        op0=ALU.add), reads=(f"var_t{tb}",), writes=(f"var_t{tb}",))
                    s.op("scalar", lambda e, tb=tb: e.activation(
                        out=std_t2[tb][:], in_=var_t2[tb][:], func=AF.Sqrt),
                        reads=(f"var_t{tb}",), writes=(f"std_t{tb}",))
                    s.op("vector", lambda e, tb=tb: e.reciprocal(
                        out=rstd_t2[tb][:], in_=std_t2[tb][:]), reads=(f"std_t{tb}",), writes=(f"rstd_t{tb}",))
                    s.op("vector", lambda e, tb=tb, xb=xb: e.tensor_scalar(
                        out=xn_tok2[tb][:], in0=xb[:], scalar1=m_t2[tb][:], scalar2=rstd_t2[tb][:],
                        op0=ALU.subtract, op1=ALU.mult),
                        reads=(xkey, f"m_t{tb}", f"rstd_t{tb}"), writes=(f"xn_tok{tb}",))
                    for half in range(2):
                        pk_n = f"ps{half}"
                        for j in range(4):
                            d = half * 4 + j
                            s.op("tensor", lambda e, tb=tb, d=d, j=j, half=half: e.transpose(
                                out=ps[half][:, j * 128:(j + 1) * 128],
                                in_=xn_tok2[tb][:, d * 128:(d + 1) * 128], identity=ident[:]),
                                reads=(f"xn_tok{tb}", "ident"), writes=(pk_n,))
                        for j in range(4):
                            d = half * 4 + j
                            s.op("scalar", lambda e, tb=tb, d=d, j=j, half=half, cols=cols: e.copy(
                                out=xn_bf[d][:, cols], in_=ps[half][:, j * 128:(j + 1) * 128]),
                                reads=(pk_n,), writes=(f"xnbf{d}",))
                            s.op("vector", lambda e, tb=tb, d=d, j=j, half=half: e.tensor_copy(
                                out=xnf[d][:], in_=ps[half][:, j * 128:(j + 1) * 128]),
                                reads=(pk_n,), writes=(f"xnf{d}",))
                    if not p0router:
                        continue
                    if True:
                        for d in range(DT):
                            s.op("tensor", lambda e, tb=tb, d=d: e.matmul(
                                out=ps[4][0:8, 0:128], lhsT=rw_sb[d][:], rhs=xnf[d][:],
                                start=(d == 0), stop=(d == DT - 1)),
                                reads=(f"xnf{d}", f"rw{d}"), writes=("ps4",))
                        s.op("scalar", lambda e, tb=tb: e.copy(out=lg_sb2[tb][:], in_=ps[4][0:8, 0:128]),
                             reads=("ps4",), writes=(f"lg_sb{tb}",))
                        s.op("tensor", lambda e, tb=tb: e.transpose(
                            out=ps[5][0:128, 0:8], in_=lg_sb2[tb][:], identity=ident[0:8, 0:8]),
                            reads=(f"lg_sb{tb}", "ident"), writes=("ps5",))
                        s.op("vector", lambda e, tb=tb: e.tensor_copy(out=ltok2[tb][:], in_=ps[5][0:128, 0:8]),
                             reads=("ps5",), writes=(f"ltok{tb}",))
                        s.op("vector", lambda e, tb=tb: e.max(out=max82[tb][:], in_=ltok2[tb][:]),
                             reads=(f"ltok{tb}",), writes=(f"max8{tb}",))
                        s.op("vector", lambda e, tb=tb: e.tensor_scalar(
                            out=negmax2[tb][:], in0=max82[tb][:, 0:1], scalar1=-1.0, scalar2=None,
                            op0=ALU.mult), reads=(f"max8{tb}",), writes=(f"negmax{tb}",))
                        s.op("scalar", lambda e, tb=tb: e.activation(
                            out=ptok2[tb][:], in_=ltok2[tb][:], func=AF.Exp, bias=negmax2[tb][:]),
                            reads=(f"ltok{tb}", f"negmax{tb}"), writes=(f"ptok{tb}",))
                        s.op("vector", lambda e, tb=tb: e.reduce_sum(
                            out=sump2[tb][:], in_=ptok2[tb][:], axis=AX.X), reads=(f"ptok{tb}",), writes=(f"sump{tb}",))
                        s.op("vector", lambda e, tb=tb: e.reciprocal(out=rsump2[tb][:], in_=sump2[tb][:]),
                             reads=(f"sump{tb}",), writes=(f"rsump{tb}",))
                        s.op("scalar", lambda e, tb=tb: e.activation(
                            out=e2v2[tb][:], in_=max82[tb][:, 1:2], func=AF.Exp, bias=negmax2[tb][:]),
                            reads=(f"max8{tb}", f"negmax{tb}"), writes=(f"e2v{tb}",))
                        s.op("vector", lambda e, tb=tb: e.tensor_scalar(
                            out=maskv2[tb][:], in0=ptok2[tb][:], scalar1=e2v2[tb][:], scalar2=None,
                            op0=ALU.is_ge), reads=(f"ptok{tb}", f"e2v{tb}"), writes=(f"maskv{tb}",))
                        s.op("vector", lambda e, tb=tb: e.tensor_tensor(
                            out=gmv2[tb][:], in0=ptok2[tb][:], in1=maskv2[tb][:], op=ALU.mult),
                            reads=(f"ptok{tb}", f"maskv{tb}"), writes=(f"gmv{tb}",))
                        s.op("vector", lambda e, tb=tb: e.tensor_scalar(
                            out=gts2[tb][:], in0=gmv2[tb][:], scalar1=rsump2[tb][:], scalar2=None,
                            op0=ALU.mult), reads=(f"gmv{tb}", f"rsump{tb}"), writes=(f"gts{tb}",))
                        s.op("tensor", lambda e, tb=tb: e.matmul(
                            out=ps[6][0:128, 0:E], lhsT=tril[:], rhs=maskv2[tb][:],
                            start=True, stop=True),
                            reads=(f"maskv{tb}", "tril"), writes=("ps6",))
                        s.op("tensor", lambda e, tb=tb: e.matmul(
                            out=ps[2][0:E, 0:1], lhsT=maskv2[tb][:], rhs=onescol[:],
                            start=True, stop=True),
                            reads=(f"maskv{tb}", "onescol"), writes=("ps2",))
                        s.op("vector", lambda e, tb=tb: e.tensor_copy(
                            out=P_sb2[tb][:], in_=ps[6][0:128, 0:E]),
                            reads=("ps6",), writes=(f"P_sb{tb}",))
                        s.op("vector", lambda e, tb=tb: e.tensor_copy(
                            out=cnt_sb2[tb][:], in_=ps[2][0:E, 0:1]),
                            reads=("ps2",), writes=(f"cnt_sb{tb}",))
                        s.op("tensor", lambda e, tb=tb: e.matmul(
                            out=ps[3][0:128, 0:E], lhsT=ones1[:], rhs=baserow[:],
                            start=True, stop=True),
                            reads=("baserow", "ones1"), writes=("ps3",))
                        s.op("vector", lambda e, tb=tb: e.tensor_tensor(
                            out=posm2[tb][:], in0=P_sb2[tb][:], in1=ps[3][0:128, 0:E], op=ALU.add),
                            reads=(f"P_sb{tb}",), writes=(f"posm{tb}", "ps3"))
                        s.op("tensor", lambda e, tb=tb: e.transpose(
                            out=ps[2][0:1, 0:E], in_=cnt_sb2[tb][:], identity=ident[0:E, 0:E]),
                            reads=(f"cnt_sb{tb}", "ident"), writes=("ps2",))
                        s.op("vector", lambda e, tb=tb: e.tensor_copy(
                            out=cntrow2[tb][:], in_=ps[2][0:1, 0:E]),
                            reads=("ps2",), writes=(f"cntrow{tb}",))
                        s.op("vector", lambda e, tb=tb: e.tensor_tensor(
                            out=baserow[:], in0=baserow[:], in1=cntrow2[tb][:], op=ALU.add),
                            reads=(f"cntrow{tb}", "baserow"), writes=("baserow",))
                        s.op("vector", lambda e, tb=tb: e.tensor_tensor(
                            out=ovf2[tb][:], in0=posm2[tb][:], in1=capthr[:], op=ALU.is_ge),
                            reads=(f"posm{tb}", "capthr"), writes=(f"ovf{tb}",))
                        s.op("vector", lambda e, tb=tb: e.tensor_scalar(
                            out=scr82[tb][:], in0=maskv2[tb][:], scalar1=-1.0, scalar2=1.0,
                            op0=ALU.mult, op1=ALU.add),
                            reads=(f"maskv{tb}",), writes=(f"scr8{tb}",))
                        s.op("vector", lambda e, tb=tb: e.tensor_tensor(
                            out=scr82[tb][:], in0=scr82[tb][:], in1=ovf2[tb][:], op=ALU.add),
                            reads=(f"scr8{tb}", f"ovf{tb}"), writes=(f"scr8{tb}",))
                        s.op("vector", lambda e, tb=tb: e.scalar_tensor_tensor(
                            out=posm2[tb][:], in0=scr82[tb][:], scalar=HUGE, in1=posm2[tb][:],
                            op0=ALU.mult, op1=ALU.add),
                            reads=(f"scr8{tb}", f"posm{tb}"), writes=(f"posm{tb}",))
                        s.op("vector", lambda e, tb=tb: e.tensor_tensor(
                            out=scr82[tb][:], in0=posm2[tb][:], in1=trmap[:], op=ALU.min),
                            reads=(f"posm{tb}", "trmap"), writes=(f"scr8{tb}",))
                        s.op("vector", lambda e, tb=tb: e.tensor_copy(out=posi2[tb][:], in_=scr82[tb][:]),
                             reads=(f"scr8{tb}",), writes=(f"posi{tb}",))
                        for ei in range(E):
                            s.dma("gpsimd", f"sct{ei}",
                                  lambda e, tb=tb, ei=ei: e.indirect_dma_start(
                                      out=xng_d.ap(),
                                      out_offset=bass.IndirectOffsetOnAxis(
                                          ap=posi2[tb][:, ei:ei + 1], axis=0),
                                      in_=xn_tok2[tb][:], in_offset=None),
                                  reads=(f"posi{tb}", f"xn_tok{tb}"), writes=(f"xng_e{ei}",))
                        s.op("vector", lambda e, tb=tb: e.tensor_reduce(
                            out=p0f2[tb][:], in_=posm2[tb][:], axis=AX.X, op=ALU.min),
                            reads=(f"posm{tb}",), writes=(f"p0f{tb}",))
                        s.op("vector", lambda e, tb=tb: e.tensor_scalar(
                            out=scr82[tb][:], in0=posm2[tb][:], scalar1=p0f2[tb][:], scalar2=None,
                            op0=ALU.is_equal), reads=(f"posm{tb}", f"p0f{tb}"), writes=(f"scr8{tb}",))
                        s.op("vector", lambda e, tb=tb, tt=tt: e.tensor_copy(
                            out=pos0i[:, tt:tt + 1], in_=p0f2[tb][:]),
                            reads=(f"p0f{tb}",), writes=("pos0i",))
                        s.op("vector", lambda e, tb=tb: e.tensor_tensor(
                            out=ovf2[tb][:], in0=gts2[tb][:], in1=scr82[tb][:], op=ALU.mult),
                            reads=(f"gts{tb}", f"scr8{tb}"), writes=(f"ovf{tb}",))
                        s.op("vector", lambda e, tb=tb, tt=tt: e.reduce_sum(
                            out=g0_sb[:, tt:tt + 1], in_=ovf2[tb][:], axis=AX.X),
                            reads=(f"ovf{tb}",), writes=("g0_sb",))
                        s.op("vector", lambda e, tb=tb: e.scalar_tensor_tensor(
                            out=posm2[tb][:], in0=scr82[tb][:], scalar=HUGE, in1=posm2[tb][:],
                            op0=ALU.mult, op1=ALU.add),
                            reads=(f"scr8{tb}", f"posm{tb}"), writes=(f"posm{tb}",))
                        s.op("vector", lambda e, tb=tb: e.tensor_reduce(
                            out=p1f2[tb][:], in_=posm2[tb][:], axis=AX.X, op=ALU.min),
                            reads=(f"posm{tb}",), writes=(f"p1f{tb}",))
                        s.op("vector", lambda e, tb=tb: e.tensor_scalar(
                            out=scr82[tb][:], in0=posm2[tb][:], scalar1=p1f2[tb][:], scalar2=None,
                            op0=ALU.is_equal), reads=(f"posm{tb}", f"p1f{tb}"), writes=(f"scr8{tb}",))
                        s.op("vector", lambda e, tb=tb, tt=tt: e.tensor_copy(
                            out=pos1i[:, tt:tt + 1], in_=p1f2[tb][:]),
                            reads=(f"p1f{tb}",), writes=("pos1i",))
                        s.op("vector", lambda e, tb=tb: e.tensor_tensor(
                            out=ovf2[tb][:], in0=gts2[tb][:], in1=scr82[tb][:], op=ALU.mult),
                            reads=(f"gts{tb}", f"scr8{tb}"), writes=(f"ovf{tb}",))
                        s.op("vector", lambda e, tb=tb, tt=tt: e.reduce_sum(
                            out=g1_sb[:, tt:tt + 1], in_=ovf2[tb][:], axis=AX.X),
                            reads=(f"ovf{tb}",), writes=("g1_sb",))
                s.join()

            if 1 in phases:
                ca = lambda n, shp, dt=F32: ctxA.enter_context(nc.sbuf_tensor(n, shp, dt))
                Zp = [ca(f"zp{i}", [128, CAP], BF16) for i in range(4)]
                wv_bf = [ca(f"wvbf{i}", [128, 2048], BF16) for i in range(2)]
                w_bf = [ca(f"wbf{i}", [128, 2048], BF16) for i in range(2)]
                w1_bf = [ca(f"w1bf{i}", [128, 2048], BF16) for i in range(2)]
                xn_g = [[ca(f"xng{b}_{d}", [128, CAP], BF16) for d in range(DT)]
                        for b in range(2)]
                tokrow = [ca(f"tokrow{i}", [128, D]) for i in range(2)]
                po = [ca(f"po{d}", [128, CAP]) for d in range(DT)]
                ptk = [ca(f"ptok{i}", [128, D]) for i in range(2)]
                gb0 = [ca(f"gb0_{i}", [128, D]) for i in range(2)]
                gb1 = [ca(f"gb1_{i}", [128, D]) for i in range(2)]
                mo = [ca(f"mo{i}", [128, D]) for i in range(2)]
                sv_sb = [ca(f"svsb{i}", [128, 32]) for i in range(2)]
                ss_sb = [ca(f"sssb{i}", [128, 32]) for i in range(2)]
                s1_sb = [ca(f"s1sb{i}", [128, DT]) for i in range(2)]
                gy = [ca(f"gy{i}", [128, 512]) for i in range(3)]

            cast_i = [0]

            def cast_eng():
                cast_i[0] += 1
                return ("vector", "scalar")[cast_i[0] % 2]

            def emit_xg(ei):
                xgkeys = tuple(f"xng_e{q}" for q in range(E))
                for rt in range(CAP // 128):
                    rb = rt % 2
                    s.dma("sync", f"xg{rb}", lambda e, ei=ei, rt=rt, rb=rb: e.dma_start(
                        out=tokrow[rb][:],
                        in_=xng_d.ap()[ei * CAP + rt * 128:ei * CAP + (rt + 1) * 128, :]),
                        reads=xgkeys, writes=(f"tokrow{rb}",))
                    for d in range(DT):
                        pbk = 6 + (d // 4)
                        s.op("tensor", lambda e, d=d, rb=rb, pbk=pbk: e.transpose(
                            out=ps[pbk][:, (d % 4) * 128:(d % 4 + 1) * 128],
                            in_=tokrow[rb][:, d * 128:(d + 1) * 128], identity=ident[:]),
                            reads=(f"tokrow{rb}", "ident"), writes=(f"ps{pbk}",))
                        if d % 4 == 3:
                            for dd in range(d - 3, d + 1):
                                s.op("scalar", lambda e, dd=dd, rt=rt, ei=ei, pbk=pbk: e.copy(
                                    out=xn_g[ei % 2][dd][:, rt * 128:(rt + 1) * 128],
                                    in_=ps[pbk][:, (dd % 4) * 128:(dd % 4 + 1) * 128]),
                                    reads=(f"ps{pbk}",), writes=(f"xg{ei % 2}_{dd}",))

            if 1 in phases:
                emit_xg(0)

            for ei in (range(E) if 1 in phases else []):
                ep = ei % 2
                s.dma("scalar", f"svp{ep}", lambda e, ei=ei, ep=ep: e.dma_start(
                    out=sv_sb[ep][:], in_=svt_d[ei]), reads=(), writes=(f"svsb{ep}",))
                s.dma("scalar", f"svp{ep}", lambda e, ei=ei, ep=ep: e.dma_start(
                    out=ss_sb[ep][:], in_=st_d[ei]), reads=(), writes=(f"sssb{ep}",))
                s.dma("scalar", f"svp{ep}", lambda e, ei=ei, ep=ep: e.dma_start(
                    out=s1_sb[ep][:], in_=s1t_d[ei]), reads=(), writes=(f"s1sb{ep}",))
                for fg2 in range(16):
                    if fg2 == 8 and ei + 1 < E:
                        emit_xg(ei + 1)
                    gb = fg2 % 2
                    cq = fg2 * 256
                    s.dma("sync", f"qwv{gb}", lambda e, ei=ei, gb=gb, cq=cq: e.dma_start(
                        out=wv_bf[gb][:].rearrange("p (dt f) -> p dt f", dt=8),
                        in_=wv_dE[ei][:, cq:cq + 256].rearrange(
                            "(dt p) f -> p dt f", p=128)),
                        reads=(f"AGwv{ei}",), writes=(f"wvbf{gb}",))
                    s.dma("sync", f"qw{gb}", lambda e, ei=ei, gb=gb, cq=cq: e.dma_start(
                        out=w_bf[gb][:].rearrange("p (dt f) -> p dt f", dt=8),
                        in_=w_dE[ei][:, cq:cq + 256].rearrange(
                            "(dt p) f -> p dt f", p=128)),
                        reads=(f"AGw{ei}",), writes=(f"wbf{gb}",))
                    s.dma("sync", f"qw1{gb}", lambda e, ei=ei, gb=gb, cq=cq: e.dma_start(
                        out=w1_bf[gb][:].rearrange("p (ft d) -> p ft d", ft=2),
                        in_=w1_dE[ei][cq:cq + 256, :].rearrange(
                            "(ft p) d -> p ft d", p=128)),
                        reads=(f"AGw1{ei}",), writes=(f"w1bf{gb}",))

                    for fb2 in range(2):
                        ftg = fg2 * 2 + fb2
                        ftl = (fg2 % 2) * 2 + fb2
                        for rci, (r0, rw) in enumerate(RCS):
                            u = (ftg * 2 + rci) % 3
                            vk, yk = f"ps{u}", f"ps{3 + u}"
                            rcs = slice(r0, r0 + rw)
                            for d in range(DT):
                                s.op("tensor", lambda e, d=d, fb2=fb2, rcs=rcs, rw=rw, u=u, gb=gb, ei=ei: e.matmul(
                                    out=ps[u][:, 0:rw],
                                    lhsT=wv_bf[gb][:, d * 256 + fb2 * 128:d * 256 + (fb2 + 1) * 128],
                                    rhs=xn_g[ei % 2][d][:, rcs],
                                    start=(d == 0), stop=(d == DT - 1)),
                                    reads=(f"wvbf{gb}", f"xg{ei % 2}_{d}"), writes=(vk,))
                            for d in range(DT):
                                s.op("tensor", lambda e, d=d, fb2=fb2, rcs=rcs, rw=rw, u=u, gb=gb, ei=ei: e.matmul(
                                    out=ps[3 + u][:, 0:rw],
                                    lhsT=w_bf[gb][:, d * 256 + fb2 * 128:d * 256 + (fb2 + 1) * 128],
                                    rhs=xn_g[ei % 2][d][:, rcs],
                                    start=(d == 0), stop=(d == DT - 1)),
                                    reads=(f"wbf{gb}", f"xg{ei % 2}_{d}"), writes=(yk,))
                            s.op("scalar", lambda e, u=u, rw=rw, ftg=ftg, ep=ep: e.activation(
                                out=gy[u][:, 0:rw], in_=ps[3 + u][:, 0:rw],
                                func=AF.Gelu_apprx_tanh, scale=ss_sb[ep][:, ftg:ftg + 1]),
                                reads=(yk, f"sssb{ep}"), writes=(f"gy{u}",))
                            s.op("vector", lambda e, u=u, ftl=ftl, ftg=ftg, rcs=rcs, rw=rw, ep=ep: e.scalar_tensor_tensor(
                                out=Zp[ftl][:, rcs], in0=ps[u][:, 0:rw],
                                scalar=sv_sb[ep][:, ftg:ftg + 1], in1=gy[u][:, 0:rw],
                                op0=ALU.mult, op1=ALU.mult),
                                reads=(vk, f"gy{u}", f"svsb{ep}"), writes=(f"zp{ftl}",))
                    if fg2 % 2 == 0:
                        continue
                    first_window = (fg2 == 1)
                    for d in range(DT):
                        for rci, (r0, rw) in enumerate(RCS):
                            o = (d * 2 + rci) % 2
                            ok = f"ps{6 + o}"
                            rcs = slice(r0, r0 + rw)
                            for ftl in range(4):
                                w1b = ftl // 2
                                s.op("tensor", lambda e, d=d, ftl=ftl, rcs=rcs, rw=rw, o=o, w1b=w1b: e.matmul(
                                    out=ps[6 + o][:, 0:rw],
                                    lhsT=w1_bf[w1b][:, (ftl % 2) * 1024 + d * 128:(ftl % 2) * 1024 + (d + 1) * 128],
                                    rhs=Zp[ftl][:, rcs],
                                    start=(ftl == 0), stop=(ftl == 3)),
                                    reads=(f"w1bf{w1b}", f"zp{ftl}"), writes=(ok,))
                            if first_window:
                                s.op("vector", lambda e, d=d, rcs=rcs, rw=rw, o=o, ep=ep: e.tensor_scalar(
                                    out=po[d][:, rcs], in0=ps[6 + o][:, 0:rw],
                                    scalar1=s1_sb[ep][:, d:d + 1], scalar2=None,
                                    op0=ALU.mult),
                                    reads=(ok, f"s1sb{ep}"), writes=(f"po{d}",))
                            else:
                                s.op("vector", lambda e, d=d, rcs=rcs, rw=rw, o=o, ep=ep: e.scalar_tensor_tensor(
                                    out=po[d][:, rcs], in0=ps[6 + o][:, 0:rw],
                                    scalar=s1_sb[ep][:, d:d + 1], in1=po[d][:, rcs],
                                    op0=ALU.mult, op1=ALU.add),
                                    reads=(ok, f"s1sb{ep}", f"po{d}"), writes=(f"po{d}",))

                for rt in range(CAP // 128):
                    rb = rt % 2
                    for d in range(DT):
                        pbk = 7 - (d // 4)
                        s.op("tensor", lambda e, d=d, rt=rt, pbk=pbk: e.transpose(
                            out=ps[pbk][:, (d % 4) * 128:(d % 4 + 1) * 128],
                            in_=po[d][:, rt * 128:(rt + 1) * 128], identity=ident[:]),
                            reads=(f"po{d}", "ident"), writes=(f"ps{pbk}",))
                        if d % 4 == 3:
                            for dd in range(d - 3, d + 1):
                                s.op("vector", lambda e, dd=dd, rb=rb, pbk=pbk: e.tensor_copy(
                                    out=ptk[rb][:, dd * 128:(dd + 1) * 128],
                                    in_=ps[pbk][:, (dd % 4) * 128:(dd % 4 + 1) * 128]),
                                    reads=(f"ps{pbk}",), writes=(f"ptok{rb}",))
                    s.dma("sync", f"pb{rb}", lambda e, ei=ei, rt=rt, rb=rb: e.dma_start(
                        out=pairbuf_d.ap()[ei * CAP + rt * 128:ei * CAP + (rt + 1) * 128, :],
                        in_=ptk[rb][:]),
                        reads=(f"ptok{rb}",), writes=("pairbuf",))

            for tt in (range(S // 128) if 1 in phases else []):
                mb_ = tt % 2
                s.dma("gpsimd", f"gth{mb_}", lambda e, tt=tt, mb_=mb_: e.indirect_dma_start(
                    out=gb0[mb_][:], out_offset=None,
                    in_=pairbuf_d.ap(),
                    in_offset=bass.IndirectOffsetOnAxis(ap=pos0i[:, tt:tt + 1], axis=0)),
                    reads=("pairbuf", "pos0i"), writes=(f"gb0_{mb_}",))
                s.dma("gpsimd", f"gth{mb_}", lambda e, tt=tt, mb_=mb_: e.indirect_dma_start(
                    out=gb1[mb_][:], out_offset=None,
                    in_=pairbuf_d.ap(),
                    in_offset=bass.IndirectOffsetOnAxis(ap=pos1i[:, tt:tt + 1], axis=0)),
                    reads=("pairbuf", "pos1i"), writes=(f"gb1_{mb_}",))
                s.op("vector", lambda e, tt=tt, mb_=mb_: e.tensor_scalar(
                    out=mo[mb_][:], in0=gb0[mb_][:], scalar1=g0_sb[:, tt:tt + 1],
                    scalar2=None, op0=ALU.mult),
                    reads=(f"gb0_{mb_}", "g0_sb"), writes=(f"mo{mb_}",))
                s.op("vector", lambda e, tt=tt, mb_=mb_: e.scalar_tensor_tensor(
                    out=mo[mb_][:], in0=gb1[mb_][:], scalar=g1_sb[:, tt:tt + 1],
                    in1=mo[mb_][:], op0=ALU.mult, op1=ALU.add),
                    reads=(f"gb1_{mb_}", "g1_sb", f"mo{mb_}"), writes=(f"mo{mb_}",))
                s.dma("gpsimd", "outadd", lambda e, tt=tt, mb_=mb_: e.dma_start(
                    out=out_d[tt * 128:(tt + 1) * 128, :], in_=mo[mb_][:],
                    accum_op=ALU.add),
                    reads=(f"mo{mb_}", f"outrow{tt}"), writes=(f"outrow{tt}",), bag=True)
            s.join()

        with ExitStack() as ctx2:
            if 2 in phases:
                inner = [[sb(f"inner{b}_{c}", [128, T + 2], BF16, c=ctx2) for c in range(CT)]
                         for b in range(2)]
                sg = [sb(f"sg_{c}", [128, T], BF16, c=ctx2) for c in range(CT)]
                h_bf = [sb(f"h{c}", [128, T], BF16, c=ctx2) for c in range(CT)]
                Bb = [sb(f"Bb{k}", [128, T], BF16, c=ctx2) for k in range(K)]
                brow = [sb(f"brow{i}", [1, T], c=ctx2) for i in range(1)]
                wi_bf = [sb(f"wibf{i}", [128, 4096], BF16, c=ctx2) for i in range(1)]
                wt_bf = [sb(f"wtbf{i}", [128, 4096], BF16, c=ctx2) for i in range(2)]
                wo_bf = [sb(f"wobf{i}", [128, 4096], BF16, c=ctx2) for i in range(1)]
                att = [sb(f"att{d}", [128, T], c=ctx2) for d in range(DT)]
                ones33 = sb("ones33", [33, 128], c=ctx2)
                mem_all = sb("mem_all", [128, K * T], BF16, c=ctx2)
                dA = [sb(f"dA{i}", [128, T], BF16, c=ctx2) for i in range(2)]
                dB = [sb(f"dB{i}", [128, T], BF16, c=ctx2) for i in range(2)]
                dtb = [sb(f"dtb{i}", [128, T], c=ctx2) for i in range(2)]
                g_sc2 = [sb(f"g_sc{i}", [128, T], BF16, c=ctx2) for i in range(2)]
                y_sc2 = [sb(f"y_sc{i}", [128, T], c=ctx2) for i in range(2)]
                c1 = sb("c1", [128, T], c=ctx2)
                c2 = sb("c2", [128, T], c=ctx2)
                carr = [sb(f"carr{c}", [128, K], c=ctx2) for c in range(CT)]
                cw_sb = [sb(f"cw{c}", [128, 3], c=ctx2) for c in range(CT)]
                parm = [sb(f"parm{c}", [128, 4], c=ctx2) for c in range(CT)]
                cbp = [parm[c][:, 0:1] for c in range(CT)]
                btp = [parm[c][:, 1:2] for c in range(CT)]
                skp = [parm[c][:, 2:3] for c in range(CT)]
                nA = [sb(f"nA{c}", [128, K], c=ctx2) for c in range(CT)]
                wst_bf = [sb(f"wst{c}", [128, 16], BF16, c=ctx2) for c in range(CT)]
                wst_f = sb("wst_f", [128, 32], c=ctx2)

            for c in (range(CT) if 2 in phases else []):
                s.dma("scalar", "c_par", lambda e, c=c: e.dma_start(
                    out=cw_sb[c][:], in_=convw_d[c]), writes=(f"cw{c}",), bag=True)
                s.dma("scalar", "c_par", lambda e, c=c: e.dma_start(
                    out=cbp[c], in_=convb_d[c]), writes=(f"parm{c}",), bag=True)
                s.dma("scalar", "c_par", lambda e, c=c: e.dma_start(
                    out=btp[c], in_=btime_d[c]), writes=(f"parm{c}",), bag=True)
                s.dma("scalar", "c_par", lambda e, c=c: e.dma_start(
                    out=skp[c], in_=skip_d[c]), writes=(f"parm{c}",), bag=True)
                s.dma("scalar", "asbq", lambda e, c=c: e.dma_start(
                    out=wst_f[:, 0:K], in_=alog_d[c]), reads=(), writes=("wst_f",))
                s.op("scalar", lambda e, c=c: e.activation(
                    out=nA[c][:], in_=wst_f[:, 0:K], func=AF.Exp),
                    reads=("wst_f",), writes=(f"nA{c}",))
                s.op("vector", lambda e, c=c: e.tensor_scalar(
                    out=nA[c][:], in0=nA[c][:], scalar1=-1.0, scalar2=None,
                    op0=ALU.mult), reads=(f"nA{c}",), writes=(f"nA{c}",))
                s.dma("scalar", "wstq", lambda e, c=c: e.dma_start(
                    out=wst_f[:], in_=wst_d[c * 128:(c + 1) * 128, :]),
                    reads=(), writes=("wst_f",))
                s.op("vector", lambda e, c=c: e.tensor_copy(
                    out=wst_bf[c][:], in_=wst_f[:, 16:32]), reads=("wst_f",), writes=(f"wst{c}",))
                s.op("gpsimd", lambda e, c=c: e.memset(carr[c][:], 0.0),
                     writes=(f"carr{c}",))
                s.op("gpsimd", lambda e, c=c: e.memset(inner[0][c][:, 0:1], 0.0),
                     writes=(f"in0_{c}",))
            if 2 in phases:
                s.op("vector", lambda e: e.memset(ones33[:], 1.0),
                     writes=("ones33",))

            def emit_zx_part(k, gate, cgs=None):
                kb = k % 2
                tcs = slice(k * T, (k + 1) * T)
                for cg in (cgs if cgs is not None
                           else (range(4, 8) if gate else range(4))):
                    for hf in range(2):
                        s.dma("sync", f"qwi{hf}", lambda e, cg=cg, hf=hf: e.dma_start(
                            out=wi_bf[0][:, hf * 2048:(hf + 1) * 2048].rearrange(
                                "p (dt f) -> p dt f", dt=4),
                            in_=win_d[hf * 512:(hf + 1) * 512,
                                      cg * 512:(cg + 1) * 512].rearrange(
                                "(dt p) f -> p dt f", p=128)),
                            reads=("AGwi",), writes=(f"wibf{hf}",))
                    for pair in range(2):
                        for hf in range(2):
                            for cb2 in (pair * 2, pair * 2 + 1):
                                cbg = cg * 4 + cb2
                                pk = f"ps{cbg % 2}"
                                for d in range(hf * 4, hf * 4 + 4):
                                    s.op("tensor", lambda e, d=d, cb2=cb2, tcs=tcs, cbg=cbg: e.matmul(
                                        out=ps[cbg % 2][:, 0:T],
                                        lhsT=wi_bf[0][:, d * 512 + cb2 * 128:d * 512 + (cb2 + 1) * 128],
                                        rhs=xn_bf[d][:, tcs],
                                        start=(d == 0), stop=(d == DT - 1)),
                                        reads=(f"wibf{hf}", f"xnbf{d}"), writes=(pk,))
                                if hf == 0:
                                    continue
                                if cbg < 16:
                                    s.op("scalar", lambda e, cbg=cbg, kb=kb: e.copy(
                                        out=inner[kb][cbg][:, 1:T + 1], in_=ps[cbg % 2][:, 0:T]),
                                        reads=(pk,), writes=(f"in{kb}_{cbg}",))
                                else:
                                    gbi = cbg - 16
                                    s.op("scalar", lambda e, gbi=gbi, cbg=cbg: e.activation(
                                        out=sg[gbi][:], in_=ps[cbg % 2][:, 0:T], func=AF.Silu),
                                        reads=(pk,), writes=(f"sg_{gbi}",))

            def emit_prep(kn):
                pb = kn % 2
                qb = (kn - 1) % 2
                for c in range(CT):
                    s.op("gpsimd", lambda e, c=c, pb=pb, qb=qb: e.tensor_copy(
                        out=inner[pb][c][:, 0:1], in_=inner[qb][c][:, T:T + 1]),
                        reads=(f"in{qb}_{c}",), writes=(f"in{pb}_{c}",))
                emit_zx_part(kn, False)
                for c in range(CT):
                    s.op("gpsimd", lambda e, c=c, pb=pb, qb=qb: e.tensor_copy(
                        out=inner[qb][c][:, T + 1:T + 2], in_=inner[pb][c][:, 1:2]),
                        reads=(f"in{pb}_{c}",), writes=(f"in{qb}_{c}",))

            def emit_store(ks):
                for tt2 in range(T // 128):
                    tt = ks * (T // 128) + tt2
                    ob = out_tok[0]
                    okey = "otok0"
                    for half in range(2):
                        pk = f"ps{4 + half}"
                        for j in range(4):
                            d = half * 4 + j
                            s.op("tensor", lambda e, d=d, j=j, half=half, tt2=tt2: e.transpose(
                                out=ps[4 + half][:, j * 128:(j + 1) * 128],
                                in_=att[d][:, tt2 * 128:(tt2 + 1) * 128], identity=ident[:]),
                                reads=(f"att{d}", "ident"), writes=(pk,))
                        s.op("scalar", lambda e, half=half, ob=ob: e.copy(
                            out=ob[:, half * 512:(half + 1) * 512], in_=ps[4 + half][:, 0:512]),
                            reads=(pk,), writes=(okey,))
                    s.dma("gpsimd", "outadd", lambda e, tt=tt, ob=ob: e.dma_start(
                        out=out_d[tt * 128:(tt + 1) * 128, :], in_=ob[:],
                        accum_op=ALU.add),
                        reads=(okey, f"outrow{tt}"), writes=(f"outrow{tt}",), bag=True)

            def emit_wout(ks):
                for dg in range(4):
                    for hf in range(2):
                        s.dma("sync", f"qwo{hf}", lambda e, dg=dg, hf=hf: e.dma_start(
                            out=wo_bf[0][:, hf * 2048:(hf + 1) * 2048].rearrange(
                                "p (ct f) -> p ct f", ct=8),
                            in_=wout_d[hf * 1024:(hf + 1) * 1024,
                                       dg * 256:(dg + 1) * 256].rearrange(
                                "(ct p) f -> p ct f", p=128)),
                            reads=("AGwo",), writes=(f"wobf{hf}",))
                    for hf in range(2):
                        for db2 in range(2):
                            d = dg * 2 + db2
                            pk = f"ps{6 + d % 2}"
                            for c in range(hf * 8, hf * 8 + 8):
                                s.op("tensor", lambda e, c=c, db2=db2, d=d: e.matmul(
                                    out=ps[6 + d % 2][:, 0:T],
                                    lhsT=wo_bf[0][:, c * 256 + db2 * 128:c * 256 + (db2 + 1) * 128],
                                    rhs=sg[c][:],
                                    start=(c == 0), stop=(c == CT - 1)),
                                    reads=(f"wobf{hf}", f"sg_{c}"), writes=(pk,))
                            if hf == 1:
                                s.op("scalar", lambda e, d=d: e.copy(
                                    out=att[d][:], in_=ps[6 + d % 2][:, 0:T]),
                                    reads=(pk,), writes=(f"att{d}",))

            if 2 in phases:
                emit_zx_part(0, False)
                emit_prep(1)
            for k in (range(NCH) if 2 in phases else []):
                kb = k % 2
                nkb = (k + 1) % 2
                s.mark(f"conv{k}")

                for c in range(CT):
                    ik = f"in{kb}_{c}"
                    s.op("vector", lambda e, c=c, kb=kb: e.tensor_scalar(
                        out=c1[:], in0=inner[kb][c][:, 0:T], scalar1=cw_sb[c][:, 0:1],
                        scalar2=None, op0=ALU.mult),
                        reads=(ik, f"cw{c}"), writes=("c1",))
                    s.op("vector", lambda e, c=c, kb=kb: e.scalar_tensor_tensor(
                        out=c2[:], in0=inner[kb][c][:, 1:T + 1], scalar=cw_sb[c][:, 1:2],
                        in1=c1[:], op0=ALU.mult, op1=ALU.add),
                        reads=(ik, f"cw{c}", "c1"), writes=("c2",))
                    s.op("vector", lambda e, c=c, kb=kb: e.scalar_tensor_tensor(
                        out=h_bf[c][:], in0=inner[kb][c][:, 2:T + 2], scalar=cw_sb[c][:, 2:3],
                        in1=c2[:], op0=ALU.mult, op1=ALU.add),
                        reads=(ik, f"cw{c}", "c2"), writes=(f"h{c}",))
                    s.op("scalar", lambda e, c=c: e.activation(
                        out=h_bf[c][:], in_=h_bf[c][:], func=AF.Silu, bias=cbp[c]),
                        reads=(f"h{c}", f"parm{c}"), writes=(f"h{c}",))

                for c in range(CT):
                    s.op("tensor", lambda e, c=c: e.matmul(
                        out=ps[4][0:16, 0:T], lhsT=wst_bf[c][:], rhs=h_bf[c][:],
                        start=(c == 0), stop=(c == CT - 1)),
                        reads=(f"wst{c}", f"h{c}"), writes=("ps4",))
                s.op("scalar", lambda e: e.copy(out=out_tok[0][0:16, 0:T], in_=ps[4][0:16, 0:T]),
                     reads=("ps4",), writes=("otok0",))
                for kk in range(K):
                    bp = kk % 2
                    bb_pk = 5 - bp
                    bbuf = brow[0][:] if bp == 0 else out_tok[0][32:33, 0:T]
                    bkey = "brow0" if bp == 0 else "otok0"
                    s.dma("scalar", f"brw{bp}", lambda e, kk=kk, bbuf=bbuf: e.dma_start(
                        out=bbuf, in_=out_tok[0][kk:kk + 1, 0:T]),
                        reads=("otok0",), writes=(bkey,))
                    lT = (lambda: ones1[:]) if bp == 0 else (lambda: ones33[32:33, :])
                    lk = "ones1" if bp == 0 else "ones33"
                    s.op("tensor", lambda e, bbuf=bbuf, bb_pk=bb_pk, lT=lT: e.matmul(
                        out=ps[bb_pk][:, 0:T], lhsT=lT(), rhs=bbuf[0:1, :],
                        start=True, stop=True),
                        reads=(bkey, lk), writes=(f"ps{bb_pk}",))
                    s.op("scalar", lambda e, kk=kk, bb_pk=bb_pk: e.copy(
                        out=Bb[kk][:], in_=ps[bb_pk][:, 0:T]),
                        reads=(f"ps{bb_pk}",), writes=(f"Bb{kk}",))

                def emit_dt_mm(cog, cb2, wb):
                    cb = cog * 2 + cb2
                    dp = cb % 2
                    pk = f"ps{2 + dp}"
                    for c in range(CT):
                        s.op("tensor", lambda e, c=c, cb2=cb2, dp=dp, wb=wb: e.matmul(
                            out=ps[2 + dp][:, 0:T],
                            lhsT=wt_bf[wb][:, c * 256 + cb2 * 128:c * 256 + (cb2 + 1) * 128],
                            rhs=h_bf[c][:],
                            start=(c == 0), stop=(c == CT - 1)),
                            reads=(f"wtbf{wb}", f"h{c}"), writes=(pk,))

                for cog in range(8):
                    if cog == 2:
                        emit_zx_part(k, True, cgs=[5])
                        if k + 2 <= NCH - 1:
                            emit_prep(k + 2)
                        elif k + 2 == NCH:
                            for c in range(CT):
                                s.op("gpsimd", lambda e, c=c, nkb=nkb: e.memset(
                                    inner[nkb][c][:, T + 1:T + 2], 0.0),
                                    writes=(f"in{nkb}_{c}",))
                        if k > 0:
                            emit_store(k - 1)
                    elif cog == 4:
                        emit_zx_part(k, True, cgs=[6])
                    elif cog == 6:
                        emit_zx_part(k, True, cgs=[7])
                    wb = (k * 8 + cog) % 2
                    s.dma("sync", f"qwt{wb}", lambda e, cog=cog, wb=wb: e.dma_start(
                        out=wt_bf[wb][:].rearrange("p (ct f) -> p ct f", ct=16),
                        in_=wtime_d[:, cog * 256:(cog + 1) * 256].rearrange(
                            "(ct p) f -> p ct f", p=128)),
                        reads=("AGwt",), writes=(f"wtbf{wb}",))
                    if cog == 0:
                        emit_dt_mm(0, 0, wb)
                        emit_dt_mm(0, 1, wb)
                        if k > 0:
                            emit_wout(k - 1)
                        emit_zx_part(k, True, cgs=[4])
                    for cb2 in range(2):
                        cb = cog * 2 + cb2
                        dp = cb % 2
                        pk = f"ps{2 + dp}"
                        if cog != 0:
                            emit_dt_mm(cog, cb2, wb)
                        s.op("scalar", lambda e, cb=cb, dp=dp: e.activation(
                            out=c1[:], in_=ps[2 + dp][:, 0:T], func=AF.Abs,
                            bias=btp[cb]), reads=(pk, f"parm{cb}"), writes=("c1",))
                        s.op("scalar", lambda e: e.activation(
                            out=c2[:], in_=c1[:], func=AF.Exp, scale=-1.0),
                            reads=("c1",), writes=("c2",))
                        s.op("scalar", lambda e: e.activation(
                            out=c1[:], in_=c2[:], func=AF.Ln, bias=1.0),
                            reads=("c2",), writes=("c1",))
                        s.op("vector", lambda e, cb=cb, dp=dp: e.tensor_scalar(
                            out=y_sc2[cb % 2][:], in0=ps[2 + dp][:, 0:T], scalar1=btp[cb],
                            scalar2=0.0, op0=ALU.add, op1=ALU.max),
                            reads=(pk, f"parm{cb}"), writes=(f"y_sc{cb % 2}",))
                        s.op("vector", lambda e, dp=dp, cb=cb: e.tensor_tensor(
                            out=dtb[dp][:], in0=y_sc2[cb % 2][:], in1=c1[:], op=ALU.add),
                            reads=(f"y_sc{cb % 2}", "c1"), writes=(f"dtb{dp}",))
                        s.op("vector", lambda e, cb=cb, dp=dp: e.tensor_tensor(
                            out=g_sc2[cb % 2][:], in0=dtb[dp][:], in1=h_bf[cb][:], op=ALU.mult),
                            reads=(f"dtb{dp}", f"h{cb}"), writes=(f"g_sc{cb % 2}",))
                        s.op("vector", lambda e, cb=cb: e.tensor_scalar(
                            out=y_sc2[cb % 2][:], in0=h_bf[cb][:], scalar1=skp[cb],
                            scalar2=None, op0=ALU.mult),
                            reads=(f"h{cb}", f"parm{cb}"), writes=(f"y_sc{cb % 2}",))
                        yeng = "gpsimd" if cb % 2 == 0 else "vector"
                        for kk in range(K):
                            ap = kk % 2
                            s.op("scalar", lambda e, cb=cb, kk=kk, ap=ap, dp=dp: e.activation(
                                out=dA[ap][:], in_=dtb[dp][:], func=AF.Exp,
                                scale=nA[cb][:, kk:kk + 1]),
                                reads=(f"dtb{dp}", f"nA{cb}"), writes=(f"dA{ap}",))
                            dbeng = "vector"
                            s.op(dbeng, lambda e, kk=kk, ap=ap, cb=cb: e.tensor_tensor(
                                out=dB[ap][:], in0=g_sc2[cb % 2][:], in1=Bb[kk][:], op=ALU.mult),
                                reads=(f"g_sc{cb % 2}", f"Bb{kk}"), writes=(f"dB{ap}",))
                            s.op("vector", lambda e, kk=kk, ap=ap, cb=cb: e.tensor_tensor_scan(
                                out=mem_all[:, kk * T:(kk + 1) * T], data0=dA[ap][:],
                                data1=dB[ap][:], initial=carr[cb][:, kk:kk + 1],
                                op0=ALU.mult, op1=ALU.add),
                                reads=(f"dA{ap}", f"dB{ap}", f"carr{cb}"),
                                writes=(f"mem{kk}",))
                            s.op(yeng, lambda e, kk=kk, cb=cb: e.tensor_tensor(
                                out=y_sc2[cb % 2][:], in0=y_sc2[cb % 2][:],
                                in1=mem_all[:, kk * T:(kk + 1) * T], op=ALU.add),
                                reads=(f"y_sc{cb % 2}", f"mem{kk}"), writes=(f"y_sc{cb % 2}",))
                        s.op("gpsimd", lambda e, cb=cb: e.tensor_copy(
                            out=carr[cb][:], in_=mem_all[:, T - 1::T]),
                            reads=tuple(f"mem{kk}" for kk in range(K)),
                            writes=(f"carr{cb}",))
                        s.op("vector", lambda e, cb=cb, kb=kb: e.tensor_tensor(
                            out=sg[cb][:], in0=y_sc2[cb % 2][:], in1=sg[cb][:], op=ALU.mult),
                            reads=(f"y_sc{cb % 2}",), writes=(f"sg_{cb}",))

            if 2 in phases:
                emit_wout(NCH - 1)
                emit_store(NCH - 1)

            s.join()
            s.op("sync", lambda e: e.nop())

        _LAST_SCHED.clear()
        _LAST_SCHED.append(sched)
        with nc.Block() as block:
            sched.finalize(block)
    return nc


_NC_CACHE = {}


def kernel(**inputs):
    x_full = np.ascontiguousarray(np.asarray(inputs["L0_attn_x"], dtype=np.float32))
    f32 = lambda a: np.ascontiguousarray(np.asarray(a, dtype=np.float32))

    win = f32(inputs["L0_attn_W_in"])
    convw = f32(inputs["L0_attn_conv_w"]).reshape(CT, 128, 3)
    convb = f32(inputs["L0_attn_conv_b"]).reshape(CT, 128, 1)
    wst = f32(inputs["L0_attn_W_state"])
    wtime = f32(inputs["L0_attn_W_time"])
    btime = f32(inputs["L0_attn_b_time"]).reshape(CT, 128, 1)
    alog = f32(inputs["L0_attn_A_log"]).reshape(CT, 128, K)
    skip = f32(inputs["L0_attn_skip"]).reshape(CT, 128, 1)
    wout = f32(inputs["L0_attn_W_out"])
    rw = f32(inputs["L0_ffn_router_w"])
    wv = f32(inputs["L0_ffn_wv"])
    sv_t = np.ascontiguousarray(f32(inputs["L0_ffn_sv"]).reshape(E, F // 128, 128).transpose(0, 2, 1))
    w = f32(inputs["L0_ffn_w"])
    s_t = np.ascontiguousarray(f32(inputs["L0_ffn_s"]).reshape(E, F // 128, 128).transpose(0, 2, 1))
    w1 = f32(inputs["L0_ffn_w1"])
    s1_t = np.ascontiguousarray(f32(inputs["L0_ffn_s1"]).reshape(E, DT, 128).transpose(0, 2, 1))
    ident = np.eye(128, dtype=np.float32)
    ones1 = np.ones((1, 128), dtype=np.float32)

    if "nc" not in _NC_CACHE:
        _NC_CACHE["nc"] = build_program()
    nc = _NC_CACHE["nc"]

    common = dict(conv_w=convw, conv_b=convb, w_state=wst,
                  b_time=btime, a_log=alog, skip=skip,
                  router_w=rw, sv_t=sv_t, s_t=s_t,
                  s1_t=s1_t, ident=ident, ones1=ones1,
                  tril=np.tril(np.ones((128, 128), np.float32), -1),
                  onescol=np.ones((128, 1), np.float32),
                  ecap=(np.arange(E, dtype=np.float32) * CAP)[None, :],
                  capthr=np.ascontiguousarray(np.broadcast_to(
                      (np.arange(E, dtype=np.float32) + 1) * CAP, (128, E))),
                  trmap=np.ascontiguousarray(np.broadcast_to(
                      GROWS + np.arange(E, dtype=np.float32), (128, E))))
    in_maps = [dict(common,
                    x=np.ascontiguousarray(x_full[c]),
                    wv_sh=np.ascontiguousarray(
                        wv[:, c * 128:(c + 1) * 128, :].reshape(E * 128, F)),
                    w_sh=np.ascontiguousarray(
                        w[:, c * 128:(c + 1) * 128, :].reshape(E * 128, F)),
                    w1_sh=np.ascontiguousarray(
                        w1[:, c * 512:(c + 1) * 512, :].reshape(E * 512, D)),
                    wi_sh=np.ascontiguousarray(win[c * (D // 8):(c + 1) * (D // 8), :]),
                    wt_sh=np.ascontiguousarray(wtime[c * (C // 8):(c + 1) * (C // 8), :]),
                    wo_sh=np.ascontiguousarray(wout[c * (C // 8):(c + 1) * (C // 8), :]))
               for c in range(N_CORES)]
    res = run_bass_kernel_spmd(nc, in_maps, core_ids=list(range(N_CORES)))
    out = np.stack([res.results[c]["out"] for c in range(N_CORES)], axis=0)
    return out


# version 32 (final)
# speedup vs baseline: 1.0245x; 1.0245x over previous
import numpy as np
from contextlib import ExitStack

import concourse.bass as bass
import concourse.mybir as mybir
from concourse.bass_utils import run_bass_kernel_spmd

F32 = mybir.dt.float32
BF16 = mybir.dt.bfloat16
AF = mybir.ActivationFunctionType
ALU = mybir.AluOpType
AX = mybir.AxisListType

S, D, C, K, E, F = 2048, 1024, 2048, 16, 8, 4096
T = 512
NCH = S // T
DT = D // 128
CT = C // 128
EPS = 1e-5
N_CORES = 8
CAP = 768
GROWS = E * CAP
RCS = ((0, 512), (512, 256))
HUGE = float(1 << 22)


class Sched:
    ENG = ("sync", "scalar", "vector", "tensor", "gpsimd")

    def __init__(self, nc, ctx):
        self.nc = nc
        self.ctx = ctx
        self.sems = {}
        self.cnt = {}
        self.bags = set()
        for name in self.ENG:
            self.sems[name] = ctx.enter_context(nc.semaphore("sem_" + name))
            self.cnt[name] = 0
        self.ops = {e: [] for e in self.ENG}
        self.resw = {}
        self.resr = {}
        self.waited = {e: {} for e in self.ENG}
        self.pending = {e: [] for e in self.ENG}
        self.marks = []

    def mark(self, label):
        self.marks.append((label, dict(self.cnt)))

    def _qsem(self, stream, bag):
        if stream not in self.sems:
            self.sems[stream] = self.ctx.enter_context(
                self.nc.semaphore("sem_" + stream))
            self.cnt[stream] = 0
            if bag:
                self.bags.add(stream)
        return self.sems[stream]

    def _add_wait(self, eng, waits, actor, val):
        if actor in self.bags:
            val = self.cnt[actor]
        if val <= 0:
            return
        if eng == "tensor" and actor == "tensor":
            return
        if self.waited[eng].get(actor, 0) >= val:
            return
        self.waited[eng][actor] = val
        waits.append((self.sems[actor], val))

    def _gather(self, eng, actor, reads, writes):
        skip = actor if actor not in self.ENG else "\x00none"
        waits = []
        for a, v in self.pending[eng]:
            self._add_wait(eng, waits, a, v)
        self.pending[eng] = []
        for r in reads:
            w = self.resw.get(r)
            if w is not None and w[0] != skip:
                self._add_wait(eng, waits, w[0], w[1])
        for r in writes:
            w = self.resw.get(r)
            if w is not None and w[0] != skip:
                self._add_wait(eng, waits, w[0], w[1])
            for a, v in self.resr.get(r, {}).items():
                if a != skip:
                    self._add_wait(eng, waits, a, v)
        return waits

    def _commit(self, actor, val, reads, writes):
        for r in reads:
            d = self.resr.setdefault(r, {})
            d[actor] = max(d.get(actor, 0), val)
        for r in writes:
            self.resw[r] = (actor, val)
            self.resr[r] = {}

    @staticmethod
    def _excl_psum(reads, writes):
        pr = tuple(r for r in reads if r.startswith("ps"))
        if not pr:
            return reads, writes
        return (tuple(r for r in reads if not r.startswith("ps")),
                tuple(writes) + pr)

    def op(self, eng, fn, reads=(), writes=()):
        reads, writes = self._excl_psum(reads, writes)
        waits = self._gather(eng, eng, reads, writes)
        self.cnt[eng] += 1
        self.ops[eng].append((fn, waits, (self.sems[eng], 1)))
        self._commit(eng, self.cnt[eng], reads, writes)

    def dma(self, eng, stream, fn, reads=(), writes=(), bag=False, inc=16):
        reads, writes = self._excl_psum(reads, writes)
        sem = self._qsem(stream, False)
        waits = self._gather(eng, stream, reads, writes)
        if self.cnt[stream] > 0:
            self._add_wait(eng, waits, stream, self.cnt[stream])
        self.cnt[stream] += inc
        self.ops[eng].append((fn, waits, (sem, inc)))
        self._commit(stream, self.cnt[stream], reads, writes)

    def join(self):
        for e in self.ENG:
            pend = []
            for a in self.cnt:
                if a != e and self.cnt[a] > 0:
                    pend.append((a, self.cnt[a]))
            self.pending[e] = pend

    def finalize(self, block):
        for e in self.ENG:
            if not self.ops[e]:
                continue
            oplist = self.ops[e]

            def body(eng, oplist=oplist):
                for fn, waits, inc in oplist:
                    for s, v in waits:
                        eng.wait_ge(s, v)
                    inst = fn(eng)
                    inst.then_inc(inc[0], inc[1])
            getattr(block, e)(body)


def _ecopy(e, out, in_):
    if hasattr(e, "tensor_copy"):
        return e.tensor_copy(out=out, in_=in_)
    return e.copy(out=out, in_=in_)


_LAST_SCHED = []


def build_program(phases=(0, 1, 2), p0router=True, agmode=2):
    nc = bass.Bass("TRN2", target_bir_lowering=False, debug=False)
    dram = {}

    def din(name, shape):
        dram[name] = nc.dram_tensor(name, shape, F32, kind="ExternalInput").ap()
        return dram[name]

    x_d = din("x", [S, D])
    wi_sh = din("wi_sh", [D // 8, 2 * C])
    convw_d = din("conv_w", [CT, 128, 3])
    convb_d = din("conv_b", [CT, 128, 1])
    wst_d = din("w_state", [C, 32])
    wt_sh = din("wt_sh", [C // 8, C])
    btime_d = din("b_time", [CT, 128, 1])
    alog_d = din("a_log", [CT, 128, K])
    skip_d = din("skip", [CT, 128, 1])
    wo_sh = din("wo_sh", [C // 8, D])
    rw_d = din("router_w", [D, E])
    wv_sh = din("wv_sh", [D, F])
    svt_d = din("sv_t", [E, 128, F // 128])
    w_sh = din("w_sh", [D, F])
    st_d = din("s_t", [E, 128, F // 128])
    w1_sh = din("w1_sh", [F, D])
    s1t_d = din("s1_t", [E, 128, DT])
    ident_d = din("ident", [128, 128])
    ones_d = din("ones1", [1, 128])
    tril_d = din("tril", [128, 128])
    onescol_d = din("onescol", [128, 1])
    ecap_d = din("ecap", [1, E])
    capthr_d = din("capthr", [128, E])
    trmap_d = din("trmap", [128, E])
    out_d = nc.dram_tensor("out", [S, D], F32, kind="ExternalOutput").ap()
    def dint(name, shape):
        return nc.dram_tensor(name, shape, BF16, kind="Internal",
                              addr_space="Shared")
    def dintl(name, shape):
        return nc.dram_tensor(name, shape, BF16, kind="Internal")
    stg = {k: dintl(k + "_st", shp) for k, shp in
           [("wt", [C // 8, C]), ("wi", [D // 8, 2 * C]), ("wo", [C // 8, D])]}
    stgE = {k: [dintl(f"{k}_se{e2}", shp) for e2 in range(E)] for k, shp in
            [("wv", [D // 8, F]), ("w", [D // 8, F]), ("w1", [F // 8, D])]}
    wv_allE = [dint(f"wv_ae{e2}", [D, F]) for e2 in range(E)]
    w_allE = [dint(f"w_ae{e2}", [D, F]) for e2 in range(E)]
    w1_allE = [dint(f"w1_ae{e2}", [F, D]) for e2 in range(E)]
    wt_all = dint("wt_all", [C, C])
    wi_all = dint("wi_all", [D, 2 * C])
    wo_all = dint("wo_all", [C, D])
    win_d = wi_all.ap()
    wtime_d = wt_all.ap()
    wout_d = wo_all.ap()
    wv_dE = [t.ap() for t in wv_allE]
    w_dE = [t.ap() for t in w_allE]
    w1_dE = [t.ap() for t in w1_allE]
    xng_d = nc.dram_tensor("xng", [GROWS + 8, D], F32, kind="Internal")
    pairbuf_d = nc.dram_tensor("pairbuf", [GROWS, D], F32, kind="Internal")

    with ExitStack() as ctx:
        sched = Sched(nc, ctx)

        def sb(name, shape, dt=F32, c=ctx):
            return c.enter_context(nc.sbuf_tensor(name, shape, dt))

        xn_bf = [sb(f"xn_bf{d}", [128, S], BF16) for d in range(DT)]
        ident = sb("ident_sb", [128, 128])
        ones1 = sb("ones_sb", [1, 128])
        pos0i = sb("pos0i", [128, S // 128], mybir.dt.int32)
        pos1i = sb("pos1i", [128, S // 128], mybir.dt.int32)
        g0_sb = sb("g0_sb", [128, S // 128])
        g1_sb = sb("g1_sb", [128, S // 128])
        out_tok = [sb(f"out_tok{i}", [128, D]) for i in range(1)]

        ps = [ctx.enter_context(nc.psum_tensor(f"ps{i}", [128, 512], F32))
              for i in range(8)]

        s = sched
        s.dma("sync", "c_const", lambda e: e.dma_start(out=ident[:], in_=ident_d),
              writes=("ident",), bag=True)
        s.dma("sync", "c_const", lambda e: e.dma_start(out=ones1[:], in_=ones_d),
              writes=("ones1",), bag=True)
        s.dma("sync", "c_const", lambda e: e.dma_start(out=tril[:], in_=tril_d),
              writes=("tril",), bag=True)
        s.dma("sync", "c_const", lambda e: e.dma_start(out=onescol[:], in_=onescol_d),
              writes=("onescol",), bag=True)
        s.dma("sync", "c_const", lambda e: e.dma_start(out=capthr[:], in_=capthr_d),
              writes=("capthr",), bag=True)
        s.dma("sync", "c_const", lambda e: e.dma_start(out=trmap[:], in_=trmap_d),
              writes=("trmap",), bag=True)
        s.dma("sync", "outw", lambda e: e.dma_start(out=out_d, in_=x_d),
              writes=tuple(f"outrow{tt}" for tt in range(S // 128)), bag=True)


        with ExitStack() as ctxA:
            tril = ctxA.enter_context(nc.sbuf_tensor("tril_sb", [128, 128], F32))
            onescol = ctxA.enter_context(nc.sbuf_tensor("onescol_sb", [128, 1], F32))
            capthr = ctxA.enter_context(nc.sbuf_tensor("capthr_sb", [128, E], F32))
            trmap = ctxA.enter_context(nc.sbuf_tensor("trmap_sb", [128, E], F32))
            castf = [ctxA.enter_context(nc.sbuf_tensor(f"castf{i}", [128, 1024], F32)) for i in range(4)]
            castb = [ctxA.enter_context(nc.sbuf_tensor(f"castb{i}", [128, 1024], BF16)) for i in range(4)]
            ci = 0
            events = []
            agread = {}
            def cjob(dst_t, tag, src, r0, c0, dr, dc, cw):
                k2 = f"STG{tag}_{len(agread.setdefault(tag, []))}"
                agread[tag].append(k2)
                events.append(("cast", dst_t, k2, src, r0, c0, dr, dc, cw))
            if agmode >= 1:
                for ex in range(E):
                    for nm, src in [("wv", wv_sh), ("w", w_sh)]:
                        for c0 in range(0, F, 1024):
                            cjob(stgE[nm][ex], f"{nm}_{ex}", src,
                                 ex * 128, c0, 0, c0, 1024)
                    for j in range(4):
                        cjob(stgE["w1"][ex], f"w1_{ex}", w1_sh,
                             ex * (F // 8) + j * 128, 0, j * 128, 0, 1024)
                    if agmode >= 2:
                        for nm, dstE in [("wv", wv_allE), ("w", w_allE),
                                         ("w1", w1_allE)]:
                            events.append(("ag", f"ag{nm}{ex}", stgE[nm][ex],
                                           dstE[ex], f"AG{nm}{ex}", f"{nm}_{ex}"))
                for nm, src, rows in [("wt", wt_sh, C // 8), ("wi", wi_sh, D // 8),
                                      ("wo", wo_sh, C // 8)]:
                    cols = src.shape[1]
                    for r0 in range(0, rows, 128):
                        for c0 in range(0, cols, 1024):
                            cjob(stg[nm], nm, src, r0, c0, r0, c0,
                                 min(1024, cols - c0))
                if agmode >= 2:
                    for nm, dst in [("wt", wt_all), ("wi", wi_all), ("wo", wo_all)]:
                        events.append(("ag", f"ag_{nm}", stg[nm], dst,
                                       f"AG{nm}", nm))
            for ev in events:
                if ev[0] == "ag":
                    _, strm, src_t, dst_t, agkey, tag = ev
                    s.dma("gpsimd", strm,
                          lambda e, src_t=src_t, dst_t=dst_t: e.collective_compute(
                              "AllGather", ALU.bypass,
                              replica_groups=[list(range(N_CORES))],
                              ins=[src_t.ap().opt()],
                              outs=[dst_t.ap().opt()]),
                          reads=tuple(agread[tag]), writes=(agkey,), inc=1)
                    continue
                _, dst_t, skey, src, r0, c0, dr, dc, cw = ev
                b = ci % 4
                s.dma("gpsimd", f"cst{b}", lambda e, src=src, r0=r0, c0=c0, cw=cw, b=b: e.dma_start(
                    out=castf[b][:, 0:cw], in_=src[r0:r0 + 128, c0:c0 + cw]),
                    writes=(f"castf{b}",))
                s.op("vector" if ci % 2 == 0 else "scalar",
                     lambda e, cw=cw, b=b: _ecopy(e, castb[b][:, 0:cw], in_=castf[b][:, 0:cw]),
                     reads=(f"castf{b}",), writes=(f"castb{b}",))
                s.dma("gpsimd", f"cso{b}", lambda e, dst_t=dst_t, dr=dr, dc=dc, cw=cw, b=b: e.dma_start(
                    out=dst_t.ap()[dr:dr + 128, dc:dc + cw], in_=castb[b][:, 0:cw]),
                    reads=(f"castb{b}",), writes=(skey,))
                ci += 1
            s.op("vector", lambda e: e.memset(castf[0][:], 0.0), writes=("castf0",))
            for rt in range(GROWS // 128):
                s.dma("gpsimd", f"zf{rt % 4}", lambda e, rt=rt: e.dma_start(
                    out=xng_d.ap()[rt * 128:(rt + 1) * 128, :], in_=castf[0][:]),
                    reads=("castf0",), writes=("xng",), bag=True)

            with ExitStack() as ctx0:
                x_tok = [sb(f"x_tok{i}", [128, D], c=ctx0) for i in range(2)]
                xn_tok2 = [sb(f"xn_tok_{i}", [128, D], c=ctx0) for i in range(2)]
                sq_scr2 = [sb(f"sq_scr_{i}", [128, D], c=ctx0) for i in range(2)]
                xnf = [sb(f"xnf{d}", [128, 128], c=ctx0) for d in range(DT)]
                rw_sb = [sb(f"rw{d}", [128, E], c=ctx0) for d in range(DT)]
                m_t2 = [sb(f"m_t_{i}", [128, 1], c=ctx0) for i in range(2)]
                ss_t2 = [sb(f"ss_t_{i}", [128, 1], c=ctx0) for i in range(2)]
                msq_t2 = [sb(f"msq_t_{i}", [128, 1], c=ctx0) for i in range(2)]
                var_t2 = [sb(f"var_t_{i}", [128, 1], c=ctx0) for i in range(2)]
                std_t2 = [sb(f"std_t_{i}", [128, 1], c=ctx0) for i in range(2)]
                rstd_t2 = [sb(f"rstd_t_{i}", [128, 1], c=ctx0) for i in range(2)]
                lg_sb2 = [sb(f"lg_sb_{i}", [8, 128], c=ctx0) for i in range(2)]
                ltok2 = [sb(f"ltok_{i}", [128, E], c=ctx0) for i in range(2)]
                max82 = [sb(f"max8_{i}", [128, 8], c=ctx0) for i in range(2)]
                negmax2 = [sb(f"negmax_{i}", [128, 1], c=ctx0) for i in range(2)]
                ptok2 = [sb(f"ptok_{i}", [128, E], c=ctx0) for i in range(2)]
                sump2 = [sb(f"sump_{i}", [128, 1], c=ctx0) for i in range(2)]
                rsump2 = [sb(f"rsump_{i}", [128, 1], c=ctx0) for i in range(2)]
                e2v2 = [sb(f"e2v_{i}", [128, 1], c=ctx0) for i in range(2)]
                maskv2 = [sb(f"maskv_{i}", [128, E], c=ctx0) for i in range(2)]
                gmv2 = [sb(f"gmv_{i}", [128, E], c=ctx0) for i in range(2)]
                gts2 = [sb(f"gts_{i}", [128, E], c=ctx0) for i in range(2)]
                P_sb2 = [sb(f"P_sb_{i}", [128, E], c=ctx0) for i in range(2)]
                posm2 = [sb(f"posm_{i}", [128, E], c=ctx0) for i in range(2)]
                posi2 = [sb(f"posi_{i}", [128, E], mybir.dt.int32, c=ctx0) for i in range(2)]
                ovf2 = [sb(f"ovf_{i}", [128, E], c=ctx0) for i in range(2)]
                scr82 = [sb(f"scr8_{i}", [128, E], c=ctx0) for i in range(2)]
                baserow = sb("baserow", [1, E], c=ctx0)
                cnt_sb2 = [sb(f"cnt_sb_{i}", [E, 1], c=ctx0) for i in range(2)]
                cntrow2 = [sb(f"cntrow_{i}", [1, E], c=ctx0) for i in range(2)]
                p0f2 = [sb(f"p0f_{i}", [128, 1], c=ctx0) for i in range(2)]
                p1f2 = [sb(f"p1f_{i}", [128, 1], c=ctx0) for i in range(2)]
                s.dma("scalar", "c_par", lambda e: e.dma_start(out=baserow[:], in_=ecap_d),
                      writes=("baserow",), bag=True)

                for d in range(DT):
                    s.dma("sync", "c_rw",
                          lambda e, d=d: e.dma_start(out=rw_sb[d][:],
                                                     in_=rw_d[d * 128:(d + 1) * 128, :]),
                          writes=(f"rw{d}",), bag=True)

                for tt in range(S // 128):
                    tb = tt % 2
                    xb = x_tok[tt % 2]
                    xkey = f"xtok{tt % 2}"
                    cols = slice(tt * 128, (tt + 1) * 128)
                    s.dma("sync", f"xt{tt % 2}",
                          lambda e, tb=tb, tt=tt, xb=xb: e.dma_start(
                              out=xb[:], in_=x_d[tt * 128:(tt + 1) * 128, :]),
                          writes=(xkey,))
                    s.op("vector", lambda e, tb=tb, xb=xb: e.reduce_sum(
                        out=m_t2[tb][:], in_=xb[:], axis=AX.X), reads=(xkey,), writes=(f"m_t{tb}",))
                    s.op("vector", lambda e, tb=tb: e.tensor_scalar(
                        out=m_t2[tb][:], in0=m_t2[tb][:], scalar1=1.0 / D, scalar2=None,
                        op0=ALU.mult), reads=(f"m_t{tb}",), writes=(f"m_t{tb}",))
                    s.op("scalar", lambda e, tb=tb, xb=xb: e.activation(
                        out=sq_scr2[tb][:], in_=xb[:], func=AF.Square, accum_out=ss_t2[tb][:]),
                        reads=(xkey,), writes=(f"sq_scr{tb}", f"ss_t{tb}"))
                    s.op("vector", lambda e, tb=tb: e.tensor_tensor(
                        out=msq_t2[tb][:], in0=m_t2[tb][:], in1=m_t2[tb][:], op=ALU.mult),
                        reads=(f"m_t{tb}",), writes=(f"msq_t{tb}",))
                    s.op("vector", lambda e, tb=tb: e.scalar_tensor_tensor(
                        out=var_t2[tb][:], in0=ss_t2[tb][:], scalar=1.0 / D, in1=msq_t2[tb][:],
                        op0=ALU.mult, op1=ALU.subtract),
                        reads=(f"ss_t{tb}", f"msq_t{tb}"), writes=(f"var_t{tb}",))
                    s.op("vector", lambda e, tb=tb: e.tensor_scalar(
                        out=var_t2[tb][:], in0=var_t2[tb][:], scalar1=EPS, scalar2=None,
                        op0=ALU.add), reads=(f"var_t{tb}",), writes=(f"var_t{tb}",))
                    s.op("scalar", lambda e, tb=tb: e.activation(
                        out=std_t2[tb][:], in_=var_t2[tb][:], func=AF.Sqrt),
                        reads=(f"var_t{tb}",), writes=(f"std_t{tb}",))
                    s.op("vector", lambda e, tb=tb: e.reciprocal(
                        out=rstd_t2[tb][:], in_=std_t2[tb][:]), reads=(f"std_t{tb}",), writes=(f"rstd_t{tb}",))
                    s.op("vector", lambda e, tb=tb, xb=xb: e.tensor_scalar(
                        out=xn_tok2[tb][:], in0=xb[:], scalar1=m_t2[tb][:], scalar2=rstd_t2[tb][:],
                        op0=ALU.subtract, op1=ALU.mult),
                        reads=(xkey, f"m_t{tb}", f"rstd_t{tb}"), writes=(f"xn_tok{tb}",))
                    for half in range(2):
                        pk_n = f"ps{half}"
                        for j in range(4):
                            d = half * 4 + j
                            s.op("tensor", lambda e, tb=tb, d=d, j=j, half=half: e.transpose(
                                out=ps[half][:, j * 128:(j + 1) * 128],
                                in_=xn_tok2[tb][:, d * 128:(d + 1) * 128], identity=ident[:]),
                                reads=(f"xn_tok{tb}", "ident"), writes=(pk_n,))
                        for j in range(4):
                            d = half * 4 + j
                            s.op("scalar", lambda e, tb=tb, d=d, j=j, half=half, cols=cols: e.copy(
                                out=xn_bf[d][:, cols], in_=ps[half][:, j * 128:(j + 1) * 128]),
                                reads=(pk_n,), writes=(f"xnbf{d}",))
                            s.op("vector", lambda e, tb=tb, d=d, j=j, half=half: e.tensor_copy(
                                out=xnf[d][:], in_=ps[half][:, j * 128:(j + 1) * 128]),
                                reads=(pk_n,), writes=(f"xnf{d}",))
                    if not p0router:
                        continue
                    if True:
                        for d in range(DT):
                            s.op("tensor", lambda e, tb=tb, d=d: e.matmul(
                                out=ps[4][0:8, 0:128], lhsT=rw_sb[d][:], rhs=xnf[d][:],
                                start=(d == 0), stop=(d == DT - 1)),
                                reads=(f"xnf{d}", f"rw{d}"), writes=("ps4",))
                        s.op("scalar", lambda e, tb=tb: e.copy(out=lg_sb2[tb][:], in_=ps[4][0:8, 0:128]),
                             reads=("ps4",), writes=(f"lg_sb{tb}",))
                        s.op("tensor", lambda e, tb=tb: e.transpose(
                            out=ps[5][0:128, 0:8], in_=lg_sb2[tb][:], identity=ident[0:8, 0:8]),
                            reads=(f"lg_sb{tb}", "ident"), writes=("ps5",))
                        s.op("vector", lambda e, tb=tb: e.tensor_copy(out=ltok2[tb][:], in_=ps[5][0:128, 0:8]),
                             reads=("ps5",), writes=(f"ltok{tb}",))
                        s.op("vector", lambda e, tb=tb: e.max(out=max82[tb][:], in_=ltok2[tb][:]),
                             reads=(f"ltok{tb}",), writes=(f"max8{tb}",))
                        s.op("vector", lambda e, tb=tb: e.tensor_scalar(
                            out=negmax2[tb][:], in0=max82[tb][:, 0:1], scalar1=-1.0, scalar2=None,
                            op0=ALU.mult), reads=(f"max8{tb}",), writes=(f"negmax{tb}",))
                        s.op("scalar", lambda e, tb=tb: e.activation(
                            out=ptok2[tb][:], in_=ltok2[tb][:], func=AF.Exp, bias=negmax2[tb][:]),
                            reads=(f"ltok{tb}", f"negmax{tb}"), writes=(f"ptok{tb}",))
                        s.op("vector", lambda e, tb=tb: e.reduce_sum(
                            out=sump2[tb][:], in_=ptok2[tb][:], axis=AX.X), reads=(f"ptok{tb}",), writes=(f"sump{tb}",))
                        s.op("vector", lambda e, tb=tb: e.reciprocal(out=rsump2[tb][:], in_=sump2[tb][:]),
                             reads=(f"sump{tb}",), writes=(f"rsump{tb}",))
                        s.op("scalar", lambda e, tb=tb: e.activation(
                            out=e2v2[tb][:], in_=max82[tb][:, 1:2], func=AF.Exp, bias=negmax2[tb][:]),
                            reads=(f"max8{tb}", f"negmax{tb}"), writes=(f"e2v{tb}",))
                        s.op("vector", lambda e, tb=tb: e.tensor_scalar(
                            out=maskv2[tb][:], in0=ptok2[tb][:], scalar1=e2v2[tb][:], scalar2=None,
                            op0=ALU.is_ge), reads=(f"ptok{tb}", f"e2v{tb}"), writes=(f"maskv{tb}",))
                        s.op("vector", lambda e, tb=tb: e.tensor_tensor(
                            out=gmv2[tb][:], in0=ptok2[tb][:], in1=maskv2[tb][:], op=ALU.mult),
                            reads=(f"ptok{tb}", f"maskv{tb}"), writes=(f"gmv{tb}",))
                        s.op("vector", lambda e, tb=tb: e.tensor_scalar(
                            out=gts2[tb][:], in0=gmv2[tb][:], scalar1=rsump2[tb][:], scalar2=None,
                            op0=ALU.mult), reads=(f"gmv{tb}", f"rsump{tb}"), writes=(f"gts{tb}",))
                        s.op("tensor", lambda e, tb=tb: e.matmul(
                            out=ps[6][0:128, 0:E], lhsT=tril[:], rhs=maskv2[tb][:],
                            start=True, stop=True),
                            reads=(f"maskv{tb}", "tril"), writes=("ps6",))
                        s.op("tensor", lambda e, tb=tb: e.matmul(
                            out=ps[2][0:E, 0:1], lhsT=maskv2[tb][:], rhs=onescol[:],
                            start=True, stop=True),
                            reads=(f"maskv{tb}", "onescol"), writes=("ps2",))
                        s.op("vector", lambda e, tb=tb: e.tensor_copy(
                            out=P_sb2[tb][:], in_=ps[6][0:128, 0:E]),
                            reads=("ps6",), writes=(f"P_sb{tb}",))
                        s.op("vector", lambda e, tb=tb: e.tensor_copy(
                            out=cnt_sb2[tb][:], in_=ps[2][0:E, 0:1]),
                            reads=("ps2",), writes=(f"cnt_sb{tb}",))
                        s.op("tensor", lambda e, tb=tb: e.matmul(
                            out=ps[3][0:128, 0:E], lhsT=ones1[:], rhs=baserow[:],
                            start=True, stop=True),
                            reads=("baserow", "ones1"), writes=("ps3",))
                        s.op("vector", lambda e, tb=tb: e.tensor_tensor(
                            out=posm2[tb][:], in0=P_sb2[tb][:], in1=ps[3][0:128, 0:E], op=ALU.add),
                            reads=(f"P_sb{tb}",), writes=(f"posm{tb}", "ps3"))
                        s.op("tensor", lambda e, tb=tb: e.transpose(
                            out=ps[2][0:1, 0:E], in_=cnt_sb2[tb][:], identity=ident[0:E, 0:E]),
                            reads=(f"cnt_sb{tb}", "ident"), writes=("ps2",))
                        s.op("vector", lambda e, tb=tb: e.tensor_copy(
                            out=cntrow2[tb][:], in_=ps[2][0:1, 0:E]),
                            reads=("ps2",), writes=(f"cntrow{tb}",))
                        s.op("vector", lambda e, tb=tb: e.tensor_tensor(
                            out=baserow[:], in0=baserow[:], in1=cntrow2[tb][:], op=ALU.add),
                            reads=(f"cntrow{tb}", "baserow"), writes=("baserow",))
                        s.op("vector", lambda e, tb=tb: e.tensor_tensor(
                            out=ovf2[tb][:], in0=posm2[tb][:], in1=capthr[:], op=ALU.is_ge),
                            reads=(f"posm{tb}", "capthr"), writes=(f"ovf{tb}",))
                        s.op("vector", lambda e, tb=tb: e.tensor_scalar(
                            out=scr82[tb][:], in0=maskv2[tb][:], scalar1=-1.0, scalar2=1.0,
                            op0=ALU.mult, op1=ALU.add),
                            reads=(f"maskv{tb}",), writes=(f"scr8{tb}",))
                        s.op("vector", lambda e, tb=tb: e.tensor_tensor(
                            out=scr82[tb][:], in0=scr82[tb][:], in1=ovf2[tb][:], op=ALU.add),
                            reads=(f"scr8{tb}", f"ovf{tb}"), writes=(f"scr8{tb}",))
                        s.op("vector", lambda e, tb=tb: e.scalar_tensor_tensor(
                            out=posm2[tb][:], in0=scr82[tb][:], scalar=HUGE, in1=posm2[tb][:],
                            op0=ALU.mult, op1=ALU.add),
                            reads=(f"scr8{tb}", f"posm{tb}"), writes=(f"posm{tb}",))
                        s.op("vector", lambda e, tb=tb: e.tensor_tensor(
                            out=scr82[tb][:], in0=posm2[tb][:], in1=trmap[:], op=ALU.min),
                            reads=(f"posm{tb}", "trmap"), writes=(f"scr8{tb}",))
                        s.op("vector", lambda e, tb=tb: e.tensor_copy(out=posi2[tb][:], in_=scr82[tb][:]),
                             reads=(f"scr8{tb}",), writes=(f"posi{tb}",))
                        for ei in range(E):
                            s.dma("gpsimd", f"sct{ei}",
                                  lambda e, tb=tb, ei=ei: e.indirect_dma_start(
                                      out=xng_d.ap(),
                                      out_offset=bass.IndirectOffsetOnAxis(
                                          ap=posi2[tb][:, ei:ei + 1], axis=0),
                                      in_=xn_tok2[tb][:], in_offset=None),
                                  reads=(f"posi{tb}", f"xn_tok{tb}"), writes=(f"xng_e{ei}",))
                        s.op("vector", lambda e, tb=tb: e.tensor_reduce(
                            out=p0f2[tb][:], in_=posm2[tb][:], axis=AX.X, op=ALU.min),
                            reads=(f"posm{tb}",), writes=(f"p0f{tb}",))
                        s.op("vector", lambda e, tb=tb: e.tensor_scalar(
                            out=scr82[tb][:], in0=posm2[tb][:], scalar1=p0f2[tb][:], scalar2=None,
                            op0=ALU.is_equal), reads=(f"posm{tb}", f"p0f{tb}"), writes=(f"scr8{tb}",))
                        s.op("vector", lambda e, tb=tb, tt=tt: e.tensor_copy(
                            out=pos0i[:, tt:tt + 1], in_=p0f2[tb][:]),
                            reads=(f"p0f{tb}",), writes=("pos0i",))
                        s.op("vector", lambda e, tb=tb: e.tensor_tensor(
                            out=ovf2[tb][:], in0=gts2[tb][:], in1=scr82[tb][:], op=ALU.mult),
                            reads=(f"gts{tb}", f"scr8{tb}"), writes=(f"ovf{tb}",))
                        s.op("vector", lambda e, tb=tb, tt=tt: e.reduce_sum(
                            out=g0_sb[:, tt:tt + 1], in_=ovf2[tb][:], axis=AX.X),
                            reads=(f"ovf{tb}",), writes=("g0_sb",))
                        s.op("vector", lambda e, tb=tb: e.scalar_tensor_tensor(
                            out=posm2[tb][:], in0=scr82[tb][:], scalar=HUGE, in1=posm2[tb][:],
                            op0=ALU.mult, op1=ALU.add),
                            reads=(f"scr8{tb}", f"posm{tb}"), writes=(f"posm{tb}",))
                        s.op("vector", lambda e, tb=tb: e.tensor_reduce(
                            out=p1f2[tb][:], in_=posm2[tb][:], axis=AX.X, op=ALU.min),
                            reads=(f"posm{tb}",), writes=(f"p1f{tb}",))
                        s.op("vector", lambda e, tb=tb: e.tensor_scalar(
                            out=scr82[tb][:], in0=posm2[tb][:], scalar1=p1f2[tb][:], scalar2=None,
                            op0=ALU.is_equal), reads=(f"posm{tb}", f"p1f{tb}"), writes=(f"scr8{tb}",))
                        s.op("vector", lambda e, tb=tb, tt=tt: e.tensor_copy(
                            out=pos1i[:, tt:tt + 1], in_=p1f2[tb][:]),
                            reads=(f"p1f{tb}",), writes=("pos1i",))
                        s.op("vector", lambda e, tb=tb: e.tensor_tensor(
                            out=ovf2[tb][:], in0=gts2[tb][:], in1=scr82[tb][:], op=ALU.mult),
                            reads=(f"gts{tb}", f"scr8{tb}"), writes=(f"ovf{tb}",))
                        s.op("vector", lambda e, tb=tb, tt=tt: e.reduce_sum(
                            out=g1_sb[:, tt:tt + 1], in_=ovf2[tb][:], axis=AX.X),
                            reads=(f"ovf{tb}",), writes=("g1_sb",))
                s.join()

            if 1 in phases:
                ca = lambda n, shp, dt=F32: ctxA.enter_context(nc.sbuf_tensor(n, shp, dt))
                Zp = [ca(f"zp{i}", [128, CAP], BF16) for i in range(4)]
                wv_bf = [ca(f"wvbf{i}", [128, 2048], BF16) for i in range(2)]
                w_bf = [ca(f"wbf{i}", [128, 2048], BF16) for i in range(2)]
                w1_bf = [ca(f"w1bf{i}", [128, 2048], BF16) for i in range(2)]
                xn_g = [[ca(f"xng{b}_{d}", [128, CAP], BF16) for d in range(DT)]
                        for b in range(2)]
                tokrow = [ca(f"tokrow{i}", [128, D]) for i in range(2)]
                po = [ca(f"po{d}", [128, CAP]) for d in range(DT)]
                ptk = [ca(f"ptok{i}", [128, D]) for i in range(2)]
                gb0 = [ca(f"gb0_{i}", [128, D]) for i in range(2)]
                gb1 = [ca(f"gb1_{i}", [128, D]) for i in range(2)]
                mo = [ca(f"mo{i}", [128, D]) for i in range(2)]
                sv_sb = [ca(f"svsb{i}", [128, 32]) for i in range(2)]
                ss_sb = [ca(f"sssb{i}", [128, 32]) for i in range(2)]
                s1_sb = [ca(f"s1sb{i}", [128, DT]) for i in range(2)]
                gy = [ca(f"gy{i}", [128, 512]) for i in range(3)]

            cast_i = [0]

            def cast_eng():
                cast_i[0] += 1
                return ("vector", "scalar")[cast_i[0] % 2]

            for ei in (range(E) if 1 in phases else []):
                ep = ei % 2
                s.dma("scalar", f"svp{ep}", lambda e, ei=ei, ep=ep: e.dma_start(
                    out=sv_sb[ep][:], in_=svt_d[ei]), reads=(), writes=(f"svsb{ep}",))
                s.dma("scalar", f"svp{ep}", lambda e, ei=ei, ep=ep: e.dma_start(
                    out=ss_sb[ep][:], in_=st_d[ei]), reads=(), writes=(f"sssb{ep}",))
                s.dma("scalar", f"svp{ep}", lambda e, ei=ei, ep=ep: e.dma_start(
                    out=s1_sb[ep][:], in_=s1t_d[ei]), reads=(), writes=(f"s1sb{ep}",))
                xgkeys = tuple(f"xng_e{q}" for q in range(E))
                for rt in range(CAP // 128):
                    rb = rt % 2
                    s.dma("sync", f"xg{rb}", lambda e, ei=ei, rt=rt, rb=rb: e.dma_start(
                        out=tokrow[rb][:],
                        in_=xng_d.ap()[ei * CAP + rt * 128:ei * CAP + (rt + 1) * 128, :]),
                        reads=xgkeys, writes=(f"tokrow{rb}",))
                    for d in range(DT):
                        pbk = 6 + (d // 4)
                        s.op("tensor", lambda e, d=d, rb=rb, pbk=pbk: e.transpose(
                            out=ps[pbk][:, (d % 4) * 128:(d % 4 + 1) * 128],
                            in_=tokrow[rb][:, d * 128:(d + 1) * 128], identity=ident[:]),
                            reads=(f"tokrow{rb}", "ident"), writes=(f"ps{pbk}",))
                        if d % 4 == 3:
                            for dd in range(d - 3, d + 1):
                                s.op("scalar", lambda e, dd=dd, rt=rt, ei=ei, pbk=pbk: e.copy(
                                    out=xn_g[ei % 2][dd][:, rt * 128:(rt + 1) * 128],
                                    in_=ps[pbk][:, (dd % 4) * 128:(dd % 4 + 1) * 128]),
                                    reads=(f"ps{pbk}",), writes=(f"xg{ei % 2}_{dd}",))

                for fg2 in range(16):
                    gb = fg2 % 2
                    cq = fg2 * 256
                    s.dma("sync", f"qwv{gb}", lambda e, ei=ei, gb=gb, cq=cq: e.dma_start(
                        out=wv_bf[gb][:].rearrange("p (dt f) -> p dt f", dt=8),
                        in_=wv_dE[ei][:, cq:cq + 256].rearrange(
                            "(dt p) f -> p dt f", p=128)),
                        reads=(f"AGwv{ei}",), writes=(f"wvbf{gb}",))
                    s.dma("sync", f"qw{gb}", lambda e, ei=ei, gb=gb, cq=cq: e.dma_start(
                        out=w_bf[gb][:].rearrange("p (dt f) -> p dt f", dt=8),
                        in_=w_dE[ei][:, cq:cq + 256].rearrange(
                            "(dt p) f -> p dt f", p=128)),
                        reads=(f"AGw{ei}",), writes=(f"wbf{gb}",))
                    s.dma("sync", f"qw1{gb}", lambda e, ei=ei, gb=gb, cq=cq: e.dma_start(
                        out=w1_bf[gb][:].rearrange("p (ft d) -> p ft d", ft=2),
                        in_=w1_dE[ei][cq:cq + 256, :].rearrange(
                            "(ft p) d -> p ft d", p=128)),
                        reads=(f"AGw1{ei}",), writes=(f"w1bf{gb}",))

                    for fb2 in range(2):
                        ftg = fg2 * 2 + fb2
                        ftl = (fg2 % 2) * 2 + fb2
                        for rci, (r0, rw) in enumerate(RCS):
                            u = (ftg * 2 + rci) % 3
                            vk, yk = f"ps{u}", f"ps{3 + u}"
                            rcs = slice(r0, r0 + rw)
                            for d in range(DT):
                                s.op("tensor", lambda e, d=d, fb2=fb2, rcs=rcs, rw=rw, u=u, gb=gb, ei=ei: e.matmul(
                                    out=ps[u][:, 0:rw],
                                    lhsT=wv_bf[gb][:, d * 256 + fb2 * 128:d * 256 + (fb2 + 1) * 128],
                                    rhs=xn_g[ei % 2][d][:, rcs],
                                    start=(d == 0), stop=(d == DT - 1)),
                                    reads=(f"wvbf{gb}", f"xg{ei % 2}_{d}"), writes=(vk,))
                            for d in range(DT):
                                s.op("tensor", lambda e, d=d, fb2=fb2, rcs=rcs, rw=rw, u=u, gb=gb, ei=ei: e.matmul(
                                    out=ps[3 + u][:, 0:rw],
                                    lhsT=w_bf[gb][:, d * 256 + fb2 * 128:d * 256 + (fb2 + 1) * 128],
                                    rhs=xn_g[ei % 2][d][:, rcs],
                                    start=(d == 0), stop=(d == DT - 1)),
                                    reads=(f"wbf{gb}", f"xg{ei % 2}_{d}"), writes=(yk,))
                            s.op("scalar", lambda e, u=u, rw=rw, ftg=ftg, ep=ep: e.activation(
                                out=gy[u][:, 0:rw], in_=ps[3 + u][:, 0:rw],
                                func=AF.Gelu_apprx_tanh, scale=ss_sb[ep][:, ftg:ftg + 1]),
                                reads=(yk, f"sssb{ep}"), writes=(f"gy{u}",))
                            s.op("vector", lambda e, u=u, ftl=ftl, ftg=ftg, rcs=rcs, rw=rw, ep=ep: e.scalar_tensor_tensor(
                                out=Zp[ftl][:, rcs], in0=ps[u][:, 0:rw],
                                scalar=sv_sb[ep][:, ftg:ftg + 1], in1=gy[u][:, 0:rw],
                                op0=ALU.mult, op1=ALU.mult),
                                reads=(vk, f"gy{u}", f"svsb{ep}"), writes=(f"zp{ftl}",))
                    if fg2 % 2 == 0:
                        continue
                    first_window = (fg2 == 1)
                    for d in range(DT):
                        for rci, (r0, rw) in enumerate(RCS):
                            o = (d * 2 + rci) % 2
                            ok = f"ps{6 + o}"
                            rcs = slice(r0, r0 + rw)
                            for ftl in range(4):
                                w1b = ftl // 2
                                s.op("tensor", lambda e, d=d, ftl=ftl, rcs=rcs, rw=rw, o=o, w1b=w1b: e.matmul(
                                    out=ps[6 + o][:, 0:rw],
                                    lhsT=w1_bf[w1b][:, (ftl % 2) * 1024 + d * 128:(ftl % 2) * 1024 + (d + 1) * 128],
                                    rhs=Zp[ftl][:, rcs],
                                    start=(ftl == 0), stop=(ftl == 3)),
                                    reads=(f"w1bf{w1b}", f"zp{ftl}"), writes=(ok,))
                            if first_window:
                                s.op("vector", lambda e, d=d, rcs=rcs, rw=rw, o=o, ep=ep: e.tensor_scalar(
                                    out=po[d][:, rcs], in0=ps[6 + o][:, 0:rw],
                                    scalar1=s1_sb[ep][:, d:d + 1], scalar2=None,
                                    op0=ALU.mult),
                                    reads=(ok, f"s1sb{ep}"), writes=(f"po{d}",))
                            else:
                                s.op("vector", lambda e, d=d, rcs=rcs, rw=rw, o=o, ep=ep: e.scalar_tensor_tensor(
                                    out=po[d][:, rcs], in0=ps[6 + o][:, 0:rw],
                                    scalar=s1_sb[ep][:, d:d + 1], in1=po[d][:, rcs],
                                    op0=ALU.mult, op1=ALU.add),
                                    reads=(ok, f"s1sb{ep}", f"po{d}"), writes=(f"po{d}",))

                for rt in range(CAP // 128):
                    rb = rt % 2
                    for d in range(DT):
                        pbk = 7 - (d // 4)
                        s.op("tensor", lambda e, d=d, rt=rt, pbk=pbk: e.transpose(
                            out=ps[pbk][:, (d % 4) * 128:(d % 4 + 1) * 128],
                            in_=po[d][:, rt * 128:(rt + 1) * 128], identity=ident[:]),
                            reads=(f"po{d}", "ident"), writes=(f"ps{pbk}",))
                        if d % 4 == 3:
                            for dd in range(d - 3, d + 1):
                                s.op("vector", lambda e, dd=dd, rb=rb, pbk=pbk: e.tensor_copy(
                                    out=ptk[rb][:, dd * 128:(dd + 1) * 128],
                                    in_=ps[pbk][:, (dd % 4) * 128:(dd % 4 + 1) * 128]),
                                    reads=(f"ps{pbk}",), writes=(f"ptok{rb}",))
                    s.dma("sync", f"pb{rb}", lambda e, ei=ei, rt=rt, rb=rb: e.dma_start(
                        out=pairbuf_d.ap()[ei * CAP + rt * 128:ei * CAP + (rt + 1) * 128, :],
                        in_=ptk[rb][:]),
                        reads=(f"ptok{rb}",), writes=("pairbuf",))

            for tt in (range(S // 128) if 1 in phases else []):
                mb_ = tt % 2
                s.dma("gpsimd", f"gth{mb_}", lambda e, tt=tt, mb_=mb_: e.indirect_dma_start(
                    out=gb0[mb_][:], out_offset=None,
                    in_=pairbuf_d.ap(),
                    in_offset=bass.IndirectOffsetOnAxis(ap=pos0i[:, tt:tt + 1], axis=0)),
                    reads=("pairbuf", "pos0i"), writes=(f"gb0_{mb_}",))
                s.dma("gpsimd", f"gth{mb_}", lambda e, tt=tt, mb_=mb_: e.indirect_dma_start(
                    out=gb1[mb_][:], out_offset=None,
                    in_=pairbuf_d.ap(),
                    in_offset=bass.IndirectOffsetOnAxis(ap=pos1i[:, tt:tt + 1], axis=0)),
                    reads=("pairbuf", "pos1i"), writes=(f"gb1_{mb_}",))
                s.op("vector", lambda e, tt=tt, mb_=mb_: e.tensor_scalar(
                    out=mo[mb_][:], in0=gb0[mb_][:], scalar1=g0_sb[:, tt:tt + 1],
                    scalar2=None, op0=ALU.mult),
                    reads=(f"gb0_{mb_}", "g0_sb"), writes=(f"mo{mb_}",))
                s.op("vector", lambda e, tt=tt, mb_=mb_: e.scalar_tensor_tensor(
                    out=mo[mb_][:], in0=gb1[mb_][:], scalar=g1_sb[:, tt:tt + 1],
                    in1=mo[mb_][:], op0=ALU.mult, op1=ALU.add),
                    reads=(f"gb1_{mb_}", "g1_sb", f"mo{mb_}"), writes=(f"mo{mb_}",))
                s.dma("gpsimd", "outadd", lambda e, tt=tt, mb_=mb_: e.dma_start(
                    out=out_d[tt * 128:(tt + 1) * 128, :], in_=mo[mb_][:],
                    accum_op=ALU.add),
                    reads=(f"mo{mb_}", f"outrow{tt}"), writes=(f"outrow{tt}",), bag=True)
            s.join()

        with ExitStack() as ctx2:
            if 2 in phases:
                inner = [[sb(f"inner{b}_{c}", [128, T + 2], BF16, c=ctx2) for c in range(CT)]
                         for b in range(2)]
                sg = [sb(f"sg_{c}", [128, T], BF16, c=ctx2) for c in range(CT)]
                h_bf = [sb(f"h{c}", [128, T], BF16, c=ctx2) for c in range(CT)]
                Bb = [sb(f"Bb{k}", [128, T], BF16, c=ctx2) for k in range(K)]
                brow = [sb(f"brow{i}", [1, T], c=ctx2) for i in range(1)]
                wi_bf = [sb(f"wibf{i}", [128, 4096], BF16, c=ctx2) for i in range(1)]
                wt_bf = [sb(f"wtbf{i}", [128, 4096], BF16, c=ctx2) for i in range(2)]
                wo_bf = [sb(f"wobf{i}", [128, 4096], BF16, c=ctx2) for i in range(1)]
                att = [sb(f"att{d}", [128, T], c=ctx2) for d in range(DT)]
                ones33 = sb("ones33", [33, 128], c=ctx2)
                mem_all = sb("mem_all", [128, K * T], BF16, c=ctx2)
                dA = [sb(f"dA{i}", [128, T], BF16, c=ctx2) for i in range(2)]
                dB = [sb(f"dB{i}", [128, T], BF16, c=ctx2) for i in range(2)]
                dtb = [sb(f"dtb{i}", [128, T], c=ctx2) for i in range(2)]
                g_sc2 = [sb(f"g_sc{i}", [128, T], BF16, c=ctx2) for i in range(2)]
                y_sc2 = [sb(f"y_sc{i}", [128, T], c=ctx2) for i in range(2)]
                c1 = sb("c1", [128, T], c=ctx2)
                c2 = sb("c2", [128, T], c=ctx2)
                carr = [sb(f"carr{c}", [128, K], c=ctx2) for c in range(CT)]
                cw_sb = [sb(f"cw{c}", [128, 3], c=ctx2) for c in range(CT)]
                parm = [sb(f"parm{c}", [128, 4], c=ctx2) for c in range(CT)]
                cbp = [parm[c][:, 0:1] for c in range(CT)]
                btp = [parm[c][:, 1:2] for c in range(CT)]
                skp = [parm[c][:, 2:3] for c in range(CT)]
                nA = [sb(f"nA{c}", [128, K], c=ctx2) for c in range(CT)]
                wst_bf = [sb(f"wst{c}", [128, 16], BF16, c=ctx2) for c in range(CT)]
                wst_f = sb("wst_f", [128, 32], c=ctx2)

            for c in (range(CT) if 2 in phases else []):
                s.dma("scalar", "c_par", lambda e, c=c: e.dma_start(
                    out=cw_sb[c][:], in_=convw_d[c]), writes=(f"cw{c}",), bag=True)
                s.dma("scalar", "c_par", lambda e, c=c: e.dma_start(
                    out=cbp[c], in_=convb_d[c]), writes=(f"parm{c}",), bag=True)
                s.dma("scalar", "c_par", lambda e, c=c: e.dma_start(
                    out=btp[c], in_=btime_d[c]), writes=(f"parm{c}",), bag=True)
                s.dma("scalar", "c_par", lambda e, c=c: e.dma_start(
                    out=skp[c], in_=skip_d[c]), writes=(f"parm{c}",), bag=True)
                s.dma("scalar", "asbq", lambda e, c=c: e.dma_start(
                    out=wst_f[:, 0:K], in_=alog_d[c]), reads=(), writes=("wst_f",))
                s.op("scalar", lambda e, c=c: e.activation(
                    out=nA[c][:], in_=wst_f[:, 0:K], func=AF.Exp),
                    reads=("wst_f",), writes=(f"nA{c}",))
                s.op("vector", lambda e, c=c: e.tensor_scalar(
                    out=nA[c][:], in0=nA[c][:], scalar1=-1.0, scalar2=None,
                    op0=ALU.mult), reads=(f"nA{c}",), writes=(f"nA{c}",))
                s.dma("scalar", "wstq", lambda e, c=c: e.dma_start(
                    out=wst_f[:], in_=wst_d[c * 128:(c + 1) * 128, :]),
                    reads=(), writes=("wst_f",))
                s.op("vector", lambda e, c=c: e.tensor_copy(
                    out=wst_bf[c][:], in_=wst_f[:, 16:32]), reads=("wst_f",), writes=(f"wst{c}",))
                s.op("gpsimd", lambda e, c=c: e.memset(carr[c][:], 0.0),
                     writes=(f"carr{c}",))
                s.op("gpsimd", lambda e, c=c: e.memset(inner[0][c][:, 0:1], 0.0),
                     writes=(f"in0_{c}",))
            if 2 in phases:
                s.op("vector", lambda e: e.memset(ones33[:], 1.0),
                     writes=("ones33",))

            def emit_zx_part(k, gate, cgs=None):
                kb = k % 2
                tcs = slice(k * T, (k + 1) * T)
                for cg in (cgs if cgs is not None
                           else (range(4, 8) if gate else range(4))):
                    for hf in range(2):
                        s.dma("sync", f"qwi{hf}", lambda e, cg=cg, hf=hf: e.dma_start(
                            out=wi_bf[0][:, hf * 2048:(hf + 1) * 2048].rearrange(
                                "p (dt f) -> p dt f", dt=4),
                            in_=win_d[hf * 512:(hf + 1) * 512,
                                      cg * 512:(cg + 1) * 512].rearrange(
                                "(dt p) f -> p dt f", p=128)),
                            reads=("AGwi",), writes=(f"wibf{hf}",))
                    for pair in range(2):
                        for hf in range(2):
                            for cb2 in (pair * 2, pair * 2 + 1):
                                cbg = cg * 4 + cb2
                                pk = f"ps{cbg % 2}"
                                for d in range(hf * 4, hf * 4 + 4):
                                    s.op("tensor", lambda e, d=d, cb2=cb2, tcs=tcs, cbg=cbg: e.matmul(
                                        out=ps[cbg % 2][:, 0:T],
                                        lhsT=wi_bf[0][:, d * 512 + cb2 * 128:d * 512 + (cb2 + 1) * 128],
                                        rhs=xn_bf[d][:, tcs],
                                        start=(d == 0), stop=(d == DT - 1)),
                                        reads=(f"wibf{hf}", f"xnbf{d}"), writes=(pk,))
                                if hf == 0:
                                    continue
                                if cbg < 16:
                                    s.op("scalar", lambda e, cbg=cbg, kb=kb: e.copy(
                                        out=inner[kb][cbg][:, 1:T + 1], in_=ps[cbg % 2][:, 0:T]),
                                        reads=(pk,), writes=(f"in{kb}_{cbg}",))
                                else:
                                    gbi = cbg - 16
                                    s.op("scalar", lambda e, gbi=gbi, cbg=cbg: e.activation(
                                        out=sg[gbi][:], in_=ps[cbg % 2][:, 0:T], func=AF.Silu),
                                        reads=(pk,), writes=(f"sg_{gbi}",))

            def emit_prep(kn):
                pb = kn % 2
                qb = (kn - 1) % 2
                for c in range(CT):
                    s.op("gpsimd", lambda e, c=c, pb=pb, qb=qb: e.tensor_copy(
                        out=inner[pb][c][:, 0:1], in_=inner[qb][c][:, T:T + 1]),
                        reads=(f"in{qb}_{c}",), writes=(f"in{pb}_{c}",))
                emit_zx_part(kn, False)
                for c in range(CT):
                    s.op("gpsimd", lambda e, c=c, pb=pb, qb=qb: e.tensor_copy(
                        out=inner[qb][c][:, T + 1:T + 2], in_=inner[pb][c][:, 1:2]),
                        reads=(f"in{pb}_{c}",), writes=(f"in{qb}_{c}",))

            def emit_store(ks):
                for tt2 in range(T // 128):
                    tt = ks * (T // 128) + tt2
                    ob = out_tok[0]
                    okey = "otok0"
                    for half in range(2):
                        pk = f"ps{4 + half}"
                        for j in range(4):
                            d = half * 4 + j
                            s.op("tensor", lambda e, d=d, j=j, half=half, tt2=tt2: e.transpose(
                                out=ps[4 + half][:, j * 128:(j + 1) * 128],
                                in_=att[d][:, tt2 * 128:(tt2 + 1) * 128], identity=ident[:]),
                                reads=(f"att{d}", "ident"), writes=(pk,))
                        s.op("scalar", lambda e, half=half, ob=ob: e.copy(
                            out=ob[:, half * 512:(half + 1) * 512], in_=ps[4 + half][:, 0:512]),
                            reads=(pk,), writes=(okey,))
                    s.dma("gpsimd", "outadd", lambda e, tt=tt, ob=ob: e.dma_start(
                        out=out_d[tt * 128:(tt + 1) * 128, :], in_=ob[:],
                        accum_op=ALU.add),
                        reads=(okey, f"outrow{tt}"), writes=(f"outrow{tt}",), bag=True)

            def emit_wout(ks):
                for dg in range(4):
                    for hf in range(2):
                        s.dma("sync", f"qwo{hf}", lambda e, dg=dg, hf=hf: e.dma_start(
                            out=wo_bf[0][:, hf * 2048:(hf + 1) * 2048].rearrange(
                                "p (ct f) -> p ct f", ct=8),
                            in_=wout_d[hf * 1024:(hf + 1) * 1024,
                                       dg * 256:(dg + 1) * 256].rearrange(
                                "(ct p) f -> p ct f", p=128)),
                            reads=("AGwo",), writes=(f"wobf{hf}",))
                    for hf in range(2):
                        for db2 in range(2):
                            d = dg * 2 + db2
                            pk = f"ps{6 + d % 2}"
                            for c in range(hf * 8, hf * 8 + 8):
                                s.op("tensor", lambda e, c=c, db2=db2, d=d: e.matmul(
                                    out=ps[6 + d % 2][:, 0:T],
                                    lhsT=wo_bf[0][:, c * 256 + db2 * 128:c * 256 + (db2 + 1) * 128],
                                    rhs=sg[c][:],
                                    start=(c == 0), stop=(c == CT - 1)),
                                    reads=(f"wobf{hf}", f"sg_{c}"), writes=(pk,))
                            if hf == 1:
                                s.op("scalar", lambda e, d=d: e.copy(
                                    out=att[d][:], in_=ps[6 + d % 2][:, 0:T]),
                                    reads=(pk,), writes=(f"att{d}",))

            if 2 in phases:
                emit_zx_part(0, False)
                emit_prep(1)
            for k in (range(NCH) if 2 in phases else []):
                kb = k % 2
                nkb = (k + 1) % 2
                s.mark(f"conv{k}")

                for c in range(CT):
                    ik = f"in{kb}_{c}"
                    s.op("vector", lambda e, c=c, kb=kb: e.tensor_scalar(
                        out=c1[:], in0=inner[kb][c][:, 0:T], scalar1=cw_sb[c][:, 0:1],
                        scalar2=None, op0=ALU.mult),
                        reads=(ik, f"cw{c}"), writes=("c1",))
                    s.op("vector", lambda e, c=c, kb=kb: e.scalar_tensor_tensor(
                        out=c2[:], in0=inner[kb][c][:, 1:T + 1], scalar=cw_sb[c][:, 1:2],
                        in1=c1[:], op0=ALU.mult, op1=ALU.add),
                        reads=(ik, f"cw{c}", "c1"), writes=("c2",))
                    s.op("vector", lambda e, c=c, kb=kb: e.scalar_tensor_tensor(
                        out=h_bf[c][:], in0=inner[kb][c][:, 2:T + 2], scalar=cw_sb[c][:, 2:3],
                        in1=c2[:], op0=ALU.mult, op1=ALU.add),
                        reads=(ik, f"cw{c}", "c2"), writes=(f"h{c}",))
                    s.op("scalar", lambda e, c=c: e.activation(
                        out=h_bf[c][:], in_=h_bf[c][:], func=AF.Silu, bias=cbp[c]),
                        reads=(f"h{c}", f"parm{c}"), writes=(f"h{c}",))

                for c in range(CT):
                    s.op("tensor", lambda e, c=c: e.matmul(
                        out=ps[4][0:16, 0:T], lhsT=wst_bf[c][:], rhs=h_bf[c][:],
                        start=(c == 0), stop=(c == CT - 1)),
                        reads=(f"wst{c}", f"h{c}"), writes=("ps4",))
                s.op("scalar", lambda e: e.copy(out=out_tok[0][0:16, 0:T], in_=ps[4][0:16, 0:T]),
                     reads=("ps4",), writes=("otok0",))
                for kk in range(K):
                    bp = kk % 2
                    bb_pk = 5 - bp
                    bbuf = brow[0][:] if bp == 0 else out_tok[0][32:33, 0:T]
                    bkey = "brow0" if bp == 0 else "otok0"
                    s.dma("scalar", f"brw{bp}", lambda e, kk=kk, bbuf=bbuf: e.dma_start(
                        out=bbuf, in_=out_tok[0][kk:kk + 1, 0:T]),
                        reads=("otok0",), writes=(bkey,))
                    lT = (lambda: ones1[:]) if bp == 0 else (lambda: ones33[32:33, :])
                    lk = "ones1" if bp == 0 else "ones33"
                    s.op("tensor", lambda e, bbuf=bbuf, bb_pk=bb_pk, lT=lT: e.matmul(
                        out=ps[bb_pk][:, 0:T], lhsT=lT(), rhs=bbuf[0:1, :],
                        start=True, stop=True),
                        reads=(bkey, lk), writes=(f"ps{bb_pk}",))
                    s.op("scalar", lambda e, kk=kk, bb_pk=bb_pk: e.copy(
                        out=Bb[kk][:], in_=ps[bb_pk][:, 0:T]),
                        reads=(f"ps{bb_pk}",), writes=(f"Bb{kk}",))

                def emit_dt_mm(cog, cb2, wb):
                    cb = cog * 2 + cb2
                    dp = cb % 2
                    pk = f"ps{2 + dp}"
                    for c in range(CT):
                        s.op("tensor", lambda e, c=c, cb2=cb2, dp=dp, wb=wb: e.matmul(
                            out=ps[2 + dp][:, 0:T],
                            lhsT=wt_bf[wb][:, c * 256 + cb2 * 128:c * 256 + (cb2 + 1) * 128],
                            rhs=h_bf[c][:],
                            start=(c == 0), stop=(c == CT - 1)),
                            reads=(f"wtbf{wb}", f"h{c}"), writes=(pk,))

                for cog in range(8):
                    if cog == 2:
                        emit_zx_part(k, True, cgs=[5])
                        if k + 2 <= NCH - 1:
                            emit_prep(k + 2)
                        elif k + 2 == NCH:
                            for c in range(CT):
                                s.op("gpsimd", lambda e, c=c, nkb=nkb: e.memset(
                                    inner[nkb][c][:, T + 1:T + 2], 0.0),
                                    writes=(f"in{nkb}_{c}",))
                        if k > 0:
                            emit_store(k - 1)
                    elif cog == 4:
                        emit_zx_part(k, True, cgs=[6])
                    elif cog == 6:
                        emit_zx_part(k, True, cgs=[7])
                    wb = (k * 8 + cog) % 2
                    for cg_ld in ((0, 1) if cog == 0 else
                                  () if cog == 1 else (cog,)):
                        wbl = (k * 8 + cg_ld) % 2
                        s.dma("sync", f"qwt{wbl}", lambda e, cg_ld=cg_ld, wbl=wbl: e.dma_start(
                            out=wt_bf[wbl][:].rearrange("p (ct f) -> p ct f", ct=16),
                            in_=wtime_d[:, cg_ld * 256:(cg_ld + 1) * 256].rearrange(
                                "(ct p) f -> p ct f", p=128)),
                            reads=("AGwt",), writes=(f"wtbf{wbl}",))
                    if cog == 0:
                        emit_dt_mm(0, 0, wb)
                        emit_dt_mm(0, 1, wb)
                        if k > 0:
                            emit_wout(k - 1)
                        emit_zx_part(k, True, cgs=[4])
                    for cb2 in range(2):
                        cb = cog * 2 + cb2
                        dp = cb % 2
                        pk = f"ps{2 + dp}"
                        if cog != 0:
                            emit_dt_mm(cog, cb2, wb)
                        s.op("scalar", lambda e, cb=cb, dp=dp: e.activation(
                            out=c1[:], in_=ps[2 + dp][:, 0:T], func=AF.Abs,
                            bias=btp[cb]), reads=(pk, f"parm{cb}"), writes=("c1",))
                        s.op("scalar", lambda e: e.activation(
                            out=c2[:], in_=c1[:], func=AF.Exp, scale=-1.0),
                            reads=("c1",), writes=("c2",))
                        s.op("scalar", lambda e: e.activation(
                            out=c1[:], in_=c2[:], func=AF.Ln, bias=1.0),
                            reads=("c2",), writes=("c1",))
                        s.op("vector", lambda e, cb=cb, dp=dp: e.tensor_scalar(
                            out=y_sc2[cb % 2][:], in0=ps[2 + dp][:, 0:T], scalar1=btp[cb],
                            scalar2=0.0, op0=ALU.add, op1=ALU.max),
                            reads=(pk, f"parm{cb}"), writes=(f"y_sc{cb % 2}",))
                        s.op("vector", lambda e, dp=dp, cb=cb: e.tensor_tensor(
                            out=dtb[dp][:], in0=y_sc2[cb % 2][:], in1=c1[:], op=ALU.add),
                            reads=(f"y_sc{cb % 2}", "c1"), writes=(f"dtb{dp}",))
                        s.op("vector", lambda e, cb=cb, dp=dp: e.tensor_tensor(
                            out=g_sc2[cb % 2][:], in0=dtb[dp][:], in1=h_bf[cb][:], op=ALU.mult),
                            reads=(f"dtb{dp}", f"h{cb}"), writes=(f"g_sc{cb % 2}",))
                        s.op("vector", lambda e, cb=cb: e.tensor_scalar(
                            out=y_sc2[cb % 2][:], in0=h_bf[cb][:], scalar1=skp[cb],
                            scalar2=None, op0=ALU.mult),
                            reads=(f"h{cb}", f"parm{cb}"), writes=(f"y_sc{cb % 2}",))
                        yeng = "gpsimd" if cb % 2 == 0 else "vector"
                        for kk in range(K):
                            ap = kk % 2
                            s.op("scalar", lambda e, cb=cb, kk=kk, ap=ap, dp=dp: e.activation(
                                out=dA[ap][:], in_=dtb[dp][:], func=AF.Exp,
                                scale=nA[cb][:, kk:kk + 1]),
                                reads=(f"dtb{dp}", f"nA{cb}"), writes=(f"dA{ap}",))
                            dbeng = "vector"
                            s.op(dbeng, lambda e, kk=kk, ap=ap, cb=cb: e.tensor_tensor(
                                out=dB[ap][:], in0=g_sc2[cb % 2][:], in1=Bb[kk][:], op=ALU.mult),
                                reads=(f"g_sc{cb % 2}", f"Bb{kk}"), writes=(f"dB{ap}",))
                            s.op("vector", lambda e, kk=kk, ap=ap, cb=cb: e.tensor_tensor_scan(
                                out=mem_all[:, kk * T:(kk + 1) * T], data0=dA[ap][:],
                                data1=dB[ap][:], initial=carr[cb][:, kk:kk + 1],
                                op0=ALU.mult, op1=ALU.add),
                                reads=(f"dA{ap}", f"dB{ap}", f"carr{cb}"),
                                writes=(f"mem{kk}",))
                            s.op(yeng, lambda e, kk=kk, cb=cb: e.tensor_tensor(
                                out=y_sc2[cb % 2][:], in0=y_sc2[cb % 2][:],
                                in1=mem_all[:, kk * T:(kk + 1) * T], op=ALU.add),
                                reads=(f"y_sc{cb % 2}", f"mem{kk}"), writes=(f"y_sc{cb % 2}",))
                        s.op("gpsimd", lambda e, cb=cb: e.tensor_copy(
                            out=carr[cb][:], in_=mem_all[:, T - 1::T]),
                            reads=tuple(f"mem{kk}" for kk in range(K)),
                            writes=(f"carr{cb}",))
                        s.op("vector", lambda e, cb=cb, kb=kb: e.tensor_tensor(
                            out=sg[cb][:], in0=y_sc2[cb % 2][:], in1=sg[cb][:], op=ALU.mult),
                            reads=(f"y_sc{cb % 2}",), writes=(f"sg_{cb}",))

            if 2 in phases:
                emit_wout(NCH - 1)
                emit_store(NCH - 1)

            s.join()
            s.op("sync", lambda e: e.nop())

        _LAST_SCHED.clear()
        _LAST_SCHED.append(sched)
        with nc.Block() as block:
            sched.finalize(block)
    return nc


_NC_CACHE = {}


def kernel(**inputs):
    x_full = np.ascontiguousarray(np.asarray(inputs["L0_attn_x"], dtype=np.float32))
    f32 = lambda a: np.ascontiguousarray(np.asarray(a, dtype=np.float32))

    win = f32(inputs["L0_attn_W_in"])
    convw = f32(inputs["L0_attn_conv_w"]).reshape(CT, 128, 3)
    convb = f32(inputs["L0_attn_conv_b"]).reshape(CT, 128, 1)
    wst = f32(inputs["L0_attn_W_state"])
    wtime = f32(inputs["L0_attn_W_time"])
    btime = f32(inputs["L0_attn_b_time"]).reshape(CT, 128, 1)
    alog = f32(inputs["L0_attn_A_log"]).reshape(CT, 128, K)
    skip = f32(inputs["L0_attn_skip"]).reshape(CT, 128, 1)
    wout = f32(inputs["L0_attn_W_out"])
    rw = f32(inputs["L0_ffn_router_w"])
    wv = f32(inputs["L0_ffn_wv"])
    sv_t = np.ascontiguousarray(f32(inputs["L0_ffn_sv"]).reshape(E, F // 128, 128).transpose(0, 2, 1))
    w = f32(inputs["L0_ffn_w"])
    s_t = np.ascontiguousarray(f32(inputs["L0_ffn_s"]).reshape(E, F // 128, 128).transpose(0, 2, 1))
    w1 = f32(inputs["L0_ffn_w1"])
    s1_t = np.ascontiguousarray(f32(inputs["L0_ffn_s1"]).reshape(E, DT, 128).transpose(0, 2, 1))
    ident = np.eye(128, dtype=np.float32)
    ones1 = np.ones((1, 128), dtype=np.float32)

    if "nc" not in _NC_CACHE:
        _NC_CACHE["nc"] = build_program()
    nc = _NC_CACHE["nc"]

    common = dict(conv_w=convw, conv_b=convb, w_state=wst,
                  b_time=btime, a_log=alog, skip=skip,
                  router_w=rw, sv_t=sv_t, s_t=s_t,
                  s1_t=s1_t, ident=ident, ones1=ones1,
                  tril=np.tril(np.ones((128, 128), np.float32), -1),
                  onescol=np.ones((128, 1), np.float32),
                  ecap=(np.arange(E, dtype=np.float32) * CAP)[None, :],
                  capthr=np.ascontiguousarray(np.broadcast_to(
                      (np.arange(E, dtype=np.float32) + 1) * CAP, (128, E))),
                  trmap=np.ascontiguousarray(np.broadcast_to(
                      GROWS + np.arange(E, dtype=np.float32), (128, E))))
    in_maps = [dict(common,
                    x=np.ascontiguousarray(x_full[c]),
                    wv_sh=np.ascontiguousarray(
                        wv[:, c * 128:(c + 1) * 128, :].reshape(E * 128, F)),
                    w_sh=np.ascontiguousarray(
                        w[:, c * 128:(c + 1) * 128, :].reshape(E * 128, F)),
                    w1_sh=np.ascontiguousarray(
                        w1[:, c * 512:(c + 1) * 512, :].reshape(E * 512, D)),
                    wi_sh=np.ascontiguousarray(win[c * (D // 8):(c + 1) * (D // 8), :]),
                    wt_sh=np.ascontiguousarray(wtime[c * (C // 8):(c + 1) * (C // 8), :]),
                    wo_sh=np.ascontiguousarray(wout[c * (C // 8):(c + 1) * (C // 8), :]))
               for c in range(N_CORES)]
    res = run_bass_kernel_spmd(nc, in_maps, core_ids=list(range(N_CORES)))
    out = np.stack([res.results[c]["out"] for c in range(N_CORES)], axis=0)
    return out


# version 33
# speedup vs baseline: 1.0361x; 1.0114x over previous
import numpy as np
from contextlib import ExitStack

import concourse.bass as bass
import concourse.mybir as mybir
from concourse.bass_utils import run_bass_kernel_spmd

F32 = mybir.dt.float32
BF16 = mybir.dt.bfloat16
AF = mybir.ActivationFunctionType
ALU = mybir.AluOpType
AX = mybir.AxisListType

S, D, C, K, E, F = 2048, 1024, 2048, 16, 8, 4096
T = 512
NCH = S // T
DT = D // 128
CT = C // 128
EPS = 1e-5
N_CORES = 8
CAP = 768
GROWS = E * CAP
RCS = ((0, 512), (512, 256))
HUGE = float(1 << 22)


class Sched:
    ENG = ("sync", "scalar", "vector", "tensor", "gpsimd")

    def __init__(self, nc, ctx):
        self.nc = nc
        self.ctx = ctx
        self.sems = {}
        self.cnt = {}
        self.bags = set()
        for name in self.ENG:
            self.sems[name] = ctx.enter_context(nc.semaphore("sem_" + name))
            self.cnt[name] = 0
        self.ops = {e: [] for e in self.ENG}
        self.resw = {}
        self.resr = {}
        self.waited = {e: {} for e in self.ENG}
        self.pending = {e: [] for e in self.ENG}
        self.marks = []

    def mark(self, label):
        self.marks.append((label, dict(self.cnt)))

    def _qsem(self, stream, bag):
        if stream not in self.sems:
            self.sems[stream] = self.ctx.enter_context(
                self.nc.semaphore("sem_" + stream))
            self.cnt[stream] = 0
            if bag:
                self.bags.add(stream)
        return self.sems[stream]

    def _add_wait(self, eng, waits, actor, val):
        if actor in self.bags:
            val = self.cnt[actor]
        if val <= 0:
            return
        if eng == "tensor" and actor == "tensor":
            return
        if self.waited[eng].get(actor, 0) >= val:
            return
        self.waited[eng][actor] = val
        waits.append((self.sems[actor], val))

    def _gather(self, eng, actor, reads, writes):
        skip = actor if actor not in self.ENG else "\x00none"
        waits = []
        for a, v in self.pending[eng]:
            self._add_wait(eng, waits, a, v)
        self.pending[eng] = []
        for r in reads:
            w = self.resw.get(r)
            if w is not None and w[0] != skip:
                self._add_wait(eng, waits, w[0], w[1])
        for r in writes:
            w = self.resw.get(r)
            if w is not None and w[0] != skip:
                self._add_wait(eng, waits, w[0], w[1])
            for a, v in self.resr.get(r, {}).items():
                if a != skip:
                    self._add_wait(eng, waits, a, v)
        return waits

    def _commit(self, actor, val, reads, writes):
        for r in reads:
            d = self.resr.setdefault(r, {})
            d[actor] = max(d.get(actor, 0), val)
        for r in writes:
            self.resw[r] = (actor, val)
            self.resr[r] = {}

    @staticmethod
    def _excl_psum(reads, writes):
        pr = tuple(r for r in reads if r.startswith("ps"))
        if not pr:
            return reads, writes
        return (tuple(r for r in reads if not r.startswith("ps")),
                tuple(writes) + pr)

    def op(self, eng, fn, reads=(), writes=()):
        reads, writes = self._excl_psum(reads, writes)
        waits = self._gather(eng, eng, reads, writes)
        self.cnt[eng] += 1
        self.ops[eng].append((fn, waits, (self.sems[eng], 1)))
        self._commit(eng, self.cnt[eng], reads, writes)

    def dma(self, eng, stream, fn, reads=(), writes=(), bag=False, inc=16):
        reads, writes = self._excl_psum(reads, writes)
        sem = self._qsem(stream, False)
        waits = self._gather(eng, stream, reads, writes)
        if self.cnt[stream] > 0:
            self._add_wait(eng, waits, stream, self.cnt[stream])
        self.cnt[stream] += inc
        self.ops[eng].append((fn, waits, (sem, inc)))
        self._commit(stream, self.cnt[stream], reads, writes)

    def join(self):
        for e in self.ENG:
            pend = []
            for a in self.cnt:
                if a != e and self.cnt[a] > 0:
                    pend.append((a, self.cnt[a]))
            self.pending[e] = pend

    def finalize(self, block):
        for e in self.ENG:
            if not self.ops[e]:
                continue
            oplist = self.ops[e]

            def body(eng, oplist=oplist):
                for fn, waits, inc in oplist:
                    for s, v in waits:
                        eng.wait_ge(s, v)
                    inst = fn(eng)
                    inst.then_inc(inc[0], inc[1])
            getattr(block, e)(body)


def _ecopy(e, out, in_):
    if hasattr(e, "tensor_copy"):
        return e.tensor_copy(out=out, in_=in_)
    return e.copy(out=out, in_=in_)


_LAST_SCHED = []


def build_program(phases=(0, 1, 2), p0router=True, agmode=2):
    nc = bass.Bass("TRN2", target_bir_lowering=False, debug=False)
    dram = {}

    def din(name, shape):
        dram[name] = nc.dram_tensor(name, shape, F32, kind="ExternalInput").ap()
        return dram[name]

    x_d = din("x", [S, D])
    wi_sh = din("wi_sh", [D // 8, 2 * C])
    convw_d = din("conv_w", [CT, 128, 3])
    convb_d = din("conv_b", [CT, 128, 1])
    wst_d = din("w_state", [C, 32])
    wt_sh = din("wt_sh", [C // 8, C])
    btime_d = din("b_time", [CT, 128, 1])
    alog_d = din("a_log", [CT, 128, K])
    skip_d = din("skip", [CT, 128, 1])
    wo_sh = din("wo_sh", [C // 8, D])
    rw_d = din("router_w", [D, E])
    wv_sh = din("wv_sh", [D, F])
    svt_d = din("sv_t", [E, 128, F // 128])
    w_sh = din("w_sh", [D, F])
    st_d = din("s_t", [E, 128, F // 128])
    w1_sh = din("w1_sh", [F, D])
    s1t_d = din("s1_t", [E, 128, DT])
    ident_d = din("ident", [128, 128])
    ones_d = din("ones1", [1, 128])
    tril_d = din("tril", [128, 128])
    onescol_d = din("onescol", [128, 1])
    ecap_d = din("ecap", [1, E])
    capthr_d = din("capthr", [128, E])
    trmap_d = din("trmap", [128, E])
    out_d = nc.dram_tensor("out", [S, D], F32, kind="ExternalOutput").ap()
    def dint(name, shape):
        return nc.dram_tensor(name, shape, BF16, kind="Internal",
                              addr_space="Shared")
    def dintl(name, shape):
        return nc.dram_tensor(name, shape, BF16, kind="Internal")
    stg = {k: dintl(k + "_st", shp) for k, shp in
           [("wt", [C // 8, C]), ("wi", [D // 8, 2 * C]), ("wo", [C // 8, D])]}
    stgE = {k: [dintl(f"{k}_se{e2}", shp) for e2 in range(E)] for k, shp in
            [("wv", [D // 8, F]), ("w", [D // 8, F]), ("w1", [F // 8, D])]}
    wv_allE = [dint(f"wv_ae{e2}", [D, F]) for e2 in range(E)]
    w_allE = [dint(f"w_ae{e2}", [D, F]) for e2 in range(E)]
    w1_allE = [dint(f"w1_ae{e2}", [F, D]) for e2 in range(E)]
    wt_all = dint("wt_all", [C, C])
    wi_all = dint("wi_all", [D, 2 * C])
    wo_all = dint("wo_all", [C, D])
    win_d = wi_all.ap()
    wtime_d = wt_all.ap()
    wout_d = wo_all.ap()
    wv_dE = [t.ap() for t in wv_allE]
    w_dE = [t.ap() for t in w_allE]
    w1_dE = [t.ap() for t in w1_allE]
    xng_d = nc.dram_tensor("xng", [GROWS + 8, D], F32, kind="Internal")
    pairbuf_d = nc.dram_tensor("pairbuf", [GROWS, D], F32, kind="Internal")

    with ExitStack() as ctx:
        sched = Sched(nc, ctx)

        def sb(name, shape, dt=F32, c=ctx):
            return c.enter_context(nc.sbuf_tensor(name, shape, dt))

        xn_bf = [sb(f"xn_bf{d}", [128, S], BF16) for d in range(DT)]
        ident = sb("ident_sb", [128, 128])
        ones1 = sb("ones_sb", [1, 128])
        pos0i = sb("pos0i", [128, S // 128], mybir.dt.int32)
        pos1i = sb("pos1i", [128, S // 128], mybir.dt.int32)
        g0_sb = sb("g0_sb", [128, S // 128])
        g1_sb = sb("g1_sb", [128, S // 128])
        out_tok = [sb(f"out_tok{i}", [128, D]) for i in range(1)]

        ps = [ctx.enter_context(nc.psum_tensor(f"ps{i}", [128, 512], F32))
              for i in range(8)]

        s = sched
        s.dma("sync", "c_const", lambda e: e.dma_start(out=ident[:], in_=ident_d),
              writes=("ident",), bag=True)
        s.dma("sync", "c_const", lambda e: e.dma_start(out=ones1[:], in_=ones_d),
              writes=("ones1",), bag=True)
        s.dma("sync", "c_const", lambda e: e.dma_start(out=tril[:], in_=tril_d),
              writes=("tril",), bag=True)
        s.dma("sync", "c_const", lambda e: e.dma_start(out=onescol[:], in_=onescol_d),
              writes=("onescol",), bag=True)
        s.dma("sync", "c_const", lambda e: e.dma_start(out=capthr[:], in_=capthr_d),
              writes=("capthr",), bag=True)
        s.dma("sync", "c_const", lambda e: e.dma_start(out=trmap[:], in_=trmap_d),
              writes=("trmap",), bag=True)
        s.dma("sync", "outw", lambda e: e.dma_start(out=out_d, in_=x_d),
              writes=tuple(f"outrow{tt}" for tt in range(S // 128)), bag=True)


        with ExitStack() as ctxA:
            tril = ctxA.enter_context(nc.sbuf_tensor("tril_sb", [128, 128], F32))
            onescol = ctxA.enter_context(nc.sbuf_tensor("onescol_sb", [128, 1], F32))
            capthr = ctxA.enter_context(nc.sbuf_tensor("capthr_sb", [128, E], F32))
            trmap = ctxA.enter_context(nc.sbuf_tensor("trmap_sb", [128, E], F32))
            castf = [ctxA.enter_context(nc.sbuf_tensor(f"castf{i}", [128, 1024], F32)) for i in range(4)]
            castb = [ctxA.enter_context(nc.sbuf_tensor(f"castb{i}", [128, 1024], BF16)) for i in range(4)]
            ci = 0
            events = []
            agread = {}
            def cjob(dst_t, tag, src, r0, c0, dr, dc, cw):
                k2 = f"STG{tag}_{len(agread.setdefault(tag, []))}"
                agread[tag].append(k2)
                events.append(("cast", dst_t, k2, src, r0, c0, dr, dc, cw))
            if agmode >= 1:
                for ex in range(E):
                    for nm, src in [("wv", wv_sh), ("w", w_sh)]:
                        for c0 in range(0, F, 1024):
                            cjob(stgE[nm][ex], f"{nm}_{ex}", src,
                                 ex * 128, c0, 0, c0, 1024)
                    for j in range(4):
                        cjob(stgE["w1"][ex], f"w1_{ex}", w1_sh,
                             ex * (F // 8) + j * 128, 0, j * 128, 0, 1024)
                    if agmode >= 2:
                        for nm, dstE in [("wv", wv_allE), ("w", w_allE),
                                         ("w1", w1_allE)]:
                            events.append(("ag", f"ag{nm}{ex}", stgE[nm][ex],
                                           dstE[ex], f"AG{nm}{ex}", f"{nm}_{ex}"))
                for nm, src, rows in [("wt", wt_sh, C // 8), ("wi", wi_sh, D // 8),
                                      ("wo", wo_sh, C // 8)]:
                    cols = src.shape[1]
                    for r0 in range(0, rows, 128):
                        for c0 in range(0, cols, 1024):
                            cjob(stg[nm], nm, src, r0, c0, r0, c0,
                                 min(1024, cols - c0))
                if agmode >= 2:
                    for nm, dst in [("wt", wt_all), ("wi", wi_all), ("wo", wo_all)]:
                        events.append(("ag", f"ag_{nm}", stg[nm], dst,
                                       f"AG{nm}", nm))
            for ev in events:
                if ev[0] == "ag":
                    _, strm, src_t, dst_t, agkey, tag = ev
                    s.dma("gpsimd", strm,
                          lambda e, src_t=src_t, dst_t=dst_t: e.collective_compute(
                              "AllGather", ALU.bypass,
                              replica_groups=[list(range(N_CORES))],
                              ins=[src_t.ap().opt()],
                              outs=[dst_t.ap().opt()]),
                          reads=tuple(agread[tag]), writes=(agkey,), inc=1)
                    continue
                _, dst_t, skey, src, r0, c0, dr, dc, cw = ev
                b = ci % 4
                s.dma("gpsimd", f"cst{b}", lambda e, src=src, r0=r0, c0=c0, cw=cw, b=b: e.dma_start(
                    out=castf[b][:, 0:cw], in_=src[r0:r0 + 128, c0:c0 + cw]),
                    writes=(f"castf{b}",))
                s.op("vector" if ci % 2 == 0 else "scalar",
                     lambda e, cw=cw, b=b: _ecopy(e, castb[b][:, 0:cw], in_=castf[b][:, 0:cw]),
                     reads=(f"castf{b}",), writes=(f"castb{b}",))
                s.dma("gpsimd", f"cso{b}", lambda e, dst_t=dst_t, dr=dr, dc=dc, cw=cw, b=b: e.dma_start(
                    out=dst_t.ap()[dr:dr + 128, dc:dc + cw], in_=castb[b][:, 0:cw]),
                    reads=(f"castb{b}",), writes=(skey,))
                ci += 1
            s.op("vector", lambda e: e.memset(castf[0][:], 0.0), writes=("castf0",))
            for rt in range(GROWS // 128):
                s.dma("gpsimd", f"zf{rt % 4}", lambda e, rt=rt: e.dma_start(
                    out=xng_d.ap()[rt * 128:(rt + 1) * 128, :], in_=castf[0][:]),
                    reads=("castf0",), writes=("xng",), bag=True)

            with ExitStack() as ctx0:
                x_tok = [sb(f"x_tok{i}", [128, D], c=ctx0) for i in range(2)]
                xn_tok2 = [sb(f"xn_tok_{i}", [128, D], c=ctx0) for i in range(2)]
                sq_scr2 = [sb(f"sq_scr_{i}", [128, D], c=ctx0) for i in range(2)]
                xnf = [sb(f"xnf{d}", [128, 128], c=ctx0) for d in range(DT)]
                rw_sb = [sb(f"rw{d}", [128, E], c=ctx0) for d in range(DT)]
                m_t2 = [sb(f"m_t_{i}", [128, 1], c=ctx0) for i in range(2)]
                ss_t2 = [sb(f"ss_t_{i}", [128, 1], c=ctx0) for i in range(2)]
                msq_t2 = [sb(f"msq_t_{i}", [128, 1], c=ctx0) for i in range(2)]
                var_t2 = [sb(f"var_t_{i}", [128, 1], c=ctx0) for i in range(2)]
                std_t2 = [sb(f"std_t_{i}", [128, 1], c=ctx0) for i in range(2)]
                rstd_t2 = [sb(f"rstd_t_{i}", [128, 1], c=ctx0) for i in range(2)]
                lg_sb2 = [sb(f"lg_sb_{i}", [8, 128], c=ctx0) for i in range(2)]
                ltok2 = [sb(f"ltok_{i}", [128, E], c=ctx0) for i in range(2)]
                max82 = [sb(f"max8_{i}", [128, 8], c=ctx0) for i in range(2)]
                negmax2 = [sb(f"negmax_{i}", [128, 1], c=ctx0) for i in range(2)]
                ptok2 = [sb(f"ptok_{i}", [128, E], c=ctx0) for i in range(2)]
                sump2 = [sb(f"sump_{i}", [128, 1], c=ctx0) for i in range(2)]
                rsump2 = [sb(f"rsump_{i}", [128, 1], c=ctx0) for i in range(2)]
                e2v2 = [sb(f"e2v_{i}", [128, 1], c=ctx0) for i in range(2)]
                maskv2 = [sb(f"maskv_{i}", [128, E], c=ctx0) for i in range(2)]
                gmv2 = [sb(f"gmv_{i}", [128, E], c=ctx0) for i in range(2)]
                gts2 = [sb(f"gts_{i}", [128, E], c=ctx0) for i in range(2)]
                P_sb2 = [sb(f"P_sb_{i}", [128, E], c=ctx0) for i in range(2)]
                posm2 = [sb(f"posm_{i}", [128, E], c=ctx0) for i in range(2)]
                posi2 = [sb(f"posi_{i}", [128, E], mybir.dt.int32, c=ctx0) for i in range(2)]
                ovf2 = [sb(f"ovf_{i}", [128, E], c=ctx0) for i in range(2)]
                scr82 = [sb(f"scr8_{i}", [128, E], c=ctx0) for i in range(2)]
                baserow = sb("baserow", [1, E], c=ctx0)
                cnt_sb2 = [sb(f"cnt_sb_{i}", [E, 1], c=ctx0) for i in range(2)]
                cntrow2 = [sb(f"cntrow_{i}", [1, E], c=ctx0) for i in range(2)]
                p0f2 = [sb(f"p0f_{i}", [128, 1], c=ctx0) for i in range(2)]
                p1f2 = [sb(f"p1f_{i}", [128, 1], c=ctx0) for i in range(2)]
                s.dma("scalar", "c_par", lambda e: e.dma_start(out=baserow[:], in_=ecap_d),
                      writes=("baserow",), bag=True)

                for d in range(DT):
                    s.dma("sync", "c_rw",
                          lambda e, d=d: e.dma_start(out=rw_sb[d][:],
                                                     in_=rw_d[d * 128:(d + 1) * 128, :]),
                          writes=(f"rw{d}",), bag=True)

                for tt in range(S // 128):
                    tb = tt % 2
                    xb = x_tok[tt % 2]
                    xkey = f"xtok{tt % 2}"
                    cols = slice(tt * 128, (tt + 1) * 128)
                    s.dma("sync", f"xt{tt % 2}",
                          lambda e, tb=tb, tt=tt, xb=xb: e.dma_start(
                              out=xb[:], in_=x_d[tt * 128:(tt + 1) * 128, :]),
                          writes=(xkey,))
                    s.op("vector", lambda e, tb=tb, xb=xb: e.reduce_sum(
                        out=m_t2[tb][:], in_=xb[:], axis=AX.X), reads=(xkey,), writes=(f"m_t{tb}",))
                    s.op("vector", lambda e, tb=tb: e.tensor_scalar(
                        out=m_t2[tb][:], in0=m_t2[tb][:], scalar1=1.0 / D, scalar2=None,
                        op0=ALU.mult), reads=(f"m_t{tb}",), writes=(f"m_t{tb}",))
                    s.op("scalar", lambda e, tb=tb, xb=xb: e.activation(
                        out=sq_scr2[tb][:], in_=xb[:], func=AF.Square, accum_out=ss_t2[tb][:]),
                        reads=(xkey,), writes=(f"sq_scr{tb}", f"ss_t{tb}"))
                    s.op("vector", lambda e, tb=tb: e.tensor_tensor(
                        out=msq_t2[tb][:], in0=m_t2[tb][:], in1=m_t2[tb][:], op=ALU.mult),
                        reads=(f"m_t{tb}",), writes=(f"msq_t{tb}",))
                    s.op("vector", lambda e, tb=tb: e.scalar_tensor_tensor(
                        out=var_t2[tb][:], in0=ss_t2[tb][:], scalar=1.0 / D, in1=msq_t2[tb][:],
                        op0=ALU.mult, op1=ALU.subtract),
                        reads=(f"ss_t{tb}", f"msq_t{tb}"), writes=(f"var_t{tb}",))
                    s.op("vector", lambda e, tb=tb: e.tensor_scalar(
                        out=var_t2[tb][:], in0=var_t2[tb][:], scalar1=EPS, scalar2=None,
                        op0=ALU.add), reads=(f"var_t{tb}",), writes=(f"var_t{tb}",))
                    s.op("scalar", lambda e, tb=tb: e.activation(
                        out=std_t2[tb][:], in_=var_t2[tb][:], func=AF.Sqrt),
                        reads=(f"var_t{tb}",), writes=(f"std_t{tb}",))
                    s.op("vector", lambda e, tb=tb: e.reciprocal(
                        out=rstd_t2[tb][:], in_=std_t2[tb][:]), reads=(f"std_t{tb}",), writes=(f"rstd_t{tb}",))
                    s.op("vector", lambda e, tb=tb, xb=xb: e.tensor_scalar(
                        out=xn_tok2[tb][:], in0=xb[:], scalar1=m_t2[tb][:], scalar2=rstd_t2[tb][:],
                        op0=ALU.subtract, op1=ALU.mult),
                        reads=(xkey, f"m_t{tb}", f"rstd_t{tb}"), writes=(f"xn_tok{tb}",))
                    for half in range(2):
                        pk_n = f"ps{half}"
                        for j in range(4):
                            d = half * 4 + j
                            s.op("tensor", lambda e, tb=tb, d=d, j=j, half=half: e.transpose(
                                out=ps[half][:, j * 128:(j + 1) * 128],
                                in_=xn_tok2[tb][:, d * 128:(d + 1) * 128], identity=ident[:]),
                                reads=(f"xn_tok{tb}", "ident"), writes=(pk_n,))
                        for j in range(4):
                            d = half * 4 + j
                            s.op("scalar", lambda e, tb=tb, d=d, j=j, half=half, cols=cols: e.copy(
                                out=xn_bf[d][:, cols], in_=ps[half][:, j * 128:(j + 1) * 128]),
                                reads=(pk_n,), writes=(f"xnbf{d}",))
                            s.op("vector", lambda e, tb=tb, d=d, j=j, half=half: e.tensor_copy(
                                out=xnf[d][:], in_=ps[half][:, j * 128:(j + 1) * 128]),
                                reads=(pk_n,), writes=(f"xnf{d}",))
                    if not p0router:
                        continue
                    if True:
                        for d in range(DT):
                            s.op("tensor", lambda e, tb=tb, d=d: e.matmul(
                                out=ps[4][0:8, 0:128], lhsT=rw_sb[d][:], rhs=xnf[d][:],
                                start=(d == 0), stop=(d == DT - 1)),
                                reads=(f"xnf{d}", f"rw{d}"), writes=("ps4",))
                        s.op("scalar", lambda e, tb=tb: e.copy(out=lg_sb2[tb][:], in_=ps[4][0:8, 0:128]),
                             reads=("ps4",), writes=(f"lg_sb{tb}",))
                        s.op("tensor", lambda e, tb=tb: e.transpose(
                            out=ps[5][0:128, 0:8], in_=lg_sb2[tb][:], identity=ident[0:8, 0:8]),
                            reads=(f"lg_sb{tb}", "ident"), writes=("ps5",))
                        s.op("vector", lambda e, tb=tb: e.tensor_copy(out=ltok2[tb][:], in_=ps[5][0:128, 0:8]),
                             reads=("ps5",), writes=(f"ltok{tb}",))
                        s.op("vector", lambda e, tb=tb: e.max(out=max82[tb][:], in_=ltok2[tb][:]),
                             reads=(f"ltok{tb}",), writes=(f"max8{tb}",))
                        s.op("vector", lambda e, tb=tb: e.tensor_scalar(
                            out=negmax2[tb][:], in0=max82[tb][:, 0:1], scalar1=-1.0, scalar2=None,
                            op0=ALU.mult), reads=(f"max8{tb}",), writes=(f"negmax{tb}",))
                        s.op("scalar", lambda e, tb=tb: e.activation(
                            out=ptok2[tb][:], in_=ltok2[tb][:], func=AF.Exp, bias=negmax2[tb][:]),
                            reads=(f"ltok{tb}", f"negmax{tb}"), writes=(f"ptok{tb}",))
                        s.op("vector", lambda e, tb=tb: e.reduce_sum(
                            out=sump2[tb][:], in_=ptok2[tb][:], axis=AX.X), reads=(f"ptok{tb}",), writes=(f"sump{tb}",))
                        s.op("vector", lambda e, tb=tb: e.reciprocal(out=rsump2[tb][:], in_=sump2[tb][:]),
                             reads=(f"sump{tb}",), writes=(f"rsump{tb}",))
                        s.op("scalar", lambda e, tb=tb: e.activation(
                            out=e2v2[tb][:], in_=max82[tb][:, 1:2], func=AF.Exp, bias=negmax2[tb][:]),
                            reads=(f"max8{tb}", f"negmax{tb}"), writes=(f"e2v{tb}",))
                        s.op("vector", lambda e, tb=tb: e.tensor_scalar(
                            out=maskv2[tb][:], in0=ptok2[tb][:], scalar1=e2v2[tb][:], scalar2=None,
                            op0=ALU.is_ge), reads=(f"ptok{tb}", f"e2v{tb}"), writes=(f"maskv{tb}",))
                        s.op("vector", lambda e, tb=tb: e.tensor_tensor(
                            out=gmv2[tb][:], in0=ptok2[tb][:], in1=maskv2[tb][:], op=ALU.mult),
                            reads=(f"ptok{tb}", f"maskv{tb}"), writes=(f"gmv{tb}",))
                        s.op("vector", lambda e, tb=tb: e.tensor_scalar(
                            out=gts2[tb][:], in0=gmv2[tb][:], scalar1=rsump2[tb][:], scalar2=None,
                            op0=ALU.mult), reads=(f"gmv{tb}", f"rsump{tb}"), writes=(f"gts{tb}",))
                        s.op("tensor", lambda e, tb=tb: e.matmul(
                            out=ps[6][0:128, 0:E], lhsT=tril[:], rhs=maskv2[tb][:],
                            start=True, stop=True),
                            reads=(f"maskv{tb}", "tril"), writes=("ps6",))
                        s.op("tensor", lambda e, tb=tb: e.matmul(
                            out=ps[2][0:E, 0:1], lhsT=maskv2[tb][:], rhs=onescol[:],
                            start=True, stop=True),
                            reads=(f"maskv{tb}", "onescol"), writes=("ps2",))
                        s.op("vector", lambda e, tb=tb: e.tensor_copy(
                            out=P_sb2[tb][:], in_=ps[6][0:128, 0:E]),
                            reads=("ps6",), writes=(f"P_sb{tb}",))
                        s.op("vector", lambda e, tb=tb: e.tensor_copy(
                            out=cnt_sb2[tb][:], in_=ps[2][0:E, 0:1]),
                            reads=("ps2",), writes=(f"cnt_sb{tb}",))
                        s.op("tensor", lambda e, tb=tb: e.matmul(
                            out=ps[3][0:128, 0:E], lhsT=ones1[:], rhs=baserow[:],
                            start=True, stop=True),
                            reads=("baserow", "ones1"), writes=("ps3",))
                        s.op("vector", lambda e, tb=tb: e.tensor_tensor(
                            out=posm2[tb][:], in0=P_sb2[tb][:], in1=ps[3][0:128, 0:E], op=ALU.add),
                            reads=(f"P_sb{tb}",), writes=(f"posm{tb}", "ps3"))
                        s.op("tensor", lambda e, tb=tb: e.transpose(
                            out=ps[2][0:1, 0:E], in_=cnt_sb2[tb][:], identity=ident[0:E, 0:E]),
                            reads=(f"cnt_sb{tb}", "ident"), writes=("ps2",))
                        s.op("vector", lambda e, tb=tb: e.tensor_copy(
                            out=cntrow2[tb][:], in_=ps[2][0:1, 0:E]),
                            reads=("ps2",), writes=(f"cntrow{tb}",))
                        s.op("vector", lambda e, tb=tb: e.tensor_tensor(
                            out=baserow[:], in0=baserow[:], in1=cntrow2[tb][:], op=ALU.add),
                            reads=(f"cntrow{tb}", "baserow"), writes=("baserow",))
                        s.op("vector", lambda e, tb=tb: e.tensor_tensor(
                            out=ovf2[tb][:], in0=posm2[tb][:], in1=capthr[:], op=ALU.is_ge),
                            reads=(f"posm{tb}", "capthr"), writes=(f"ovf{tb}",))
                        s.op("vector", lambda e, tb=tb: e.tensor_scalar(
                            out=scr82[tb][:], in0=maskv2[tb][:], scalar1=-1.0, scalar2=1.0,
                            op0=ALU.mult, op1=ALU.add),
                            reads=(f"maskv{tb}",), writes=(f"scr8{tb}",))
                        s.op("vector", lambda e, tb=tb: e.tensor_tensor(
                            out=scr82[tb][:], in0=scr82[tb][:], in1=ovf2[tb][:], op=ALU.add),
                            reads=(f"scr8{tb}", f"ovf{tb}"), writes=(f"scr8{tb}",))
                        s.op("vector", lambda e, tb=tb: e.scalar_tensor_tensor(
                            out=posm2[tb][:], in0=scr82[tb][:], scalar=HUGE, in1=posm2[tb][:],
                            op0=ALU.mult, op1=ALU.add),
                            reads=(f"scr8{tb}", f"posm{tb}"), writes=(f"posm{tb}",))
                        s.op("vector", lambda e, tb=tb: e.tensor_tensor(
                            out=scr82[tb][:], in0=posm2[tb][:], in1=trmap[:], op=ALU.min),
                            reads=(f"posm{tb}", "trmap"), writes=(f"scr8{tb}",))
                        s.op("vector", lambda e, tb=tb: e.tensor_copy(out=posi2[tb][:], in_=scr82[tb][:]),
                             reads=(f"scr8{tb}",), writes=(f"posi{tb}",))
                        for ei in range(E):
                            s.dma("gpsimd", f"sct{ei}",
                                  lambda e, tb=tb, ei=ei: e.indirect_dma_start(
                                      out=xng_d.ap(),
                                      out_offset=bass.IndirectOffsetOnAxis(
                                          ap=posi2[tb][:, ei:ei + 1], axis=0),
                                      in_=xn_tok2[tb][:], in_offset=None),
                                  reads=(f"posi{tb}", f"xn_tok{tb}"), writes=(f"xng_e{ei}",))
                        s.op("vector", lambda e, tb=tb: e.tensor_reduce(
                            out=p0f2[tb][:], in_=posm2[tb][:], axis=AX.X, op=ALU.min),
                            reads=(f"posm{tb}",), writes=(f"p0f{tb}",))
                        s.op("vector", lambda e, tb=tb: e.tensor_scalar(
                            out=scr82[tb][:], in0=posm2[tb][:], scalar1=p0f2[tb][:], scalar2=None,
                            op0=ALU.is_equal), reads=(f"posm{tb}", f"p0f{tb}"), writes=(f"scr8{tb}",))
                        s.op("vector", lambda e, tb=tb, tt=tt: e.tensor_copy(
                            out=pos0i[:, tt:tt + 1], in_=p0f2[tb][:]),
                            reads=(f"p0f{tb}",), writes=("pos0i",))
                        s.op("vector", lambda e, tb=tb: e.tensor_tensor(
                            out=ovf2[tb][:], in0=gts2[tb][:], in1=scr82[tb][:], op=ALU.mult),
                            reads=(f"gts{tb}", f"scr8{tb}"), writes=(f"ovf{tb}",))
                        s.op("vector", lambda e, tb=tb, tt=tt: e.reduce_sum(
                            out=g0_sb[:, tt:tt + 1], in_=ovf2[tb][:], axis=AX.X),
                            reads=(f"ovf{tb}",), writes=("g0_sb",))
                        s.op("vector", lambda e, tb=tb: e.scalar_tensor_tensor(
                            out=posm2[tb][:], in0=scr82[tb][:], scalar=HUGE, in1=posm2[tb][:],
                            op0=ALU.mult, op1=ALU.add),
                            reads=(f"scr8{tb}", f"posm{tb}"), writes=(f"posm{tb}",))
                        s.op("vector", lambda e, tb=tb: e.tensor_reduce(
                            out=p1f2[tb][:], in_=posm2[tb][:], axis=AX.X, op=ALU.min),
                            reads=(f"posm{tb}",), writes=(f"p1f{tb}",))
                        s.op("vector", lambda e, tb=tb: e.tensor_scalar(
                            out=scr82[tb][:], in0=posm2[tb][:], scalar1=p1f2[tb][:], scalar2=None,
                            op0=ALU.is_equal), reads=(f"posm{tb}", f"p1f{tb}"), writes=(f"scr8{tb}",))
                        s.op("vector", lambda e, tb=tb, tt=tt: e.tensor_copy(
                            out=pos1i[:, tt:tt + 1], in_=p1f2[tb][:]),
                            reads=(f"p1f{tb}",), writes=("pos1i",))
                        s.op("vector", lambda e, tb=tb: e.tensor_tensor(
                            out=ovf2[tb][:], in0=gts2[tb][:], in1=scr82[tb][:], op=ALU.mult),
                            reads=(f"gts{tb}", f"scr8{tb}"), writes=(f"ovf{tb}",))
                        s.op("vector", lambda e, tb=tb, tt=tt: e.reduce_sum(
                            out=g1_sb[:, tt:tt + 1], in_=ovf2[tb][:], axis=AX.X),
                            reads=(f"ovf{tb}",), writes=("g1_sb",))
                s.join()

            if 1 in phases:
                ca = lambda n, shp, dt=F32: ctxA.enter_context(nc.sbuf_tensor(n, shp, dt))
                Zp = [ca(f"zp{i}", [128, CAP], BF16) for i in range(4)]
                wv_bf = [ca(f"wvbf{i}", [128, 2048], BF16) for i in range(2)]
                w_bf = [ca(f"wbf{i}", [128, 2048], BF16) for i in range(2)]
                w1_bf = [ca(f"w1bf{i}", [128, 2048], BF16) for i in range(2)]
                xn_g = [[ca(f"xng{b}_{d}", [128, CAP], BF16) for d in range(DT)]
                        for b in range(2)]
                tokrow = [ca(f"tokrow{i}", [128, D]) for i in range(2)]
                po = [ca(f"po{d}", [128, CAP]) for d in range(DT)]
                ptk = [ca(f"ptok{i}", [128, D]) for i in range(2)]
                gb0 = [ca(f"gb0_{i}", [128, D]) for i in range(2)]
                gb1 = [ca(f"gb1_{i}", [128, D]) for i in range(2)]
                mo = [ca(f"mo{i}", [128, D]) for i in range(2)]
                sv_sb = [ca(f"svsb{i}", [128, 32]) for i in range(2)]
                ss_sb = [ca(f"sssb{i}", [128, 32]) for i in range(2)]
                s1_sb = [ca(f"s1sb{i}", [128, DT]) for i in range(2)]
                gy = [ca(f"gy{i}", [128, 512]) for i in range(3)]

            cast_i = [0]

            def cast_eng():
                cast_i[0] += 1
                return ("vector", "scalar")[cast_i[0] % 2]

            for ei in (range(E) if 1 in phases else []):
                ep = ei % 2
                s.dma("scalar", f"svp{ep}", lambda e, ei=ei, ep=ep: e.dma_start(
                    out=sv_sb[ep][:], in_=svt_d[ei]), reads=(), writes=(f"svsb{ep}",))
                s.dma("scalar", f"svp{ep}", lambda e, ei=ei, ep=ep: e.dma_start(
                    out=ss_sb[ep][:], in_=st_d[ei]), reads=(), writes=(f"sssb{ep}",))
                s.dma("scalar", f"svp{ep}", lambda e, ei=ei, ep=ep: e.dma_start(
                    out=s1_sb[ep][:], in_=s1t_d[ei]), reads=(), writes=(f"s1sb{ep}",))
                xgkeys = tuple(f"xng_e{q}" for q in range(E))
                for rt in range(CAP // 128):
                    rb = rt % 2
                    s.dma("sync", f"xg{rb}", lambda e, ei=ei, rt=rt, rb=rb: e.dma_start(
                        out=tokrow[rb][:],
                        in_=xng_d.ap()[ei * CAP + rt * 128:ei * CAP + (rt + 1) * 128, :]),
                        reads=xgkeys, writes=(f"tokrow{rb}",))
                    for d in range(DT):
                        pbk = 6 + (d // 4)
                        s.op("tensor", lambda e, d=d, rb=rb, pbk=pbk: e.transpose(
                            out=ps[pbk][:, (d % 4) * 128:(d % 4 + 1) * 128],
                            in_=tokrow[rb][:, d * 128:(d + 1) * 128], identity=ident[:]),
                            reads=(f"tokrow{rb}", "ident"), writes=(f"ps{pbk}",))
                        if d % 4 == 3:
                            for dd in range(d - 3, d + 1):
                                s.op("scalar", lambda e, dd=dd, rt=rt, ei=ei, pbk=pbk: e.copy(
                                    out=xn_g[ei % 2][dd][:, rt * 128:(rt + 1) * 128],
                                    in_=ps[pbk][:, (dd % 4) * 128:(dd % 4 + 1) * 128]),
                                    reads=(f"ps{pbk}",), writes=(f"xg{ei % 2}_{dd}",))

                for fg2 in range(16):
                    gb = fg2 % 2
                    cq = fg2 * 256
                    s.dma("sync", f"qwv{gb}", lambda e, ei=ei, gb=gb, cq=cq: e.dma_start(
                        out=wv_bf[gb][:].rearrange("p (dt f) -> p dt f", dt=8),
                        in_=wv_dE[ei][:, cq:cq + 256].rearrange(
                            "(dt p) f -> p dt f", p=128)),
                        reads=(f"AGwv{ei}",), writes=(f"wvbf{gb}",))
                    s.dma("sync", f"qw{gb}", lambda e, ei=ei, gb=gb, cq=cq: e.dma_start(
                        out=w_bf[gb][:].rearrange("p (dt f) -> p dt f", dt=8),
                        in_=w_dE[ei][:, cq:cq + 256].rearrange(
                            "(dt p) f -> p dt f", p=128)),
                        reads=(f"AGw{ei}",), writes=(f"wbf{gb}",))
                    s.dma("sync", f"qw1{gb}", lambda e, ei=ei, gb=gb, cq=cq: e.dma_start(
                        out=w1_bf[gb][:].rearrange("p (ft d) -> p ft d", ft=2),
                        in_=w1_dE[ei][cq:cq + 256, :].rearrange(
                            "(ft p) d -> p ft d", p=128)),
                        reads=(f"AGw1{ei}",), writes=(f"w1bf{gb}",))

                    for fb2 in range(2):
                        ftg = fg2 * 2 + fb2
                        ftl = (fg2 % 2) * 2 + fb2
                        for rci, (r0, rw) in enumerate(RCS):
                            u = (ftg * 2 + rci) % 3
                            vk, yk = f"ps{u}", f"ps{3 + u}"
                            rcs = slice(r0, r0 + rw)
                            for d in range(DT):
                                s.op("tensor", lambda e, d=d, fb2=fb2, rcs=rcs, rw=rw, u=u, gb=gb, ei=ei: e.matmul(
                                    out=ps[u][:, 0:rw],
                                    lhsT=wv_bf[gb][:, d * 256 + fb2 * 128:d * 256 + (fb2 + 1) * 128],
                                    rhs=xn_g[ei % 2][d][:, rcs],
                                    start=(d == 0), stop=(d == DT - 1)),
                                    reads=(f"wvbf{gb}", f"xg{ei % 2}_{d}"), writes=(vk,))
                            for d in range(DT):
                                s.op("tensor", lambda e, d=d, fb2=fb2, rcs=rcs, rw=rw, u=u, gb=gb, ei=ei: e.matmul(
                                    out=ps[3 + u][:, 0:rw],
                                    lhsT=w_bf[gb][:, d * 256 + fb2 * 128:d * 256 + (fb2 + 1) * 128],
                                    rhs=xn_g[ei % 2][d][:, rcs],
                                    start=(d == 0), stop=(d == DT - 1)),
                                    reads=(f"wbf{gb}", f"xg{ei % 2}_{d}"), writes=(yk,))
                            s.op("scalar", lambda e, u=u, rw=rw, ftg=ftg, ep=ep: e.activation(
                                out=gy[u][:, 0:rw], in_=ps[3 + u][:, 0:rw],
                                func=AF.Gelu_apprx_tanh, scale=ss_sb[ep][:, ftg:ftg + 1]),
                                reads=(yk, f"sssb{ep}"), writes=(f"gy{u}",))
                            s.op("vector", lambda e, u=u, ftl=ftl, ftg=ftg, rcs=rcs, rw=rw, ep=ep: e.scalar_tensor_tensor(
                                out=Zp[ftl][:, rcs], in0=ps[u][:, 0:rw],
                                scalar=sv_sb[ep][:, ftg:ftg + 1], in1=gy[u][:, 0:rw],
                                op0=ALU.mult, op1=ALU.mult),
                                reads=(vk, f"gy{u}", f"svsb{ep}"), writes=(f"zp{ftl}",))
                    if fg2 % 2 == 0:
                        continue
                    first_window = (fg2 == 1)
                    for d in range(DT):
                        for rci, (r0, rw) in enumerate(RCS):
                            o = (d * 2 + rci) % 2
                            ok = f"ps{6 + o}"
                            rcs = slice(r0, r0 + rw)
                            for ftl in range(4):
                                w1b = ftl // 2
                                s.op("tensor", lambda e, d=d, ftl=ftl, rcs=rcs, rw=rw, o=o, w1b=w1b: e.matmul(
                                    out=ps[6 + o][:, 0:rw],
                                    lhsT=w1_bf[w1b][:, (ftl % 2) * 1024 + d * 128:(ftl % 2) * 1024 + (d + 1) * 128],
                                    rhs=Zp[ftl][:, rcs],
                                    start=(ftl == 0), stop=(ftl == 3)),
                                    reads=(f"w1bf{w1b}", f"zp{ftl}"), writes=(ok,))
                            if first_window:
                                s.op("vector", lambda e, d=d, rcs=rcs, rw=rw, o=o, ep=ep: e.tensor_scalar(
                                    out=po[d][:, rcs], in0=ps[6 + o][:, 0:rw],
                                    scalar1=s1_sb[ep][:, d:d + 1], scalar2=None,
                                    op0=ALU.mult),
                                    reads=(ok, f"s1sb{ep}"), writes=(f"po{d}",))
                            else:
                                s.op("vector", lambda e, d=d, rcs=rcs, rw=rw, o=o, ep=ep: e.scalar_tensor_tensor(
                                    out=po[d][:, rcs], in0=ps[6 + o][:, 0:rw],
                                    scalar=s1_sb[ep][:, d:d + 1], in1=po[d][:, rcs],
                                    op0=ALU.mult, op1=ALU.add),
                                    reads=(ok, f"s1sb{ep}", f"po{d}"), writes=(f"po{d}",))

                for rt in range(CAP // 128):
                    rb = rt % 2
                    for d in range(DT):
                        pbk = 7 - (d // 4)
                        s.op("tensor", lambda e, d=d, rt=rt, pbk=pbk: e.transpose(
                            out=ps[pbk][:, (d % 4) * 128:(d % 4 + 1) * 128],
                            in_=po[d][:, rt * 128:(rt + 1) * 128], identity=ident[:]),
                            reads=(f"po{d}", "ident"), writes=(f"ps{pbk}",))
                        if d % 4 == 3:
                            for dd in range(d - 3, d + 1):
                                s.op("vector", lambda e, dd=dd, rb=rb, pbk=pbk: e.tensor_copy(
                                    out=ptk[rb][:, dd * 128:(dd + 1) * 128],
                                    in_=ps[pbk][:, (dd % 4) * 128:(dd % 4 + 1) * 128]),
                                    reads=(f"ps{pbk}",), writes=(f"ptok{rb}",))
                    s.dma("sync", f"pb{rb}", lambda e, ei=ei, rt=rt, rb=rb: e.dma_start(
                        out=pairbuf_d.ap()[ei * CAP + rt * 128:ei * CAP + (rt + 1) * 128, :],
                        in_=ptk[rb][:]),
                        reads=(f"ptok{rb}",), writes=("pairbuf",))

            for tt in (range(S // 128) if 1 in phases else []):
                mb_ = tt % 2
                s.dma("gpsimd", f"gth{mb_}", lambda e, tt=tt, mb_=mb_: e.indirect_dma_start(
                    out=gb0[mb_][:], out_offset=None,
                    in_=pairbuf_d.ap(),
                    in_offset=bass.IndirectOffsetOnAxis(ap=pos0i[:, tt:tt + 1], axis=0)),
                    reads=("pairbuf", "pos0i"), writes=(f"gb0_{mb_}",))
                s.dma("gpsimd", f"gth{mb_}", lambda e, tt=tt, mb_=mb_: e.indirect_dma_start(
                    out=gb1[mb_][:], out_offset=None,
                    in_=pairbuf_d.ap(),
                    in_offset=bass.IndirectOffsetOnAxis(ap=pos1i[:, tt:tt + 1], axis=0)),
                    reads=("pairbuf", "pos1i"), writes=(f"gb1_{mb_}",))
                s.op("vector", lambda e, tt=tt, mb_=mb_: e.tensor_scalar(
                    out=mo[mb_][:], in0=gb0[mb_][:], scalar1=g0_sb[:, tt:tt + 1],
                    scalar2=None, op0=ALU.mult),
                    reads=(f"gb0_{mb_}", "g0_sb"), writes=(f"mo{mb_}",))
                s.op("vector", lambda e, tt=tt, mb_=mb_: e.scalar_tensor_tensor(
                    out=mo[mb_][:], in0=gb1[mb_][:], scalar=g1_sb[:, tt:tt + 1],
                    in1=mo[mb_][:], op0=ALU.mult, op1=ALU.add),
                    reads=(f"gb1_{mb_}", "g1_sb", f"mo{mb_}"), writes=(f"mo{mb_}",))
                s.dma("gpsimd", "outadd", lambda e, tt=tt, mb_=mb_: e.dma_start(
                    out=out_d[tt * 128:(tt + 1) * 128, :], in_=mo[mb_][:],
                    accum_op=ALU.add),
                    reads=(f"mo{mb_}", f"outrow{tt}"), writes=(f"outrow{tt}",), bag=True)
            s.join()

        with ExitStack() as ctx2:
            if 2 in phases:
                inner = [[sb(f"inner{b}_{c}", [128, T + 2], BF16, c=ctx2) for c in range(CT)]
                         for b in range(2)]
                sg = [sb(f"sg_{c}", [128, T], BF16, c=ctx2) for c in range(CT)]
                h_bf = [sb(f"h{c}", [128, T], BF16, c=ctx2) for c in range(CT)]
                Bb = [sb(f"Bb{k}", [128, T], BF16, c=ctx2) for k in range(K)]
                brow = [sb(f"brow{i}", [1, T], c=ctx2) for i in range(1)]
                wi_bf = [sb(f"wibf{i}", [128, 4096], BF16, c=ctx2) for i in range(1)]
                wt_bf = [sb(f"wtbf{i}", [128, 4096], BF16, c=ctx2) for i in range(2)]
                wo_bf = [sb(f"wobf{i}", [128, 4096], BF16, c=ctx2) for i in range(1)]
                att = [sb(f"att{d}", [128, T], c=ctx2) for d in range(DT)]
                ones33 = sb("ones33", [33, 128], c=ctx2)
                mem_all = sb("mem_all", [128, K * T], BF16, c=ctx2)
                dA = [sb(f"dA{i}", [128, T], BF16, c=ctx2) for i in range(3)]
                dB = [sb(f"dB{i}", [128, T], BF16, c=ctx2) for i in range(2)]
                dtb = [sb(f"dtb{i}", [128, T], c=ctx2) for i in range(2)]
                g_sc2 = [sb(f"g_sc{i}", [128, T], BF16, c=ctx2) for i in range(2)]
                y_sc2 = [sb(f"y_sc{i}", [128, T], c=ctx2) for i in range(2)]
                c1 = sb("c1", [128, T], c=ctx2)
                c2 = sb("c2", [128, T], c=ctx2)
                carr = [sb(f"carr{c}", [128, K], c=ctx2) for c in range(CT)]
                cw_sb = [sb(f"cw{c}", [128, 3], c=ctx2) for c in range(CT)]
                parm = [sb(f"parm{c}", [128, 4], c=ctx2) for c in range(CT)]
                cbp = [parm[c][:, 0:1] for c in range(CT)]
                btp = [parm[c][:, 1:2] for c in range(CT)]
                skp = [parm[c][:, 2:3] for c in range(CT)]
                nA = [sb(f"nA{c}", [128, K], c=ctx2) for c in range(CT)]
                wst_bf = [sb(f"wst{c}", [128, 16], BF16, c=ctx2) for c in range(CT)]
                wst_f = sb("wst_f", [128, 32], c=ctx2)

            for c in (range(CT) if 2 in phases else []):
                s.dma("scalar", "c_par", lambda e, c=c: e.dma_start(
                    out=cw_sb[c][:], in_=convw_d[c]), writes=(f"cw{c}",), bag=True)
                s.dma("scalar", "c_par", lambda e, c=c: e.dma_start(
                    out=cbp[c], in_=convb_d[c]), writes=(f"parm{c}",), bag=True)
                s.dma("scalar", "c_par", lambda e, c=c: e.dma_start(
                    out=btp[c], in_=btime_d[c]), writes=(f"parm{c}",), bag=True)
                s.dma("scalar", "c_par", lambda e, c=c: e.dma_start(
                    out=skp[c], in_=skip_d[c]), writes=(f"parm{c}",), bag=True)
                s.dma("scalar", "asbq", lambda e, c=c: e.dma_start(
                    out=wst_f[:, 0:K], in_=alog_d[c]), reads=(), writes=("wst_f",))
                s.op("scalar", lambda e, c=c: e.activation(
                    out=nA[c][:], in_=wst_f[:, 0:K], func=AF.Exp),
                    reads=("wst_f",), writes=(f"nA{c}",))
                s.op("vector", lambda e, c=c: e.tensor_scalar(
                    out=nA[c][:], in0=nA[c][:], scalar1=-1.0, scalar2=None,
                    op0=ALU.mult), reads=(f"nA{c}",), writes=(f"nA{c}",))
                s.dma("scalar", "wstq", lambda e, c=c: e.dma_start(
                    out=wst_f[:], in_=wst_d[c * 128:(c + 1) * 128, :]),
                    reads=(), writes=("wst_f",))
                s.op("vector", lambda e, c=c: e.tensor_copy(
                    out=wst_bf[c][:], in_=wst_f[:, 16:32]), reads=("wst_f",), writes=(f"wst{c}",))
                s.op("gpsimd", lambda e, c=c: e.memset(carr[c][:], 0.0),
                     writes=(f"carr{c}",))
                s.op("gpsimd", lambda e, c=c: e.memset(inner[0][c][:, 0:1], 0.0),
                     writes=(f"in0_{c}",))
            if 2 in phases:
                s.op("vector", lambda e: e.memset(ones33[:], 1.0),
                     writes=("ones33",))

            def emit_zx_part(k, gate, cgs=None):
                kb = k % 2
                tcs = slice(k * T, (k + 1) * T)
                for cg in (cgs if cgs is not None
                           else (range(4, 8) if gate else range(4))):
                    for hf in range(2):
                        s.dma("sync", f"qwi{hf}", lambda e, cg=cg, hf=hf: e.dma_start(
                            out=wi_bf[0][:, hf * 2048:(hf + 1) * 2048].rearrange(
                                "p (dt f) -> p dt f", dt=4),
                            in_=win_d[hf * 512:(hf + 1) * 512,
                                      cg * 512:(cg + 1) * 512].rearrange(
                                "(dt p) f -> p dt f", p=128)),
                            reads=("AGwi",), writes=(f"wibf{hf}",))
                    for pair in range(2):
                        for hf in range(2):
                            for cb2 in (pair * 2, pair * 2 + 1):
                                cbg = cg * 4 + cb2
                                pk = f"ps{cbg % 2}"
                                for d in range(hf * 4, hf * 4 + 4):
                                    s.op("tensor", lambda e, d=d, cb2=cb2, tcs=tcs, cbg=cbg: e.matmul(
                                        out=ps[cbg % 2][:, 0:T],
                                        lhsT=wi_bf[0][:, d * 512 + cb2 * 128:d * 512 + (cb2 + 1) * 128],
                                        rhs=xn_bf[d][:, tcs],
                                        start=(d == 0), stop=(d == DT - 1)),
                                        reads=(f"wibf{hf}", f"xnbf{d}"), writes=(pk,))
                                if hf == 0:
                                    continue
                                if cbg < 16:
                                    s.op("scalar", lambda e, cbg=cbg, kb=kb: e.copy(
                                        out=inner[kb][cbg][:, 1:T + 1], in_=ps[cbg % 2][:, 0:T]),
                                        reads=(pk,), writes=(f"in{kb}_{cbg}",))
                                else:
                                    gbi = cbg - 16
                                    s.op("scalar", lambda e, gbi=gbi, cbg=cbg: e.activation(
                                        out=sg[gbi][:], in_=ps[cbg % 2][:, 0:T], func=AF.Silu),
                                        reads=(pk,), writes=(f"sg_{gbi}",))

            def emit_prep(kn):
                pb = kn % 2
                qb = (kn - 1) % 2
                for c in range(CT):
                    s.op("gpsimd", lambda e, c=c, pb=pb, qb=qb: e.tensor_copy(
                        out=inner[pb][c][:, 0:1], in_=inner[qb][c][:, T:T + 1]),
                        reads=(f"in{qb}_{c}",), writes=(f"in{pb}_{c}",))
                emit_zx_part(kn, False)
                for c in range(CT):
                    s.op("gpsimd", lambda e, c=c, pb=pb, qb=qb: e.tensor_copy(
                        out=inner[qb][c][:, T + 1:T + 2], in_=inner[pb][c][:, 1:2]),
                        reads=(f"in{pb}_{c}",), writes=(f"in{qb}_{c}",))

            def emit_store(ks):
                for tt2 in range(T // 128):
                    tt = ks * (T // 128) + tt2
                    ob = out_tok[0]
                    okey = "otok0"
                    for half in range(2):
                        pk = f"ps{4 + half}"
                        for j in range(4):
                            d = half * 4 + j
                            s.op("tensor", lambda e, d=d, j=j, half=half, tt2=tt2: e.transpose(
                                out=ps[4 + half][:, j * 128:(j + 1) * 128],
                                in_=att[d][:, tt2 * 128:(tt2 + 1) * 128], identity=ident[:]),
                                reads=(f"att{d}", "ident"), writes=(pk,))
                        s.op("scalar", lambda e, half=half, ob=ob: e.copy(
                            out=ob[:, half * 512:(half + 1) * 512], in_=ps[4 + half][:, 0:512]),
                            reads=(pk,), writes=(okey,))
                    s.dma("gpsimd", "outadd", lambda e, tt=tt, ob=ob: e.dma_start(
                        out=out_d[tt * 128:(tt + 1) * 128, :], in_=ob[:],
                        accum_op=ALU.add),
                        reads=(okey, f"outrow{tt}"), writes=(f"outrow{tt}",), bag=True)

            def emit_wout(ks):
                for dg in range(4):
                    for hf in range(2):
                        s.dma("sync", f"qwo{hf}", lambda e, dg=dg, hf=hf: e.dma_start(
                            out=wo_bf[0][:, hf * 2048:(hf + 1) * 2048].rearrange(
                                "p (ct f) -> p ct f", ct=8),
                            in_=wout_d[hf * 1024:(hf + 1) * 1024,
                                       dg * 256:(dg + 1) * 256].rearrange(
                                "(ct p) f -> p ct f", p=128)),
                            reads=("AGwo",), writes=(f"wobf{hf}",))
                    for hf in range(2):
                        for db2 in range(2):
                            d = dg * 2 + db2
                            pk = f"ps{6 + d % 2}"
                            for c in range(hf * 8, hf * 8 + 8):
                                s.op("tensor", lambda e, c=c, db2=db2, d=d: e.matmul(
                                    out=ps[6 + d % 2][:, 0:T],
                                    lhsT=wo_bf[0][:, c * 256 + db2 * 128:c * 256 + (db2 + 1) * 128],
                                    rhs=sg[c][:],
                                    start=(c == 0), stop=(c == CT - 1)),
                                    reads=(f"wobf{hf}", f"sg_{c}"), writes=(pk,))
                            if hf == 1:
                                s.op("scalar", lambda e, d=d: e.copy(
                                    out=att[d][:], in_=ps[6 + d % 2][:, 0:T]),
                                    reads=(pk,), writes=(f"att{d}",))

            if 2 in phases:
                emit_zx_part(0, False)
                emit_prep(1)
            for k in (range(NCH) if 2 in phases else []):
                kb = k % 2
                nkb = (k + 1) % 2
                s.mark(f"conv{k}")

                for c in range(CT):
                    ik = f"in{kb}_{c}"
                    s.op("vector", lambda e, c=c, kb=kb: e.tensor_scalar(
                        out=c1[:], in0=inner[kb][c][:, 0:T], scalar1=cw_sb[c][:, 0:1],
                        scalar2=None, op0=ALU.mult),
                        reads=(ik, f"cw{c}"), writes=("c1",))
                    s.op("vector", lambda e, c=c, kb=kb: e.scalar_tensor_tensor(
                        out=c2[:], in0=inner[kb][c][:, 1:T + 1], scalar=cw_sb[c][:, 1:2],
                        in1=c1[:], op0=ALU.mult, op1=ALU.add),
                        reads=(ik, f"cw{c}", "c1"), writes=("c2",))
                    s.op("vector", lambda e, c=c, kb=kb: e.scalar_tensor_tensor(
                        out=h_bf[c][:], in0=inner[kb][c][:, 2:T + 2], scalar=cw_sb[c][:, 2:3],
                        in1=c2[:], op0=ALU.mult, op1=ALU.add),
                        reads=(ik, f"cw{c}", "c2"), writes=(f"h{c}",))
                    s.op("scalar", lambda e, c=c: e.activation(
                        out=h_bf[c][:], in_=h_bf[c][:], func=AF.Silu, bias=cbp[c]),
                        reads=(f"h{c}", f"parm{c}"), writes=(f"h{c}",))

                for c in range(CT):
                    s.op("tensor", lambda e, c=c: e.matmul(
                        out=ps[4][0:16, 0:T], lhsT=wst_bf[c][:], rhs=h_bf[c][:],
                        start=(c == 0), stop=(c == CT - 1)),
                        reads=(f"wst{c}", f"h{c}"), writes=("ps4",))
                s.op("scalar", lambda e: e.copy(out=out_tok[0][0:16, 0:T], in_=ps[4][0:16, 0:T]),
                     reads=("ps4",), writes=("otok0",))
                for kk in range(K):
                    bp = kk % 2
                    bb_pk = 5 - bp
                    bbuf = brow[0][:] if bp == 0 else out_tok[0][32:33, 0:T]
                    bkey = "brow0" if bp == 0 else "otok0"
                    s.dma("scalar", f"brw{bp}", lambda e, kk=kk, bbuf=bbuf: e.dma_start(
                        out=bbuf, in_=out_tok[0][kk:kk + 1, 0:T]),
                        reads=("otok0",), writes=(bkey,))
                    lT = (lambda: ones1[:]) if bp == 0 else (lambda: ones33[32:33, :])
                    lk = "ones1" if bp == 0 else "ones33"
                    s.op("tensor", lambda e, bbuf=bbuf, bb_pk=bb_pk, lT=lT: e.matmul(
                        out=ps[bb_pk][:, 0:T], lhsT=lT(), rhs=bbuf[0:1, :],
                        start=True, stop=True),
                        reads=(bkey, lk), writes=(f"ps{bb_pk}",))
                    s.op("scalar", lambda e, kk=kk, bb_pk=bb_pk: e.copy(
                        out=Bb[kk][:], in_=ps[bb_pk][:, 0:T]),
                        reads=(f"ps{bb_pk}",), writes=(f"Bb{kk}",))

                def emit_dt_mm(cog, cb2, wb):
                    cb = cog * 2 + cb2
                    dp = cb % 2
                    pk = f"ps{2 + dp}"
                    for c in range(CT):
                        s.op("tensor", lambda e, c=c, cb2=cb2, dp=dp, wb=wb: e.matmul(
                            out=ps[2 + dp][:, 0:T],
                            lhsT=wt_bf[wb][:, c * 256 + cb2 * 128:c * 256 + (cb2 + 1) * 128],
                            rhs=h_bf[c][:],
                            start=(c == 0), stop=(c == CT - 1)),
                            reads=(f"wtbf{wb}", f"h{c}"), writes=(pk,))

                for cog in range(8):
                    if cog == 2:
                        emit_zx_part(k, True, cgs=[5])
                        if k + 2 <= NCH - 1:
                            emit_prep(k + 2)
                        elif k + 2 == NCH:
                            for c in range(CT):
                                s.op("gpsimd", lambda e, c=c, nkb=nkb: e.memset(
                                    inner[nkb][c][:, T + 1:T + 2], 0.0),
                                    writes=(f"in{nkb}_{c}",))
                        if k > 0:
                            emit_store(k - 1)
                    elif cog == 4:
                        emit_zx_part(k, True, cgs=[6])
                    elif cog == 6:
                        emit_zx_part(k, True, cgs=[7])
                    wb = (k * 8 + cog) % 2
                    for cg_ld in ((0, 1) if cog == 0 else
                                  () if cog == 1 else (cog,)):
                        wbl = (k * 8 + cg_ld) % 2
                        s.dma("sync", f"qwt{wbl}", lambda e, cg_ld=cg_ld, wbl=wbl: e.dma_start(
                            out=wt_bf[wbl][:].rearrange("p (ct f) -> p ct f", ct=16),
                            in_=wtime_d[:, cg_ld * 256:(cg_ld + 1) * 256].rearrange(
                                "(ct p) f -> p ct f", p=128)),
                            reads=("AGwt",), writes=(f"wtbf{wbl}",))
                    if cog == 0:
                        emit_dt_mm(0, 0, wb)
                        emit_dt_mm(0, 1, wb)
                        if k > 0:
                            emit_wout(k - 1)
                        emit_zx_part(k, True, cgs=[4])
                    for cb2 in range(2):
                        cb = cog * 2 + cb2
                        dp = cb % 2
                        pk = f"ps{2 + dp}"
                        if cog != 0:
                            emit_dt_mm(cog, cb2, wb)
                        s.op("scalar", lambda e, cb=cb, dp=dp: e.activation(
                            out=c1[:], in_=ps[2 + dp][:, 0:T], func=AF.Abs,
                            bias=btp[cb]), reads=(pk, f"parm{cb}"), writes=("c1",))
                        s.op("scalar", lambda e: e.activation(
                            out=c2[:], in_=c1[:], func=AF.Exp, scale=-1.0),
                            reads=("c1",), writes=("c2",))
                        s.op("scalar", lambda e: e.activation(
                            out=c1[:], in_=c2[:], func=AF.Ln, bias=1.0),
                            reads=("c2",), writes=("c1",))
                        s.op("vector", lambda e, cb=cb, dp=dp: e.tensor_scalar(
                            out=y_sc2[cb % 2][:], in0=ps[2 + dp][:, 0:T], scalar1=btp[cb],
                            scalar2=0.0, op0=ALU.add, op1=ALU.max),
                            reads=(pk, f"parm{cb}"), writes=(f"y_sc{cb % 2}",))
                        s.op("vector", lambda e, dp=dp, cb=cb: e.tensor_tensor(
                            out=dtb[dp][:], in0=y_sc2[cb % 2][:], in1=c1[:], op=ALU.add),
                            reads=(f"y_sc{cb % 2}", "c1"), writes=(f"dtb{dp}",))
                        s.op("vector", lambda e, cb=cb, dp=dp: e.tensor_tensor(
                            out=g_sc2[cb % 2][:], in0=dtb[dp][:], in1=h_bf[cb][:], op=ALU.mult),
                            reads=(f"dtb{dp}", f"h{cb}"), writes=(f"g_sc{cb % 2}",))
                        s.op("vector", lambda e, cb=cb: e.tensor_scalar(
                            out=y_sc2[cb % 2][:], in0=h_bf[cb][:], scalar1=skp[cb],
                            scalar2=None, op0=ALU.mult),
                            reads=(f"h{cb}", f"parm{cb}"), writes=(f"y_sc{cb % 2}",))
                        yeng = "gpsimd" if cb % 2 == 0 else "vector"
                        for kk in range(K):
                            ap = kk % 2
                            a3 = (cb * K + kk) % 3
                            s.op("scalar", lambda e, cb=cb, kk=kk, a3=a3, dp=dp: e.activation(
                                out=dA[a3][:], in_=dtb[dp][:], func=AF.Exp,
                                scale=nA[cb][:, kk:kk + 1]),
                                reads=(f"dtb{dp}", f"nA{cb}"), writes=(f"dA{a3}",))
                            dbeng = "vector"
                            s.op(dbeng, lambda e, kk=kk, ap=ap, cb=cb: e.tensor_tensor(
                                out=dB[ap][:], in0=g_sc2[cb % 2][:], in1=Bb[kk][:], op=ALU.mult),
                                reads=(f"g_sc{cb % 2}", f"Bb{kk}"), writes=(f"dB{ap}",))
                            s.op("vector", lambda e, kk=kk, ap=ap, a3=a3, cb=cb: e.tensor_tensor_scan(
                                out=mem_all[:, kk * T:(kk + 1) * T], data0=dA[a3][:],
                                data1=dB[ap][:], initial=carr[cb][:, kk:kk + 1],
                                op0=ALU.mult, op1=ALU.add),
                                reads=(f"dA{a3}", f"dB{ap}", f"carr{cb}"),
                                writes=(f"mem{kk}",))
                            s.op(yeng, lambda e, kk=kk, cb=cb: e.tensor_tensor(
                                out=y_sc2[cb % 2][:], in0=y_sc2[cb % 2][:],
                                in1=mem_all[:, kk * T:(kk + 1) * T], op=ALU.add),
                                reads=(f"y_sc{cb % 2}", f"mem{kk}"), writes=(f"y_sc{cb % 2}",))
                        s.op("gpsimd", lambda e, cb=cb: e.tensor_copy(
                            out=carr[cb][:], in_=mem_all[:, T - 1::T]),
                            reads=tuple(f"mem{kk}" for kk in range(K)),
                            writes=(f"carr{cb}",))
                        s.op("vector", lambda e, cb=cb, kb=kb: e.tensor_tensor(
                            out=sg[cb][:], in0=y_sc2[cb % 2][:], in1=sg[cb][:], op=ALU.mult),
                            reads=(f"y_sc{cb % 2}",), writes=(f"sg_{cb}",))

            if 2 in phases:
                emit_wout(NCH - 1)
                emit_store(NCH - 1)

            s.join()
            s.op("sync", lambda e: e.nop())

        _LAST_SCHED.clear()
        _LAST_SCHED.append(sched)
        with nc.Block() as block:
            sched.finalize(block)
    return nc


_NC_CACHE = {}


def kernel(**inputs):
    x_full = np.ascontiguousarray(np.asarray(inputs["L0_attn_x"], dtype=np.float32))
    f32 = lambda a: np.ascontiguousarray(np.asarray(a, dtype=np.float32))

    win = f32(inputs["L0_attn_W_in"])
    convw = f32(inputs["L0_attn_conv_w"]).reshape(CT, 128, 3)
    convb = f32(inputs["L0_attn_conv_b"]).reshape(CT, 128, 1)
    wst = f32(inputs["L0_attn_W_state"])
    wtime = f32(inputs["L0_attn_W_time"])
    btime = f32(inputs["L0_attn_b_time"]).reshape(CT, 128, 1)
    alog = f32(inputs["L0_attn_A_log"]).reshape(CT, 128, K)
    skip = f32(inputs["L0_attn_skip"]).reshape(CT, 128, 1)
    wout = f32(inputs["L0_attn_W_out"])
    rw = f32(inputs["L0_ffn_router_w"])
    wv = f32(inputs["L0_ffn_wv"])
    sv_t = np.ascontiguousarray(f32(inputs["L0_ffn_sv"]).reshape(E, F // 128, 128).transpose(0, 2, 1))
    w = f32(inputs["L0_ffn_w"])
    s_t = np.ascontiguousarray(f32(inputs["L0_ffn_s"]).reshape(E, F // 128, 128).transpose(0, 2, 1))
    w1 = f32(inputs["L0_ffn_w1"])
    s1_t = np.ascontiguousarray(f32(inputs["L0_ffn_s1"]).reshape(E, DT, 128).transpose(0, 2, 1))
    ident = np.eye(128, dtype=np.float32)
    ones1 = np.ones((1, 128), dtype=np.float32)

    if "nc" not in _NC_CACHE:
        _NC_CACHE["nc"] = build_program()
    nc = _NC_CACHE["nc"]

    common = dict(conv_w=convw, conv_b=convb, w_state=wst,
                  b_time=btime, a_log=alog, skip=skip,
                  router_w=rw, sv_t=sv_t, s_t=s_t,
                  s1_t=s1_t, ident=ident, ones1=ones1,
                  tril=np.tril(np.ones((128, 128), np.float32), -1),
                  onescol=np.ones((128, 1), np.float32),
                  ecap=(np.arange(E, dtype=np.float32) * CAP)[None, :],
                  capthr=np.ascontiguousarray(np.broadcast_to(
                      (np.arange(E, dtype=np.float32) + 1) * CAP, (128, E))),
                  trmap=np.ascontiguousarray(np.broadcast_to(
                      GROWS + np.arange(E, dtype=np.float32), (128, E))))
    in_maps = [dict(common,
                    x=np.ascontiguousarray(x_full[c]),
                    wv_sh=np.ascontiguousarray(
                        wv[:, c * 128:(c + 1) * 128, :].reshape(E * 128, F)),
                    w_sh=np.ascontiguousarray(
                        w[:, c * 128:(c + 1) * 128, :].reshape(E * 128, F)),
                    w1_sh=np.ascontiguousarray(
                        w1[:, c * 512:(c + 1) * 512, :].reshape(E * 512, D)),
                    wi_sh=np.ascontiguousarray(win[c * (D // 8):(c + 1) * (D // 8), :]),
                    wt_sh=np.ascontiguousarray(wtime[c * (C // 8):(c + 1) * (C // 8), :]),
                    wo_sh=np.ascontiguousarray(wout[c * (C // 8):(c + 1) * (C // 8), :]))
               for c in range(N_CORES)]
    res = run_bass_kernel_spmd(nc, in_maps, core_ids=list(range(N_CORES)))
    out = np.stack([res.results[c]["out"] for c in range(N_CORES)], axis=0)
    return out
